# version 24
# baseline (speedup 1.0000x reference)
import math
from contextlib import ExitStack

import numpy as np
import ml_dtypes
import os
os.environ.setdefault("TRN_ALLOW_ICI_ALL_TO_ALL", "1")

import concourse.bass as bass
import concourse.mybir as mybir
from concourse.bass_utils import run_bass_kernel_spmd
from concourse.aot_env import allow_ici_all_to_all

F32 = mybir.dt.float32
BF16 = mybir.dt.bfloat16
AF = mybir.ActivationFunctionType
ALU = mybir.AluOpType
AX = mybir.AxisListType

P = 128
D = 1024
T = 8192
NT = 2048
QT = 512
NQT = T // QT

ENGS = ("sync", "gpsimd", "tensor", "vector", "scalar")


class Prog:

    def __init__(self, nc, es, sem_names):
        self.nc = nc
        self.ops = {e: [] for e in ENGS}
        self.sems = {n: es.enter_context(nc.semaphore(n)) for n in sem_names}
        self.cnt = {n: 0 for n in sem_names}

    def w(self, eng, name, val=None):
        v = self.cnt[name] if val is None else val
        if v <= 0:
            return
        s = self.sems[name]
        self.ops[eng].append(lambda e: e.wait_ge(s, v))

    def op(self, eng, fn, inc=None, k=1):
        if inc is None:
            self.ops[eng].append(fn)
        else:
            s = self.sems[inc]
            self.ops[eng].append(lambda e: fn(e).then_inc(s, k))
            self.cnt[inc] += k

    def block(self):
        cur = dict(self.cnt)
        for e in ENGS:
            for s, v in cur.items():
                if v:
                    self.w(e, s, v)

    def finalize(self):
        live = [e for e in ENGS if self.ops[e]]
        with self.nc.Block() as blk:
            for e in live:
                lst = self.ops[e]
                self.ops[e] = []

                def body(eng, _lst=lst):
                    for f in _lst:
                        f(eng)
                getattr(blk, e)(body)



def emit_attn(nc, pr, layer, pfx, xchunk, odst, wq, wk, wv,
              theta=None, gam=None, bet=None, flash_qts=None, shared=None,
              xgate=None, xgate2=None):
    HD = 128 if layer == 0 else 64
    H = 128 // HD
    TOTC = 128
    OFF = QT + 127
    GW = T + OFF + 1
    if layer == 0:
        GW2 = GW + 128
        g2 = (shared["g2"] if shared and "g2" in shared
              else nc.dram_tensor(pfx + "g2", [P, GW2], BF16))

    es = ExitStack()

    def sbuf(name, shape, dt=F32):
        return es.enter_context(nc.sbuf_tensor(pfx + name, shape, dt))

    ident = sbuf("ident", [P, P])
    identb = sbuf("identb", [P, P], BF16)
    ones_r = sbuf("ones_r", [1, P])
    ones_cb = sbuf("ones_cb", [P, 1], BF16)
    wq_sb = sbuf("wq_sb", [P, 8 * TOTC], BF16)
    wk_sb = sbuf("wk_sb", [P, 8 * TOTC], BF16)
    wv_sb = sbuf("wv_sb", [P, 8 * TOTC], BF16)
    qT_sb = sbuf("qT_sb", [P, T], BF16)
    kT_sb = sbuf("kT_sb", [P, T], BF16)
    v1 = sbuf("v1", [P, H * 64 * 65], BF16)
    v2 = sbuf("v2", [P, H * 64 * 64], BF16) if layer == 0 else None
    xbuf = [sbuf("xb%d" % i, [P, 8 * QT], BF16) for i in range(2)]
    vtmp = [sbuf("vtmp%d" % i, [P, QT], BF16) for i in range(2)]
    bias_sb = sbuf("bias_sb", [P, H])
    binm = sbuf("binm", [P, 4 * QT], BF16)
    stat = sbuf("stat", [1, 64])
    sq_scr = sbuf("sq_scr", [P, QT], BF16)
    pt_sb = [sbuf("pt%d" % i, [P, QT], BF16) for i in range(4)]
    oa_sbs = [sbuf("oa_sb%d" % i, [65, QT]) for i in range(H)]
    ob_sbs = [sbuf("ob_sb%d" % i, [64, QT]) for i in range(H)] if layer == 0 else None
    otile = [sbuf("otile%d" % i, [P, HD], BF16) for i in range(2)]
    ot_f = sbuf("ot_f", [P, 128])
    gn_scr = sbuf("gn_scr", [P, 128], BF16)
    linv = sbuf("linv", [P, 1])
    if layer == 0:
        dt_sb = [sbuf("dt%d" % i, [P, QT], BF16) for i in range(4)]
        sp_sb = [sbuf("sp%d" % i, [P, QT]) for i in range(4)]
        gB = sbuf("gB", [P, TOTC])
        bB = sbuf("bB", [P, TOTC])
        grow = sbuf("grow", [1, TOTC])
        brow = sbuf("brow", [1, TOTC])
        gnmu = sbuf("gnmu", [P, 1])
        gnsq = sbuf("gnsq", [P, 1])
        gnms = sbuf("gnms", [P, 1])
        gnvar = sbuf("gnvar", [P, 1])
        gnrs = sbuf("gnrs", [P, 1])
        rg_sb = sbuf("rg_sb", [P, 128])
        gn_f2 = sbuf("gn_f2", [P, 128])
        ones_c8 = sbuf("ones_c8", [8, 1])
    else:
        if shared is not None:
            cs_sb = shared["cs"]
            sn_sb = shared["sn"]
        else:
            cs_sb = sbuf("cs_sb", [P, 2048])
            sn_sb = sbuf("sn_sb", [P, 2048])
        rtm = sbuf("rtm", [P, 512])
        rtm2 = sbuf("rtm2", [P, 512])
        rsc = sbuf("rsc", [P, 256])
        rsc2 = sbuf("rsc2", [P, 256])
        halfpi = sbuf("halfpi", [P, 1])
        ev_sb = [sbuf("ev_sb%d" % i, [64, QT]) for i in range(4)]

    C = pr.cnt
    B0 = dict(C)

    with es:
        ph_es = ExitStack()
        ps_pr = ph_es.enter_context(nc.psum_tensor(pfx + "ps_pr", [P, QT], F32))
        ps_pr2 = ph_es.enter_context(nc.psum_tensor(pfx + "ps_pr2", [P, QT], F32))
        ps_pr3 = ph_es.enter_context(nc.psum_tensor(pfx + "ps_pr3", [P, QT], F32))
        ps_pr4 = ph_es.enter_context(nc.psum_tensor(pfx + "ps_pr4", [P, QT], F32))
        ps_pr5 = ph_es.enter_context(nc.psum_tensor(pfx + "ps_pr5", [P, QT], F32))
        ps_pr6 = ph_es.enter_context(nc.psum_tensor(pfx + "ps_pr6", [P, QT], F32))
        ps_vtb = ph_es.enter_context(nc.psum_tensor(pfx + "ps_vtb", [P, QT], BF16))

        def g_memsets(g):
            g.memset(ident[:, :], 0.0)
            g.memset(identb[:, :], 0.0)
            g.memset(ones_r[:, :], 1.0)
            g.memset(ones_cb[:, :], 1.0)
            if layer == 0:
                g.memset(ones_c8[:, :], 1.0)
            for h in range(H):
                g.memset(v1[:, h * 4160 + 64:(h + 1) * 4160:65], 1.0)
            for i in range(3):
                g.memset(binm[:, i * QT:(i + 1) * QT], 1.0)
            return g.memset(binm[:, 3 * QT:4 * QT], 1.0)
        pr.op("gpsimd", g_memsets, "gx")
        pr.w("gpsimd", "gx")

        def g_selects(g):
            g.affine_select(out=ident[:, :], in_=ident[:, :], compare_op=ALU.not_equal,
                            fill=1.0, base=0, pattern=[[-1, P]], channel_multiplier=1)
            g.affine_select(out=identb[:, :], in_=identb[:, :], compare_op=ALU.not_equal,
                            fill=1.0, base=0, pattern=[[-1, P]], channel_multiplier=1)
            r = None
            for i in range(4):
                m = binm[:, i * QT:(i + 1) * QT]
                r = g.affine_select(out=m, in_=m, compare_op=ALU.is_ge, fill=0.0,
                                    base=-128 * i, pattern=[[1, QT]], channel_multiplier=-1)
            return r
        pr.op("gpsimd", g_selects, "gx")
        pr.w("gpsimd", "gx")
        for dc in range(8):
            for w_ap, w_sb in ((wq, wq_sb), (wk, wk_sb), (wv, wv_sb)):
                pr.op("gpsimd", lambda g, a=w_ap, s=w_sb, d=dc: g.dma_start(
                    out=s[:, d * TOTC:(d + 1) * TOTC], in_=a[d * P:(d + 1) * P, :]), "lw", 16)
        pr.block()

        skip_tables = bool(shared) and shared.get("built", False)
        if layer == 0 and skip_tables:
            pr.op("gpsimd", lambda g: g.dma_start(
                out=grow[:, :], in_=bass.AP(gam, 0, [[1, 1], [1, TOTC]])), "ld", 16)
            pr.op("gpsimd", lambda g: g.dma_start(
                out=brow[:, :], in_=bass.AP(bet, 0, [[1, 1], [1, TOTC]])), "ld", 16)
            pr.w("tensor", "ld")
            pr.w("tensor", "ac")
            pr.op("tensor", lambda t: t.matmul(ps_pr[:, 0:TOTC], ones_r[:, :], grow[:, :],
                                               start=True, stop=True), "mm")
            pr.op("tensor", lambda t: t.matmul(ps_pr2[:, 0:TOTC], ones_r[:, :], brow[:, :],
                                               start=True, stop=True), "mm")
            pr.w("vector", "mm")
            pr.op("vector", lambda v: v.tensor_copy(gB[:, :], ps_pr[:, 0:TOTC]))
            pr.op("vector", lambda v: v.tensor_copy(bB[:, :], ps_pr2[:, 0:TOTC]), "ev")
            pr.block()
        elif layer == 1 and skip_tables:
            pass
        elif layer == 0:
            t0 = ExitStack()
            th_sb = t0.enter_context(nc.sbuf_tensor(pfx + "th_sb", [8, 1], F32))
            nv8 = t0.enter_context(nc.sbuf_tensor(pfx + "nv8", [8, QT], F32))
            nv8b = t0.enter_context(nc.sbuf_tensor(pfx + "nv8b", [8, QT], F32))
            ind8 = t0.enter_context(nc.sbuf_tensor(pfx + "ind8", [8, QT], F32))
            glc = [t0.enter_context(nc.sbuf_tensor(pfx + "glc%d" % i, [1, QT], BF16)) for i in range(2)]
            gl_lin = nc.dram_tensor(pfx + "gl_lin", [GW], BF16)
            with t0:
                pr.op("gpsimd", lambda g: g.dma_start(
                    out=th_sb[:, :], in_=bass.AP(theta, 0, [[1, 8], [1, 1]])), "ld", 16)
                pr.op("gpsimd", lambda g: g.dma_start(
                    out=grow[:, :], in_=bass.AP(gam, 0, [[1, 1], [1, TOTC]])), "ld", 16)
                pr.op("gpsimd", lambda g: g.dma_start(
                    out=brow[:, :], in_=bass.AP(bet, 0, [[1, 1], [1, TOTC]])), "ld", 16)
                pr.block()
                pr.w("vector", "ld")
                nch = (GW + QT - 1) // QT
                for c in range(nch):
                    w = min(QT, GW - c * QT)
                    pr.w("gpsimd", "mm")
                    pr.op("gpsimd", lambda g, cc=c, ww=w: g.iota(
                        nv8[:, 0:ww], [[1, ww]], base=-OFF + cc * QT, channel_multiplier=0,
                        allow_small_or_imprecise_dtypes=True), "gx")
                    pr.w("vector", "gx")
                    pr.op("vector", lambda v, ww=w: v.tensor_scalar(
                        ind8[:, 0:ww], nv8[:, 0:ww], 0.0, None, op0=ALU.is_ge))
                    pr.op("vector", lambda v, ww=w: v.tensor_scalar(
                        nv8b[:, 0:ww], nv8[:, 0:ww], 0.0, th_sb[:, :],
                        op0=ALU.max, op1=ALU.mult), "ev")
                    pr.w("scalar", "ev")
                    pr.op("scalar", lambda s, ww=w: s.activation(
                        nv8b[:, 0:ww], nv8b[:, 0:ww], AF.Exp, scale=-1.0), "ac")
                    pr.w("vector", "ac")
                    pr.op("vector", lambda v, ww=w: v.tensor_tensor(
                        nv8b[:, 0:ww], nv8b[:, 0:ww], ind8[:, 0:ww], op=ALU.mult), "ev")
                    pr.w("tensor", "ev")
                    pr.w("tensor", "ac")
                    pr.op("tensor", lambda t, ww=w: t.matmul(
                        ps_pr[0:1, 0:ww], ones_c8[:, :], nv8b[:, 0:ww],
                        start=True, stop=True), "mm")
                    pr.w("scalar", "mm")
                    pr.w("scalar", "glw", B0["glw"] + c * 16)
                    pr.op("scalar", lambda s, cc=c, ww=w: s.activation(
                        glc[cc % 2][:, 0:ww], ps_pr[0:1, 0:ww], AF.Copy,
                        scale=1.0 / (8.0 * math.sqrt(HD))), "ac")
                    pr.w("sync", "ac")
                    pr.op("sync", lambda sy, cc=c, ww=w: sy.dma_start(
                        out=bass.AP(gl_lin, cc * QT, [[1, 1], [1, ww]]),
                        in_=glc[cc % 2][:, 0:ww]), "glw", 16)
                pr.w("gpsimd", "glw")
                for p in range(P):
                    pr.op("gpsimd", lambda g, pp=p: g.dma_start(
                        out=bass.AP(g2, pp * GW2 + pp, [[1, 1], [1, GW]]),
                        in_=bass.AP(gl_lin, 0, [[1, 1], [1, GW]])), "g2s", 16)
                pr.w("gpsimd", "g2s")
                pr.op("gpsimd", lambda g: g.nop())
                if shared is not None:
                    shared["g2"] = g2
                    shared["built"] = True
                pr.w("tensor", "ld")
                pr.w("tensor", "ac")
                pr.op("tensor", lambda t: t.matmul(ps_pr[:, 0:TOTC], ones_r[:, :], grow[:, :],
                                                   start=True, stop=True), "mm")
                pr.op("tensor", lambda t: t.matmul(ps_pr2[:, 0:TOTC], ones_r[:, :], brow[:, :],
                                                   start=True, stop=True), "mm")
                pr.w("vector", "mm")
                pr.op("vector", lambda v: v.tensor_copy(gB[:, :], ps_pr[:, 0:TOTC]))
                pr.op("vector", lambda v: v.tensor_copy(bB[:, :], ps_pr2[:, 0:TOTC]), "ev")
                pr.block()
        else:
            t0 = ExitStack()
            inv_sb = t0.enter_context(nc.sbuf_tensor(pfx + "inv_sb", [1, 32], F32))
            ang = t0.enter_context(nc.sbuf_tensor(pfx + "ang", [P, 2048], F32))
            angk = t0.enter_context(nc.sbuf_tensor(pfx + "angk", [P, 2048], F32))
            with t0:
                pr.op("gpsimd", lambda g: g.memset(halfpi[:, :], math.pi / 2))
                pr.op("gpsimd", lambda g: g.iota(inv_sb[:, :], [[1, 32]], base=0,
                                                 channel_multiplier=0,
                                                 allow_small_or_imprecise_dtypes=True))
                pr.op("gpsimd", lambda g: g.iota(ang[:, :], [[128, 64], [0, 32]], base=0,
                                                 channel_multiplier=1,
                                                 allow_small_or_imprecise_dtypes=True))
                pr.block()
                pr.op("scalar", lambda s: s.activation(inv_sb[:, :], inv_sb[:, :], AF.Exp,
                                                       scale=-math.log(10000.0) / 32.0))
                pr.block()
                pr.op("tensor", lambda t: t.matmul(ps_pr[:, 0:32], ones_r[:, :], inv_sb[:, :],
                                                   start=True, stop=True), "mm")
                pr.w("vector", "mm")
                for kc in range(64):
                    pr.op("vector", lambda v, k=kc: v.tensor_copy(
                        angk[:, k * 32:(k + 1) * 32], ps_pr[:, 0:32]))
                pr.op("vector", lambda v: v.nop(), "ev")
                pr.block()
                CBIG = 12582912.0
                pr.op("vector", lambda v: v.tensor_tensor(ang[:, :], ang[:, :], angk[:, :],
                                                          op=ALU.mult), "ev")
                pr.w("vector", "ev")
                pr.op("vector", lambda v: v.tensor_scalar(angk[:, :], ang[:, :],
                                                          1.0 / (2 * math.pi), CBIG,
                                                          op0=ALU.mult, op1=ALU.add), "ev")
                pr.w("vector", "ev")
                pr.op("vector", lambda v: v.tensor_scalar(angk[:, :], angk[:, :], CBIG, None,
                                                          op0=ALU.subtract), "ev")
                pr.w("vector", "ev")
                pr.op("vector", lambda v: v.tensor_scalar(angk[:, :], angk[:, :],
                                                          -2 * math.pi, None, op0=ALU.mult), "ev")
                pr.w("vector", "ev")
                pr.op("vector", lambda v: v.tensor_tensor(ang[:, :], ang[:, :], angk[:, :],
                                                          op=ALU.add))
                pr.block()
                pr.op("scalar", lambda s: s.activation(sn_sb[:, :], ang[:, :], AF.Sin))
                pr.op("scalar", lambda s: s.activation(angk[:, :], ang[:, :], AF.Abs))
                pr.op("scalar", lambda s: s.activation(cs_sb[:, :], angk[:, :], AF.Sin,
                                                       scale=-1.0, bias=halfpi[:, :]))
                pr.block()
                if shared is not None:
                    shared["built"] = True

        mm_b = C["mm"]
        ev_done = {}
        evac1_of = {}
        mm_done = {}
        pr.w("gpsimd", "lw")
        if xgate is not None:
            pr.w("gpsimd", xgate[0], xgate[1])
        part2 = []
        rci = [0]
        cp_of = {}
        rv_of = {}
        bw_of = {}
        st_of = {}
        vev_of = {}
        vci = [0]
        fw_of = {}

        def flush_part2(n):
            while len(part2) > n:
                part2.pop(0)()
        order = list(range(NQT))
        if xgate2 is not None:
            order = ([t for t in range(NQT) if t % 4 < 2]
                     + [t for t in range(NQT) if t % 4 >= 2])
        for idx, tc in enumerate(order):
            if xgate2 is not None and idx == NQT // 2:
                pr.w("gpsimd", xgate2[0], xgate2[1])
            if idx >= 2:
                pr.w("gpsimd", "mm", mm_done[(idx - 2, "xfree")])
            lsem = "ld%d" % (idx % 2)
            for dc in range(8):
                pr.op("gpsimd", lambda g, t=tc, d=dc, ix=idx: g.dma_start(
                    out=xbuf[ix % 2][:, d * QT:(d + 1) * QT], in_=xchunk(d, t)), lsem, 16)
            for h in range(H):
                for mi in range(3):
                    grp = idx * H * 3 + h * 3 + mi
                    bank = (ps_pr, ps_pr4)[grp % 2]
                    pr.w("tensor", "ld%d" % (idx % 2),
                         B0["ld%d" % (idx % 2)] + (idx // 2 + 1) * 128)
                    if grp >= 2:
                        pr.w("tensor", "ev", evac1_of.get(grp - 2, 0))
                    w_sb = (wq_sb, wk_sb, wv_sb)[mi]

                    def projmm(t, h=h, mi=mi, ix=idx, w_sb=w_sb, bank=bank):
                        r = None
                        for dc in range(8):
                            r = t.matmul(bank[0:HD, :],
                                         w_sb[:, dc * TOTC + h * HD: dc * TOTC + (h + 1) * HD],
                                         xbuf[ix % 2][:, dc * QT:(dc + 1) * QT],
                                         start=(dc == 0), stop=(dc == 7))
                        return r
                    pr.op("tensor", projmm, "mm")
                    mmv = C["mm"]
                    col0 = (h * T if layer == 0 else (h // 2) * T) + tc * QT
                    pb = 0 if layer == 0 else (h % 2) * 64
                    if mi < 2 and layer == 0:
                        dst = (qT_sb, kT_sb)[mi]
                        pr.w("vector", "mm", mmv)
                        pr.op("vector", lambda v, d=dst, c=col0, bank=bank: v.tensor_copy(
                            d[0:HD, c:c + QT], bank[0:HD, :]), "ev")
                        ev_done[grp] = C["ev"]
                        evac1_of[grp] = C["ev"]
                    elif mi == 2:
                        pr.w("vector", "mm", mmv)
                        pr.op("vector", lambda v, g2_=grp, bank=bank: v.tensor_copy(
                            vtmp[g2_ % 2][0:HD, :], bank[0:HD, :]), "ev")
                        evv = C["ev"]
                        evac1_of[grp] = evv

                        vc = vci[0]
                        vci[0] += 1

                        def d_vchain(grp=grp, tc=tc, h=h, evv=evv, vc=vc):
                            pr.w("tensor", "ev", evv)
                            if vc >= 1:
                                pr.w("tensor", "ev", vev_of.get(vc - 1, 0))

                            def vtr(t, grp=grp):
                                r = None
                                for stt in range(4):
                                    if layer == 0:
                                        r = t.transpose(ps_vtb[:, stt * 128:(stt + 1) * 128],
                                                        vtmp[grp % 2][:, stt * 128:(stt + 1) * 128],
                                                        identb[:, :])
                                    else:
                                        r = t.transpose(ps_vtb[:, stt * 64:(stt + 1) * 64],
                                                        vtmp[grp % 2][0:64, stt * 128:(stt + 1) * 128],
                                                        identb[0:64, 0:64])
                                return r
                            pr.op("tensor", vtr, "mm")
                            pr.w("vector", "mm", C["mm"])

                            def vev(v, tc=tc, h=h):
                                r = None
                                for stt in range(4):
                                    kc = tc * 4 + stt
                                    if layer == 0:
                                        v.tensor_copy(v1[:, h * 4160 + kc * 65: h * 4160 + kc * 65 + 64],
                                                      ps_vtb[:, stt * 128:stt * 128 + 64])
                                        r = v.tensor_copy(v2[:, h * 4096 + kc * 64: h * 4096 + (kc + 1) * 64],
                                                          ps_vtb[:, stt * 128 + 64:(stt + 1) * 128])
                                    else:
                                        r = v.tensor_copy(v1[:, h * 4160 + kc * 65: h * 4160 + kc * 65 + 64],
                                                          ps_vtb[:, stt * 64:(stt + 1) * 64])
                                return r
                            pr.op("vector", vev, "ev")
                            ev_done[grp] = C["ev"]
                            vev_of[vc] = C["ev"]
                        part2.append(d_vchain)
                        flush_part2(2)
                    else:
                        pr.w("vector", "mm", mmv)
                        rc = rci[0]
                        rci[0] += 1
                        if rc >= 4:
                            pr.w("vector", "mm", fw_of.get(rc - 4, 0))
                        pr.op("vector", lambda v, rc=rc, bank=bank: v.tensor_copy(
                            ev_sb[rc % 4][:, :], bank[0:64, :]), "ev")
                        evac1_of[grp] = C["ev"]
                        evv = C["ev"]

                        def d_rchain(grp=grp, tc=tc, h=h, mi=mi, pb=pb, col0=col0, evv=evv,
                                     rc=rc):
                            rp = rc % 2
                            pfw = (ps_pr2, ps_pr5)[rp]
                            pbw = (ps_pr3, ps_pr6)[rp]
                            ro = rp * 256
                            rh = rp * 128
                            pr.w("tensor", "ev", evv)

                            if rc >= 2:
                                pr.w("tensor", "ac", cp_of.get(rc - 2, 0))

                            def ropefwd(t, rc=rc, pfw=pfw):
                                r = None
                                for stt in range(4):
                                    r = t.transpose(pfw[:, stt * 64:(stt + 1) * 64],
                                                    ev_sb[rc % 4][:, stt * 128:(stt + 1) * 128],
                                                    ident[0:64, 0:64])
                                return r
                            pr.op("tensor", ropefwd, "mm")
                            fw_of[rc] = C["mm"]
                            pr.w("scalar", "mm", C["mm"])
                            if rc >= 2:
                                pr.w("scalar", "ev", rv_of.get(rc - 2, 0))
                            pr.op("scalar", lambda s, pfw=pfw, ro=ro: s.copy(
                                rtm[:, ro:ro + 256], pfw[:, 0:256]), "ac")
                            cp_of[rc] = C["ac"]
                            pr.w("vector", "ac")
                            if rc >= 2:
                                pr.w("vector", "mm", bw_of.get(rc - 2, 0))

                            def ropev(v, tc=tc, ro=ro, rh=rh):
                                ecs = cs_sb[:, tc * 128:(tc + 1) * 128]
                                esn = sn_sb[:, tc * 128:(tc + 1) * 128]
                                ev_ = rtm[:, ro:ro + 256:2]
                                od_ = rtm[:, ro + 1:ro + 256:2]
                                v.tensor_tensor(rtm2[:, ro:ro + 256:2], ev_, ecs, op=ALU.mult)
                                v.tensor_tensor(rsc[:, rh:rh + 128], od_, esn, op=ALU.mult)
                                v.tensor_tensor(rtm2[:, ro + 1:ro + 256:2], ev_, esn, op=ALU.mult)
                                v.tensor_tensor(rsc2[:, rh:rh + 128], od_, ecs, op=ALU.mult)
                                v.tensor_tensor(rtm2[:, ro:ro + 256:2], rtm2[:, ro:ro + 256:2],
                                                rsc[:, rh:rh + 128], op=ALU.subtract)
                                return v.tensor_tensor(rtm2[:, ro + 1:ro + 256:2],
                                                       rtm2[:, ro + 1:ro + 256:2],
                                                       rsc2[:, rh:rh + 128], op=ALU.add)
                            pr.op("vector", ropev, "ev")
                            rv_of[rc] = C["ev"]
                            rv = C["ev"]

                            def d_rtail(grp=grp, mi=mi, pb=pb, col0=col0, pbw=pbw, ro=ro,
                                        rv=rv, rc=rc):
                                pr.w("tensor", "ev", rv)
                                if rc >= 2:
                                    pr.w("tensor", "ev", st_of.get(rc - 2, 0))

                                def ropebwd(t, pbw=pbw, ro=ro):
                                    r = None
                                    for stt in range(4):
                                        r = t.transpose(pbw[0:64, stt * 128:(stt + 1) * 128],
                                                        rtm2[:, ro + stt * 64:ro + (stt + 1) * 64],
                                                        ident[:, :])
                                    return r
                                pr.op("tensor", ropebwd, "mm")
                                bw_of[rc] = C["mm"]
                                pr.w("vector", "mm", C["mm"])
                                dst = (qT_sb, kT_sb)[mi]
                                pr.op("vector", lambda v, dst=dst, pb=pb, col0=col0, pbw=pbw:
                                      v.tensor_copy(dst[pb:pb + 64, col0:col0 + QT],
                                                    pbw[0:64, 0:QT]), "ev")
                                ev_done[grp] = C["ev"]
                                st_of[rc] = C["ev"]
                            part2.append(d_rtail)
                        part2.append(d_rchain)
                        flush_part2(2)
            mm_done[(idx, "xfree")] = C["mm"]
        flush_part2(0)
        pr.block()

        for h in range(H):
            for qk in range(2):
                src = (qT_sb, kT_sb)[qk]
                col0 = h * T if layer == 0 else (h // 2) * T
                pb = 0 if layer == 0 else (h % 2) * 64
                for c in range(NQT):
                    pr.w("scalar", "mm")
                    pr.op("scalar", lambda s, src=src, pb=pb, col0=col0, c=c: s.activation(
                        sq_scr[0:HD, :], src[pb:pb + HD, col0 + c * QT: col0 + (c + 1) * QT],
                        AF.Square), "ac")
                    pr.w("tensor", "ac")
                    pr.w("tensor", "ev")
                    pr.op("tensor", lambda t: t.matmul(ps_pr[0:1, :], ones_cb[0:HD, :],
                                                       sq_scr[0:HD, :], start=True, stop=True),
                          "mm")
                    pr.w("vector", "mm")
                    pr.op("vector", lambda v, c=c: v.tensor_reduce(
                        stat[:, c:c + 1], ps_pr[0:1, :], axis=AX.X, op=ALU.max), "ev")
                pr.w("vector", "ev")
                pr.op("vector", lambda v, h=h, qk=qk: v.tensor_reduce(
                    stat[:, 32 + h * 2 + qk: 33 + h * 2 + qk], stat[:, 0:NQT],
                    axis=AX.X, op=ALU.max), "ev")
        pr.block()
        sc = 1.0 / math.sqrt(HD) if layer == 0 else 1.0 / 8.0
        pr.w("vector", "ev")
        for h in range(H):
            pr.op("vector", lambda v, h=h: v.tensor_tensor(
                stat[:, 40 + h:41 + h], stat[:, 32 + 2 * h:33 + 2 * h],
                stat[:, 33 + 2 * h:34 + 2 * h], op=ALU.mult), "ev")
        pr.w("scalar", "ev")
        pr.op("scalar", lambda s: s.activation(stat[:, 48:48 + H], stat[:, 40:40 + H],
                                               AF.Ln), "ac")
        pr.w("scalar", "ac")
        pr.op("scalar", lambda s: s.activation(stat[:, 48:48 + H], stat[:, 48:48 + H],
                                               AF.Exp, scale=0.5), "ac")
        pr.w("vector", "ac")
        pr.op("vector", lambda v: v.tensor_scalar(stat[:, 48:48 + H], stat[:, 48:48 + H],
                                                  -sc, None, op0=ALU.mult), "ev")
        pr.w("tensor", "ev")
        for h in range(H):
            pr.op("tensor", lambda t, h=h: t.matmul(ps_pr[:, h:h + 1], ones_r[:, :],
                                                    stat[:, 48 + h:49 + h],
                                                    start=True, stop=True), "mm")
        pr.w("vector", "mm")
        pr.op("vector", lambda v: v.tensor_copy(bias_sb[:, :], ps_pr[:, 0:H]), "ev")
        pr.block()
        ph_es.close()

        fl = ExitStack()
        ps_s = [fl.enter_context(nc.psum_tensor(pfx + "ps_s%d" % i, [P, QT], F32)) for i in range(4)]
        ps_oa = [fl.enter_context(nc.psum_tensor(pfx + "ps_oa%d" % i, [65, QT], F32))
                 for i in range(H)]
        ps_obm = (fl.enter_context(nc.psum_tensor(pfx + "ps_obm", [P, QT], F32))
                  if layer == 0 else None)
        ps_ot = fl.enter_context(nc.psum_tensor(pfx + "ps_ot", [P, QT], BF16))
        ps_ep = fl.enter_context(nc.psum_tensor(pfx + "ps_ep", [P, QT], F32))
        stg = [es.enter_context(nc.sbuf_tensor(pfx + "stg%d" % i, [P, QT], BF16))
               for i in range(2)]
        with fl:
            si = 0
            cci = 0
            mkc = 0
            dv_of = {}
            px_of = {}
            ot_store = {0: C["mm"], 1: C["mm"]}
            stg_store = {0: C["so0"], 1: C["so1"]}
            stg_evl = [0]
            scnt = [0]
            oaev = {}
            obac = {}
            tr1_of = {}
            epf_free = [0]
            prev_unit = []
            ep_units = []
            pend = []

            def emit_tr2_prev(keep=1):
                while len(prev_unit) > keep:
                    emit_tr2_one()

            def emit_tr2_one():
                uqt, h, stt, fin = prev_unit.pop(0)
                pr.w("tensor", "ev", fin)
                if stt == 0:
                    pr.w("tensor", "ev", stg_evl[0])
                pr.op("tensor", lambda t, stt=stt: t.transpose(
                    ps_ot[0:HD, stt * 128:(stt + 1) * 128],
                    otile[stt % 2][:, :], identb[:, :]), "mm")
                ot_store[stt % 2] = C["mm"]
                if stt == 3:
                    scnt[0] += 1
                    par = scnt[0] % 2
                    ssem = "so%d" % par
                    pr.w("vector", "mm", C["mm"])
                    pr.w("vector", ssem, stg_store[par])
                    pr.op("vector", lambda v, par=par: v.tensor_copy(
                        stg[par][0:HD, :], ps_ot[0:HD, :]), "ev")
                    stg_evl[0] = C["ev"]
                    pr.w("sync", "ev", stg_evl[0])
                    pr.op("sync", lambda sy, uqt=uqt, h=h, par=par: sy.dma_start(
                        out=odst(h, uqt), in_=stg[par][0:HD, :]), ssem, 16)
                    stg_store[par] = C[ssem]

            def emit_unit(uqt, h, stt):
                emit_tr2_prev()
                pr.w("tensor", "ev", oaev[h])
                if layer == 0:
                    pr.w("tensor", "ac", obac[h])
                pr.w("tensor", "ev", epf_free[0])

                def tr1(t, h=h, stt=stt):
                    r = t.transpose(ps_ep[:, 0:65],
                                    oa_sbs[h][:, stt * 128:(stt + 1) * 128],
                                    ident[0:65, 0:65])
                    if layer == 0:
                        r = t.transpose(ps_ep[:, 65:129],
                                        ob_sbs[h][:, stt * 128:(stt + 1) * 128],
                                        ident[0:64, 0:64])
                    return r
                pr.op("tensor", tr1, "mm")
                tr1_of[h] = C["mm"]
                pr.w("vector", "mm", C["mm"])
                pr.op("vector", lambda v: v.reciprocal(linv[:, :], ps_ep[:, 64:65]), "ev")
                pr.w("vector", "ev")
                pr.w("vector", "mm", ot_store[stt % 2])
                if layer == 0:
                    pr.op("vector", lambda v: v.tensor_scalar(
                        ot_f[:, 0:64], ps_ep[:, 0:64],
                        linv[:, :], None, op0=ALU.mult))
                    pr.op("vector", lambda v: v.tensor_scalar(
                        ot_f[:, 64:128], ps_ep[:, 65:129],
                        linv[:, :], None, op0=ALU.mult), "ev")
                    epf_free[0] = C["ev"]
                    pr.w("vector", "ev")
                    pr.op("vector", lambda v: v.tensor_reduce(
                        gnmu[:, :], ot_f[:, :], axis=AX.X, op=ALU.add), "ev")
                    pr.w("scalar", "ev")
                    pr.op("scalar", lambda s: s.activation(
                        gn_scr[:, :], ot_f[:, :], AF.Square, accum_out=gnsq[:, :]), "ac")
                    pr.w("vector", "ac")
                    pr.op("vector", lambda v: v.tensor_scalar(
                        gnms[:, :], gnmu[:, :], 1.0 / 128.0, None, op0=ALU.mult))
                    pr.op("vector", lambda v: v.tensor_scalar(
                        gnsq[:, :], gnsq[:, :], 1.0 / 128.0, 1e-5,
                        op0=ALU.mult, op1=ALU.add), "ev")
                    pr.w("vector", "ev")
                    pr.op("vector", lambda v: v.tensor_tensor(
                        gnvar[:, :], gnms[:, :], gnms[:, :], op=ALU.mult), "ev")
                    pr.w("vector", "ev")
                    pr.op("vector", lambda v: v.tensor_tensor(
                        gnvar[:, :], gnsq[:, :], gnvar[:, :], op=ALU.subtract), "ev")
                    pr.w("scalar", "ev")
                    pr.op("scalar", lambda s: s.activation(gnrs[:, :], gnvar[:, :],
                                                           AF.Ln), "ac")
                    pr.w("scalar", "ac")
                    pr.op("scalar", lambda s: s.activation(gnrs[:, :], gnrs[:, :],
                                                           AF.Exp, scale=-0.5), "ac")
                    pr.w("vector", "ac")
                    pr.op("vector", lambda v, h=h: v.tensor_scalar(
                        rg_sb[:, :], gB[:, h * 128:(h + 1) * 128], gnrs[:, :],
                        None, op0=ALU.mult))
                    pr.op("vector", lambda v: v.tensor_scalar(
                        gn_f2[:, :], ot_f[:, :], gnms[:, :], None,
                        op0=ALU.subtract), "ev")
                    pr.w("vector", "ev")
                    pr.op("vector", lambda v: v.tensor_tensor(
                        ot_f[:, :], gn_f2[:, :], rg_sb[:, :], op=ALU.mult), "ev")
                    pr.w("vector", "ev")
                    pr.op("vector", lambda v, h=h, stt=stt: v.tensor_tensor(
                        otile[stt % 2][:, :], ot_f[:, :], bB[:, h * 128:(h + 1) * 128],
                        op=ALU.add), "ev")
                else:
                    pr.op("vector", lambda v, stt=stt: v.tensor_scalar(
                        otile[stt % 2][:, :], ps_ep[:, 0:64],
                        linv[:, :], None, op0=ALU.mult), "ev")
                    epf_free[0] = C["ev"]
                prev_unit.append((uqt, h, stt, C["ev"]))

            def emit_av(args):
                qt, kc, h, s_i, dmask, first, last = args
                pr.w("tensor", "px", B0["px"] + s_i + 1)
                if dmask is not None:
                    pr.w("tensor", "mk", dmask)
                if first:
                    pr.w("tensor", "ev", oaev.get(h, 0))
                    if layer == 0:
                        pr.w("tensor", "ac", obac.get(H - 1, 0))

                def av(t, qt=qt, kc=kc, h=h, s_i=s_i, first=first, last=last):
                    r = t.matmul(ps_oa[h][:, :], v1[:, h * 4160 + kc * 65: h * 4160 + (kc + 1) * 65],
                                 pt_sb[s_i % 4][:, :], start=first, stop=last)
                    if layer == 0:
                        r = t.matmul(ps_obm[h * 64:(h + 1) * 64, :],
                                     v2[:, h * 4096 + kc * 64: h * 4096 + (kc + 1) * 64],
                                     pt_sb[s_i % 4][:, :], start=first, stop=last)
                    return r
                pr.op("tensor", av, "av")

            qt_iter = list(range(NQT)) if flash_qts is None else list(flash_qts)
            for qt in qt_iter:
                KC = 4 * qt + 4
                for kc in range(KC):
                    if layer == 0:
                        if cci >= 4:
                            pr.w("sync", "dv", B0["dv"] + (cci - 3) * H)
                        delta = qt * QT - kc * 128
                        pr.op("sync", lambda sy, cci=cci, delta=delta: sy.dma_start(
                            out=dt_sb[cci % 4][:, :],
                            in_=bass.AP(g2, OFF + delta, [[GW2, P], [1, QT]])),
                            "dts%d" % (cci % 4), 16)
                    for h in range(H):
                        first = (kc == 0)
                        last = (kc == KC - 1)
                        diag = (kc * 128 + 127 > qt * QT)
                        if si >= 4:
                            if layer == 0:
                                pr.w("tensor", "dv", dv_of[si - 4])
                            else:
                                pr.w("tensor", "px", px_of[si - 4])

                        def smm(t, qt=qt, kc=kc, h=h, s_i=si):
                            if layer == 0:
                                return t.matmul(ps_s[s_i % 4][:, :],
                                                kT_sb[:, h * T + kc * 128: h * T + (kc + 1) * 128],
                                                qT_sb[:, h * T + qt * QT: h * T + (qt + 1) * QT],
                                                start=True, stop=True)
                            pb = (h % 2) * 64
                            cb = (h // 2) * T
                            return t.matmul(ps_s[s_i % 4][:, :],
                                            kT_sb[pb:pb + 64, cb + kc * 128: cb + (kc + 1) * 128],
                                            qT_sb[pb:pb + 64, cb + qt * QT: cb + (qt + 1) * QT],
                                            start=True, stop=True)
                        pr.op("tensor", smm, "sm")
                        if layer == 0:
                            pr.w("vector", "sm")
                            pr.w("vector", "dts%d" % (cci % 4),
                                 B0["dts%d" % (cci % 4)] + 16 * (cci // 4 + 1))
                            if si >= 4:
                                pr.w("vector", "px", px_of[si - 4])

                            def dmul(v, s_i=si, cci=cci):
                                return v.tensor_tensor(sp_sb[s_i % 4][:, :],
                                                       ps_s[s_i % 4][:, :],
                                                       dt_sb[cci % 4][:, :], op=ALU.mult)
                            pr.op("vector", dmul, "dv")
                            dv_of[si] = C["dv"]
                        if layer == 0:
                            pr.w("scalar", "dv", dv_of[si])
                        else:
                            pr.w("scalar", "sm")
                        if si >= 4:
                            pr.w("scalar", "av", B0["av"] + si - 3)

                        def expo(s, s_i=si, h=h):
                            src = sp_sb[s_i % 4][:, :] if layer == 0 else ps_s[s_i % 4][:, :]
                            scl = 1.0 if layer == 0 else 0.125
                            return s.activation(pt_sb[s_i % 4][:, :], src, AF.Exp,
                                                bias=bias_sb[:, h:h + 1], scale=scl)
                        pr.op("scalar", expo, "px")
                        px_of[si] = C["px"]
                        dmask = None
                        if diag:
                            pr.w("vector", "px", px_of[si])
                            bi = (kc * 128 - qt * QT) // 128
                            pr.op("vector", lambda v, s_i=si, bi=bi: v.tensor_tensor(
                                pt_sb[s_i % 4][:, :], pt_sb[s_i % 4][:, :],
                                binm[:, bi * QT:(bi + 1) * QT], op=ALU.mult), "mk")
                            mkc += 1
                            dmask = C["mk"]
                        pend.append((qt, kc, h, si, dmask, first, last))
                        si += 1
                    while len(pend) > 3:
                        emit_av(pend.pop(0))
                    cci += 1
                    if ep_units and (layer == 1 or kc % 2 == 1
                                     or len(ep_units) >= KC - 1 - kc):
                        emit_unit(*ep_units.pop(0))
                        if ep_units and len(ep_units) >= KC - 1 - kc:
                            emit_unit(*ep_units.pop(0))
                while pend:
                    emit_av(pend.pop(0))
                while ep_units:
                    emit_unit(*ep_units.pop(0))
                avf = C["av"]
                for h in range(H):
                    pr.w("vector", "av", avf)
                    pr.w("vector", "mm", tr1_of.get(h, 0))
                    pr.op("vector", lambda v, h=h: v.tensor_copy(
                        oa_sbs[h][:, :], ps_oa[h][:, :]), "ev")
                    oaev[h] = C["ev"]
                    if layer == 0:
                        pr.w("scalar", "av", avf)
                        pr.w("scalar", "mm", tr1_of.get(h, 0))
                        pr.op("scalar", lambda s, h=h: s.copy(
                            ob_sbs[h][:, :], ps_obm[h * 64:(h + 1) * 64, :]), "ac")
                        obac[h] = C["ac"]
                ep_units = [(qt, h, stt) for h in range(H) for stt in range(4)]
            while ep_units:
                emit_unit(*ep_units.pop(0))
            emit_tr2_prev(keep=0)
            pr.w("gpsimd", "so0")
            pr.w("gpsimd", "so1")
            pr.op("gpsimd", lambda g: g.nop())
            pr.block()


def emit_mix(nc, pr, layer, pfx, ochunk, xT, yT, W, ybf=None, ag_hook=None):
    E = 6 if layer == 0 else 8
    NE = 7 if layer == 0 else 8
    TOPK = 2 if layer == 0 else 4
    IC_UP = 8 if layer == 0 else 16
    NTH = 1 if layer == 0 else 2
    IC_DN_ = 8 if layer == 0 else 16
    THT = NT // NTH
    NTC = THT // QT
    wo, wob, gw, gb = W["wo"], W["wob"], W["gw"], W["gb"]
    if layer == 0:
        w1t, w3t, w2t = W["w1t"], W["w3t"], W["w2t"]
    else:
        gupt, gupb, dnt, dnb = W["gupt"], W["gupb"], W["dnt"], W["dnb"]

    es = ExitStack()

    def sbuf(name, shape, dt=F32):
        return es.enter_context(nc.sbuf_tensor(pfx + name, shape, dt))

    ident = sbuf("ident", [P, P])
    ones_r = sbuf("ones_r", [1, P])
    ones_rq = sbuf("ones_rq", [1, QT])
    ones_cb = sbuf("ones_cb", [P, 1], BF16)
    x_sb = sbuf("x_sb", [P, 8 * NT])
    xb_sb = sbuf("xb_sb", [P, 8 * NT], BF16)
    gw_sb = sbuf("gw_sb", [P, 8 * E], BF16)
    gb_sb = sbuf("gb_sb", [E, 1])
    ge_tm = sbuf("ge_tm", [P, 16 * E])
    lg_sb = sbuf("lg_sb", [E, NT])
    lt_sb = sbuf("lt_sb", [P, E])
    l2_sb = [sbuf("l2_sb%d" % i, [P, E]) for i in range(2)]
    sel_sb = sbuf("sel_sb", [P, E])
    mrow = [sbuf("mrow%d" % i, [P, 1]) for i in range(6)]
    sg_scr = sbuf("sg_scr", [P, 2 * QT], BF16)
    t1_scr = sbuf("t1_scr", [P, 2 * QT])
    t2_scr = sbuf("t2_scr", [P, 2 * QT])
    rb_scr = sbuf("rb_scr", [P, QT], BF16)
    sq_scr = sbuf("sq_scr", [P, QT], BF16)
    rowa = sbuf("rowa", [1, QT])
    rowb = sbuf("rowb", [1, QT])
    rowc = sbuf("rowc", [1, QT])

    C = pr.cnt
    B0 = dict(C)

    with es:
        ps_a = es.enter_context(nc.psum_tensor(pfx + "ps_a", [P, QT], F32))
        ps_b2 = es.enter_context(nc.psum_tensor(pfx + "ps_b2", [P, QT], F32))
        ps_r1 = es.enter_context(nc.psum_tensor(pfx + "ps_r1", [P, QT], F32))
        ps_r2 = es.enter_context(nc.psum_tensor(pfx + "ps_r2", [P, QT], F32))
        ps_c1 = es.enter_context(nc.psum_tensor(pfx + "ps_c1", [P, QT], F32))
        ps_c2 = es.enter_context(nc.psum_tensor(pfx + "ps_c2", [P, QT], F32))
        ps_t = es.enter_context(nc.psum_tensor(pfx + "ps_t", [P, QT], F32))
        ps_g = es.enter_context(nc.psum_tensor(pfx + "ps_g", [P, QT], F32))

        def g_memsets(g):
            g.memset(ident[:, :], 0.0)
            g.memset(ones_r[:, :], 1.0)
            g.memset(ones_rq[:, :], 1.0)
            return g.memset(ones_cb[:, :], 1.0)
        pr.op("gpsimd", g_memsets, "gx")
        pr.w("gpsimd", "gx")
        pr.op("gpsimd", lambda g: g.affine_select(
            out=ident[:, :], in_=ident[:, :], compare_op=ALU.not_equal,
            fill=1.0, base=0, pattern=[[-1, P]], channel_multiplier=1), "gx")
        o_es = ExitStack()
        wo_sb = o_es.enter_context(nc.sbuf_tensor(pfx + "wo_sb", [P, 8 * D], BF16))
        wob_sb = o_es.enter_context(nc.sbuf_tensor(pfx + "wob_sb", [1, D], F32))
        o_tc = [o_es.enter_context(nc.sbuf_tensor(pfx + "o_tc%d" % i, [P, 8 * QT], BF16))
                for i in range(2)]
        for dc in range(8):
            pr.op("gpsimd", lambda g, d=dc: g.dma_start(
                out=wo_sb[:, d * D:(d + 1) * D], in_=wo[d * P:(d + 1) * P, :]), "lw", 16)
            pr.op("gpsimd", lambda g, d=dc: g.dma_start(
                out=gw_sb[:, d * E:(d + 1) * E], in_=gw[d * P:(d + 1) * P, :]), "lw", 16)
        pr.op("sync", lambda sy: sy.dma_start(
            out=wob_sb[:, :], in_=bass.AP(wob, 0, [[1, 1], [1, D]])), "lx", 16)
        pr.op("sync", lambda sy: sy.dma_start(
            out=gb_sb[:, :], in_=bass.AP(gb, 0, [[1, E], [1, 1]])), "lx", 16)
        for dc in range(8):
            pr.op("sync", lambda sy, d=dc: sy.dma_start(
                out=x_sb[:, d * NT:(d + 1) * NT], in_=xT[d * P:(d + 1) * P, :]), "lx", 16)
        pr.block()

        wo_mmN = {}
        wo_ev = {}
        for tc in range(4):
            losem = "lo%d" % (tc % 2)
            if tc >= 2:
                pr.w("sync", "mm", wo_mmN[tc - 2])
            for dc in range(8):
                pr.op("sync", lambda sy, d=dc, tc=tc: sy.dma_start(
                    out=o_tc[tc % 2][:, d * QT:(d + 1) * QT], in_=ochunk(d, tc)), losem, 16)
            for oc in range(8):
                g_ = tc * 8 + oc
                bank = (ps_a, ps_b2)[g_ % 2]
                pr.w("tensor", "lw")
                pr.w("tensor", "lx")
                pr.w("tensor", losem, B0[losem] + (tc // 2 + 1) * 128)
                if wo_ev.get(g_ - 2, 0):
                    pr.w("tensor", "ev", wo_ev[g_ - 2])

                def womm(t, tc=tc, oc=oc, bank=bank):
                    for dc in range(8):
                        t.matmul(bank[:, :], wo_sb[:, dc * D + oc * P: dc * D + (oc + 1) * P],
                                 o_tc[tc % 2][:, dc * QT:(dc + 1) * QT],
                                 start=(dc == 0), stop=False)
                    return t.matmul(bank[:, :], wob_sb[:, oc * P:(oc + 1) * P],
                                    ones_rq[:, :], start=False, stop=True)
                pr.op("tensor", womm, "mm")
                pr.w("vector", "mm")
                pr.op("vector", lambda v, tc=tc, oc=oc, bank=bank: v.tensor_tensor(
                    x_sb[:, oc * NT + tc * QT: oc * NT + (tc + 1) * QT],
                    x_sb[:, oc * NT + tc * QT: oc * NT + (tc + 1) * QT],
                    bank[:, :], op=ALU.add), "ev")
                wo_ev[g_] = C["ev"]
            wo_mmN[tc] = C["mm"]
        pr.block()
        o_es.close()
        ge_row = sbuf("ge_row", [1, NT])
        geB = sbuf("geB", [P, NT])
        h_sb = sbuf("h_sb", [P, IC_UP * THT], BF16)
        wu1 = [sbuf("wu1_%d" % i, [P, 8 * P], BF16) for i in range(2)]
        wu3 = [sbuf("wu3_%d" % i, [P, 8 * P], BF16) for i in range(2)]
        wd2 = [sbuf("wd2_%d" % i, [P, IC_DN_ * P], BF16) for i in range(2)]
        if layer == 1:
            gub_pm = sbuf("gub_pm", [P, 32])
            gub_tmp = sbuf("gub_tmp", [32, P])
            dnb_sb = sbuf("dnb_sb", [E, D])
            cbT = sbuf("cbT", [8, QT])
            pr.op("sync", lambda sy: sy.dma_start(out=dnb_sb[:, :], in_=dnb[:, :]), "lx", 16)

        def emit_ln(tcs=(0, 1, 2, 3)):
            banks = {0: (ps_r1, ps_r2), 1: (ps_a, ps_b2)}
            rowfree = {0: C["ev"], 1: C["ev"]}
            rowsmm = {}
            bc_mm = [0]
            xrdy = C["ev"]

            def stats(tc):
                b1, b2 = banks[tc % 2]
                for dc in range(8):
                    pr.w("scalar", "ev", xrdy)
                    pr.w("scalar", "ac")
                    pr.w("scalar", "mm")
                    pr.op("scalar", lambda s, tc=tc, dc=dc: s.copy(
                        rb_scr[:, :],
                        x_sb[:, dc * NT + tc * QT: dc * NT + (tc + 1) * QT]), "ac")
                    pr.w("scalar", "ac")
                    pr.op("scalar", lambda s: s.activation(
                        sq_scr[:, :], rb_scr[:, :], AF.Square), "ac")
                    pr.w("tensor", "ac")
                    if dc == 0:
                        pr.w("tensor", "ev", rowfree[tc % 2])

                    def lnmm(t, dc=dc, b1=b1, b2=b2):
                        t.matmul(b1[0:1, :], ones_cb[:, :], rb_scr[:, :],
                                 start=(dc == 0), stop=(dc == 7))
                        return t.matmul(b2[0:1, :], ones_cb[:, :], sq_scr[:, :],
                                        start=(dc == 0), stop=(dc == 7))
                    pr.op("tensor", lnmm, "mm")
                rowsmm[tc] = C["mm"]

            def rows_and_norm(tc):
                b1, b2 = banks[tc % 2]
                pr.w("vector", "mm", rowsmm[tc])
                if bc_mm[0]:
                    pr.w("vector", "mm", bc_mm[0])
                pr.op("vector", lambda v, b1=b1: v.tensor_scalar(
                    rowa[:, :], b1[0:1, :], 1.0 / D, None, op0=ALU.mult), "ev")
                pr.w("vector", "ev")
                pr.op("vector", lambda v, b2=b2: v.tensor_scalar(
                    rowb[:, :], b2[0:1, :], 1.0 / D, 1e-5, op0=ALU.mult, op1=ALU.add),
                    "ev")
                rowfree[tc % 2] = C["ev"]
                pr.w("vector", "ev")
                pr.op("vector", lambda v: v.tensor_tensor(
                    rowc[:, :], rowa[:, :], rowa[:, :], op=ALU.mult), "ev")
                pr.w("vector", "ev")
                pr.op("vector", lambda v: v.tensor_tensor(
                    rowb[:, :], rowb[:, :], rowc[:, :], op=ALU.subtract), "ev")
                pr.w("scalar", "ev")
                pr.op("scalar", lambda s: s.activation(rowb[:, :], rowb[:, :], AF.Ln),
                      "ac")
                pr.w("scalar", "ac")
                pr.op("scalar", lambda s: s.activation(rowb[:, :], rowb[:, :], AF.Exp,
                                                       scale=-0.5), "ac")
                pr.w("vector", "ac")
                pr.op("vector", lambda v: v.tensor_tensor(
                    rowa[:, :], rowa[:, :], rowb[:, :], op=ALU.mult), "ev")
                pr.w("tensor", "ev")

                def bcmm(t):
                    t.matmul(ps_c1[:, :], ones_r[:, :], rowb[:, :], start=True, stop=True)
                    return t.matmul(ps_c2[:, :], ones_r[:, :], rowa[:, :],
                                    start=True, stop=True)
                pr.op("tensor", bcmm, "mm")
                bc_mm[0] = C["mm"]
                pr.w("vector", "mm")
                for dc in range(8):
                    pr.op("vector", lambda v, dc=dc, tc=tc: v.tensor_tensor(
                        t1_scr[:, 0:QT], x_sb[:, dc * NT + tc * QT: dc * NT + (tc + 1) * QT],
                        ps_c1[:, :], op=ALU.mult), "ev")
                    pr.w("vector", "ev")
                    pr.op("vector", lambda v, dc=dc, tc=tc: v.tensor_tensor(
                        x_sb[:, dc * NT + tc * QT: dc * NT + (tc + 1) * QT],
                        t1_scr[:, 0:QT], ps_c2[:, :], op=ALU.subtract), "ev")
                    pr.w("vector", "ev")
                    pr.op("vector", lambda v, dc=dc, tc=tc: v.tensor_copy(
                        xb_sb[:, dc * NT + tc * QT: dc * NT + (tc + 1) * QT],
                        x_sb[:, dc * NT + tc * QT: dc * NT + (tc + 1) * QT]), "ev")
                    pr.w("vector", "ev")

            prev = None
            for tc in tcs:
                stats(tc)
                if prev is not None:
                    rows_and_norm(prev)
                prev = tc
            rows_and_norm(prev)

        emit_ln()
        pr.block()

        ge_cols = {}
        for tc in range(4):
            pr.w("tensor", "ev")

            def gmm(t, tc=tc):
                r = None
                for dc in range(8):
                    r = t.matmul(ps_a[0:E, :], gw_sb[:, dc * E:(dc + 1) * E],
                                 xb_sb[:, dc * NT + tc * QT: dc * NT + (tc + 1) * QT],
                                 start=(dc == 0), stop=(dc == 7))
                return r
            pr.op("tensor", gmm, "mm")
            if layer == 0:
                pr.w("scalar", "mm")
                pr.w("scalar", "lx")
                pr.op("scalar", lambda s, tc=tc: s.activation(
                    lg_sb[:, tc * QT:(tc + 1) * QT], ps_a[0:E, :], AF.Sigmoid,
                    bias=gb_sb[:, :]), "ac")
                pr.w("tensor", "ac")
            else:
                pr.w("vector", "mm")
                pr.w("vector", "lx")
                pr.op("vector", lambda v, tc=tc: v.tensor_scalar(
                    lg_sb[:, tc * QT:(tc + 1) * QT], ps_a[0:E, :], gb_sb[:, :], None,
                    op0=ALU.add), "ev")
                pr.w("tensor", "ev")
        for tt in range(16):
            pr.w("tensor", "ev")

            def ltr(t, tt=tt):
                return t.transpose(ps_t[:, 0:E], lg_sb[:, tt * P:(tt + 1) * P],
                                   ident[0:E, 0:E])
            pr.op("tensor", ltr, "mm")
            pr.w("vector", "mm")
            pr.op("vector", lambda v: v.tensor_copy(lt_sb[:, :], ps_t[:, 0:E]), "ev")
            pr.w("vector", "ev")
            pr.op("vector", lambda v: v.tensor_reduce(
                mrow[0][:, :], lt_sb[:, :], axis=AX.X, op=ALU.max), "ev")
            pr.w("vector", "ev")
            nk = TOPK
            cur = lt_sb
            for j in range(nk - 1):
                nxt = l2_sb[j % 2]
                pr.op("vector", lambda v, cur=cur, j=j: v.tensor_scalar(
                    sel_sb[:, :], cur[:, :], mrow[j][:, :], -1.0e30,
                    op0=ALU.is_ge, op1=ALU.mult), "ev")
                pr.w("vector", "ev")
                pr.op("vector", lambda v, cur=cur, nxt=nxt: v.tensor_tensor(
                    nxt[:, :], cur[:, :], sel_sb[:, :], op=ALU.add), "ev")
                pr.w("vector", "ev")
                pr.op("vector", lambda v, nxt=nxt, j=j: v.tensor_reduce(
                    mrow[j + 1][:, :], nxt[:, :], axis=AX.X, op=ALU.max), "ev")
                pr.w("vector", "ev")
                cur = nxt
            pr.op("vector", lambda v, nk=nk: v.tensor_scalar(
                sel_sb[:, :], lt_sb[:, :], mrow[nk - 1][:, :], None, op0=ALU.is_ge), "ev")
            pr.w("vector", "ev")
            if layer == 0:
                pr.op("vector", lambda v: v.tensor_scalar(
                    mrow[4][:, :], mrow[0][:, :], mrow[1][:, :], 1e-8,
                    op0=ALU.add, op1=ALU.add), "ev")
                pr.w("vector", "ev")
                pr.op("vector", lambda v: v.reciprocal(mrow[5][:, :], mrow[4][:, :]), "ev")
                pr.w("vector", "ev")
                pr.op("vector", lambda v: v.tensor_tensor(
                    l2_sb[0][:, :], sel_sb[:, :], lt_sb[:, :], op=ALU.mult), "ev")
                pr.w("vector", "ev")
                pr.op("vector", lambda v, tt=tt: v.tensor_scalar(
                    ge_tm[:, tt * E:(tt + 1) * E], l2_sb[0][:, :], mrow[5][:, :], None,
                    op0=ALU.mult), "ev")
                pr.w("vector", "ev")
            else:
                pr.op("vector", lambda v: v.tensor_scalar(
                    l2_sb[0][:, :], lt_sb[:, :], mrow[0][:, :], None, op0=ALU.subtract),
                    "ev")
                pr.w("scalar", "ev")
                pr.op("scalar", lambda s: s.activation(
                    sg_scr[0:P, 0:E], l2_sb[0][:, :], AF.Exp), "ac")
                pr.w("vector", "ac")
                pr.op("vector", lambda v: v.tensor_tensor(
                    l2_sb[1][:, :], sel_sb[:, :], sg_scr[0:P, 0:E], op=ALU.mult), "ev")
                pr.w("vector", "ev")
                pr.op("vector", lambda v: v.tensor_reduce(
                    mrow[4][:, :], l2_sb[1][:, :], axis=AX.X, op=ALU.add), "ev")
                pr.w("vector", "ev")
                pr.op("vector", lambda v: v.reciprocal(mrow[5][:, :], mrow[4][:, :]), "ev")
                pr.w("vector", "ev")
                pr.op("vector", lambda v, tt=tt: v.tensor_scalar(
                    ge_tm[:, tt * E:(tt + 1) * E], l2_sb[1][:, :], mrow[5][:, :], None,
                    op0=ALU.mult), "ev")
                pr.w("vector", "ev")
        pr.block()


        IC_DN = 8 if layer == 0 else 16
        NDP = 2 if layer == 0 else 1
        h_w1 = w1t.tensor if layer == 0 else gupt.tensor
        h_w3 = w3t.tensor if layer == 0 else None
        h_w2 = w2t.tensor if layer == 0 else dnt.tensor

        upf = {}
        dnf = {}
        upc = [0]
        dnc = [0]
        ucnt = [0]
        gub_rdy = [(0, 0)]
        act_done = {}
        sg_of = {}
        stt_of = {}
        h_of = {}
        vcnt = [0]
        dn_ev = {}
        ub_guard = [0]

        def up_load(e, ic, th):
            par = upc[0] % 2
            pr.w("gpsimd", "mm", upf.get(upc[0] - 2, 0))
            if layer == 0:
                pr.op("gpsimd", lambda g, e=e, ic=ic, par=par: g.dma_start(
                    out=wu1[par][:, :],
                    in_=bass.AP(h_w1, e * D * D + ic * 128,
                                [[D, P], [P * D, 8], [1, P]])), "lw0" if par == 0 else "lw1", 16)
                pr.op("gpsimd", lambda g, e=e, ic=ic, par=par: g.dma_start(
                    out=wu3[par][:, :],
                    in_=bass.AP(h_w3, e * D * D + ic * 128,
                                [[D, P], [P * D, 8], [1, P]])), "lw0" if par == 0 else "lw1", 16)
            else:
                pr.op("gpsimd", lambda g, e=e, ic=ic, par=par: g.dma_start(
                    out=wu1[par][:, :],
                    in_=bass.AP(h_w1, e * D * 4096 + ic * 128,
                                [[4096, P], [P * 4096, 8], [1, P]])), "lw0" if par == 0 else "lw1", 16)
                pr.op("gpsimd", lambda g, e=e, ic=ic, par=par: g.dma_start(
                    out=wu3[par][:, :],
                    in_=bass.AP(h_w1, e * D * 4096 + (16 + ic) * 128,
                                [[4096, P], [P * 4096, 8], [1, P]])), "lw0" if par == 0 else "lw1", 16)
            upc[0] += 1
            return par, C["lw%d" % par]

        for th in range(NTH):
            tok0 = th * THT
            for e in range(NE):
                shared = (layer == 0 and e == 6)
                if layer == 1:
                    pr.w("sync", "ev")
                    pr.op("sync", lambda sy, e=e: sy.dma_start(
                        out=gub_tmp[:, :],
                        in_=bass.AP(gupb.tensor, e * 4096, [[P, 32], [1, P]])), "lx", 16)
                    pr.w("tensor", "lx")
                    pr.w("tensor", "ev")
                    pr.op("tensor", lambda t: t.transpose(ps_t[:, 0:32], gub_tmp[:, :],
                                                          ident[0:32, 0:32]), "mm")
                    pr.w("scalar", "mm")
                    pr.op("scalar", lambda s: s.activation(
                        gub_pm[:, :], ps_t[:, 0:32], AF.Copy, scale=1.0), "ac")
                    pr.w("vector", "ac")
                    pr.op("vector", lambda v: v.tensor_scalar(
                        gub_pm[:, 16:32], gub_pm[:, 16:32], 1.0, None, op0=ALU.add), "ev")
                    pr.w("vector", "ev")
                    gub_rdy[0] = (C["ac"], C["ev"])
                ubanks = [(ps_a, ps_b2), (ps_c1, ps_c2)]
                for ic in range(IC_UP):
                    par, lwn = up_load(e, ic, th)
                    sem = "lw%d" % par
                    for tcc in range(NTC):
                        u = ucnt[0]
                        ucnt[0] += 1
                        pA, pB = ubanks[u % 2]
                        uq = slice((u % 2) * QT, (u % 2) * QT + QT)
                        pr.w("tensor", sem, lwn)
                        uw = max(act_done.get(u - 2, 0), ub_guard[0])
                        if uw:
                            pr.w("tensor", "ev", uw)

                        def upmm(t, par=par, tcc=tcc, tok0=tok0, pA=pA, pB=pB):
                            r = None
                            for dc in range(8):
                                t.matmul(pA[:, :], wu1[par][:, dc * P:(dc + 1) * P],
                                         xb_sb[:, dc * NT + tok0 + tcc * QT:
                                               dc * NT + tok0 + (tcc + 1) * QT],
                                         start=(dc == 0), stop=(dc == 7))
                            for dc in range(8):
                                r = t.matmul(pB[:, :], wu3[par][:, dc * P:(dc + 1) * P],
                                             xb_sb[:, dc * NT + tok0 + tcc * QT:
                                                   dc * NT + tok0 + (tcc + 1) * QT],
                                             start=(dc == 0), stop=(dc == 7))
                            return r
                        pr.op("tensor", upmm, "mm")
                        upf[upc[0] - 1] = C["mm"]
                        mmv = C["mm"]
                        hcol = ic * THT + tcc * QT
                        if layer == 0:
                            pr.w("scalar", "mm", mmv)
                            if act_done.get(u - 2, 0):
                                pr.w("scalar", "ev", act_done[u - 2])
                            pr.op("scalar", lambda s, pA=pA, uq=uq: s.activation(
                                sg_scr[:, uq], pA[:, :], AF.Sigmoid), "ac")
                            pr.w("vector", "ac")
                            pr.op("vector", lambda v, pA=pA, uq=uq: v.tensor_tensor(
                                t1_scr[:, uq], pA[:, :], sg_scr[:, uq], op=ALU.mult), "ev")
                            pr.w("vector", "ev")
                            pr.op("vector", lambda v, hcol=hcol, pB=pB, uq=uq: v.tensor_tensor(
                                h_sb[:, hcol:hcol + QT], t1_scr[:, uq], pB[:, :],
                                op=ALU.mult), "ev")
                        else:
                            pr.w("vector", "mm", mmv)
                            pr.w("vector", "ac", gub_rdy[0][0])
                            pr.w("vector", "ev", gub_rdy[0][1])
                            if u >= 2:
                                pr.w("vector", "ev", h_of.get(u - 2, 0))
                            pr.op("vector", lambda v, ic=ic, pA=pA, uq=uq: v.tensor_scalar(
                                t1_scr[:, uq], pA[:, :], gub_pm[:, ic:ic + 1], 7.0,
                                op0=ALU.add, op1=ALU.min), "ev")
                            pr.op("vector", lambda v, ic=ic, pB=pB, uq=uq: v.tensor_scalar(
                                t2_scr[:, uq], pB[:, :], gub_pm[:, 16 + ic:17 + ic], -6.0,
                                op0=ALU.add, op1=ALU.max), "ev")
                            act_done[u] = C["ev"]
                            pr.w("scalar", "ev")
                            if u >= 2:
                                pr.w("scalar", "ev", h_of.get(u - 2, 0))
                            pr.op("scalar", lambda s, uq=uq: s.activation(
                                sg_scr[:, uq], t1_scr[:, uq], AF.Sigmoid, scale=1.702), "ac")
                            sg_of[u] = C["ac"]
                            pr.w("vector", "ev")
                            pr.op("vector", lambda v, uq=uq: v.scalar_tensor_tensor(
                                t2_scr[:, uq], t2_scr[:, uq], 8.0, t1_scr[:, uq],
                                op0=ALU.min, op1=ALU.mult), "ev")
                            stt_of[u] = C["ev"]
                            pr.w("vector", "ac", sg_of[u])
                            pr.op("vector", lambda v, hcol=hcol, uq=uq: v.tensor_tensor(
                                h_sb[:, hcol:hcol + QT], t2_scr[:, uq], sg_scr[:, uq],
                                op=ALU.mult), "ev")
                            h_of[u] = C["ev"]
                        if layer == 0:
                            act_done[u] = C["ev"]
                if not shared:
                    for tc4 in range(4):
                        pr.w("tensor", "ev")

                        def getr(t, e=e, tc4=tc4):
                            r = None
                            for tt4 in range(4):
                                tt = tc4 * 4 + tt4
                                r = t.transpose(ps_t[0:1, tt4 * P:(tt4 + 1) * P],
                                                ge_tm[:, tt * E + e: tt * E + e + 1],
                                                ident[:, :])
                            return r
                        pr.op("tensor", getr, "mm")
                        pr.w("vector", "mm")
                        pr.op("vector", lambda v, tc4=tc4: v.tensor_copy(
                            ge_row[:, tc4 * QT:(tc4 + 1) * QT], ps_t[0:1, 0:QT]), "ev")
                        pr.w("tensor", "ev")
                        pr.op("tensor", lambda t, tc4=tc4: t.matmul(
                            ps_g[:, :], ones_r[:, :], ge_row[:, tc4 * QT:(tc4 + 1) * QT],
                            start=True, stop=True), "mm")
                        pr.w("vector", "mm")
                        pr.op("vector", lambda v, tc4=tc4: v.tensor_copy(
                            geB[:, tc4 * QT:(tc4 + 1) * QT], ps_g[:, :]), "ev")
                dn_guard = C["ev"]
                dbanks = [ps_a, ps_b2, ps_c1, ps_c2]
                for do in range(8):
                    par = dnc[0] % 2
                    sem = "lw%d" % (2 + par)
                    pr.w("gpsimd", "mm", dnf.get(dnc[0] - 2, 0))
                    rows = D if layer == 0 else 2048
                    pr.op("gpsimd", lambda g, e=e, do=do, rows=rows, par=par: g.dma_start(
                        out=wd2[par][:, :],
                        in_=bass.AP(h_w2, e * rows * D + do * P,
                                    [[D, P], [P * D, IC_DN], [1, P]])), sem, 16)
                    dnc[0] += 1
                    lwn = C[sem]
                    for tcc in range(NTC):
                        dv_ = vcnt[0]
                        vcnt[0] += 1
                        bank = dbanks[dv_ % 4]
                        tq = slice((dv_ % 2) * QT, (dv_ % 2) * QT + QT)
                        pr.w("tensor", sem, lwn)
                        pr.w("tensor", "ev", max(dn_ev.get(dv_ - 4, 0), dn_guard))

                        def dnmm(t, par=par, tcc=tcc, bank=bank):
                            r = None
                            for ic2 in range(IC_DN):
                                r = t.matmul(bank[:, :], wd2[par][:, ic2 * P:(ic2 + 1) * P],
                                             h_sb[:, ic2 * THT + tcc * QT:
                                                  ic2 * THT + (tcc + 1) * QT],
                                             start=(ic2 == 0), stop=(ic2 == IC_DN - 1))
                            return r
                        pr.op("tensor", dnmm, "mm")
                        dnf[dnc[0] - 1] = C["mm"]
                        mmv = C["mm"]
                        xcol = do * NT + tok0 + tcc * QT
                        pr.w("vector", "mm", mmv)
                        if shared:
                            pr.op("vector", lambda v, xcol=xcol, bank=bank: v.tensor_tensor(
                                x_sb[:, xcol:xcol + QT], x_sb[:, xcol:xcol + QT],
                                bank[:, :], op=ALU.add), "ev")
                        else:
                            pr.op("vector", lambda v, bank=bank, tcc=tcc, tok0=tok0, tq=tq:
                                  v.tensor_tensor(t2_scr[:, tq], bank[:, :],
                                                  geB[:, tok0 + tcc * QT:
                                                      tok0 + (tcc + 1) * QT],
                                                  op=ALU.mult), "ev")
                            pr.w("vector", "ev")
                            pr.op("vector", lambda v, xcol=xcol, tq=tq: v.tensor_tensor(
                                x_sb[:, xcol:xcol + QT], x_sb[:, xcol:xcol + QT],
                                t2_scr[:, tq], op=ALU.add), "ev")
                        dn_ev[dv_] = C["ev"]
                ub_guard[0] = C["ev"]

        if layer == 1:
            for tc in range(4):
                for tt4 in range(4):
                    tt = tc * 4 + tt4
                    pr.w("tensor", "ev")
                    pr.op("tensor", lambda t, tt=tt, tt4=tt4: t.transpose(
                        ps_t[0:E, tt4 * P:(tt4 + 1) * P], ge_tm[:, tt * E:(tt + 1) * E],
                        ident[:, :]), "mm")
                pr.w("vector", "mm")
                pr.op("vector", lambda v: v.tensor_copy(
                    cbT[:, :], ps_t[0:E, 0:QT]), "ev")
                for do in range(8):
                    pr.w("tensor", "ev")
                    pr.op("tensor", lambda t, tc=tc, do=do: t.matmul(
                        ps_a[:, :], dnb_sb[:, do * P:(do + 1) * P],
                        cbT[:, :], start=True, stop=True), "mm")
                    pr.w("vector", "mm")
                    xcol = do * NT + tc * QT
                    pr.op("vector", lambda v, xcol=xcol: v.tensor_tensor(
                        x_sb[:, xcol:xcol + QT], x_sb[:, xcol:xcol + QT],
                        ps_a[:, :], op=ALU.add), "ev")
        pr.block()


        st_half = 0
        if ybf is not None:
            emit_ln(tcs=(0, 1))
            pr.w("sync", "ev")
            for dc in range(8):
                pr.op("sync", lambda sy, d=dc: sy.dma_start(
                    out=ybf[0][d * P:(d + 1) * P, :],
                    in_=xb_sb[:, d * NT: d * NT + NT // 2]), "st", 16)
            st_half = C["st"]
            if ag_hook is not None:
                ag_hook()
            emit_ln(tcs=(2, 3))
            pr.block()
            for dc in range(8):
                pr.op("sync", lambda sy, d=dc: sy.dma_start(
                    out=yT[d * P:(d + 1) * P, :], in_=x_sb[:, d * NT:(d + 1) * NT]),
                    "st", 16)
                pr.op("sync", lambda sy, d=dc: sy.dma_start(
                    out=ybf[1][d * P:(d + 1) * P, :],
                    in_=xb_sb[:, d * NT + NT // 2: (d + 1) * NT]), "st", 16)
        else:
            emit_ln()
            pr.block()
            for dc in range(8):
                pr.op("sync", lambda sy, d=dc: sy.dma_start(
                    out=yT[d * P:(d + 1) * P, :], in_=x_sb[:, d * NT:(d + 1) * NT]),
                    "st", 16)
        pr.w("gpsimd", "st")
        pr.op("gpsimd", lambda g: g.nop())
        pr.block()
        return {"st_half": st_half}




SEM_NAMES = ["ld", "ld0", "ld1", "lw", "lw0", "lw1", "lw2", "lw3", "gx",
             "mm", "ev", "ac", "sm", "dv", "px", "mk", "av", "dts0", "dts1",
             "dts2", "dts3", "g2s", "glw", "so0", "so1", "lo0", "lo1", "lx",
             "st", "cc"]


def build_attn(layer, flash_qts=None):
    nc = bass.Bass("TRN2", target_bir_lowering=False)
    xT = [nc.dram_tensor("xT%d" % b, [D, T], F32, kind="ExternalInput").ap()
          for b in range(2)]
    wq = nc.dram_tensor("wq", [D, 128], F32, kind="ExternalInput").ap()
    wk = nc.dram_tensor("wk", [D, 128], F32, kind="ExternalInput").ap()
    wv = nc.dram_tensor("wv", [D, 128], F32, kind="ExternalInput").ap()
    theta = gam = bet = None
    if layer == 0:
        theta = nc.dram_tensor("theta", [8], F32, kind="ExternalInput")
        gam = nc.dram_tensor("gam", [128], F32, kind="ExternalInput")
        bet = nc.dram_tensor("bet", [128], F32, kind="ExternalInput")
    otm = [nc.dram_tensor("otm%d" % b, [128, T], BF16, kind="ExternalOutput").ap()
           for b in range(2)]
    es = ExitStack()
    pr = Prog(nc, es, SEM_NAMES)
    HD = 128 if layer == 0 else 64
    with es:
        for b in range(2):
            emit_attn(nc, pr, layer, "a%d_" % b,
                      lambda dc, tc, b=b: xT[b][dc * P:(dc + 1) * P, tc * QT:(tc + 1) * QT],
                      lambda h, qt, b=b: otm[b][h * HD:(h + 1) * HD, qt * QT:(qt + 1) * QT],
                      wq, wk, wv, theta=theta, gam=gam, bet=bet,
                      flash_qts=flash_qts)
        pr.finalize()
    return nc


def _mix_weight_aps(nc, layer):
    E = 6 if layer == 0 else 8
    NE = 7 if layer == 0 else 8
    W = {"wo": nc.dram_tensor("wo", [D, D], F32, kind="ExternalInput").ap(),
         "wob": nc.dram_tensor("wob", [D], F32, kind="ExternalInput"),
         "gw": nc.dram_tensor("gw", [D, E], F32, kind="ExternalInput").ap(),
         "gb": nc.dram_tensor("gb", [E], F32, kind="ExternalInput")}
    if layer == 0:
        W["w1t"] = nc.dram_tensor("w1t", [NE, D, D], F32, kind="ExternalInput").ap()
        W["w3t"] = nc.dram_tensor("w3t", [NE, D, D], F32, kind="ExternalInput").ap()
        W["w2t"] = nc.dram_tensor("w2t", [NE, D, D], F32, kind="ExternalInput").ap()
    else:
        W["gupt"] = nc.dram_tensor("gupt", [E, D, 4096], F32, kind="ExternalInput").ap()
        W["gupb"] = nc.dram_tensor("gupb", [E, 4096], F32, kind="ExternalInput").ap()
        W["dnt"] = nc.dram_tensor("dnt", [E, 2048, D], F32, kind="ExternalInput").ap()
        W["dnb"] = nc.dram_tensor("dnb", [E, D], F32, kind="ExternalInput").ap()
    return W


def build_mix(layer):
    nc = bass.Bass("TRN2", target_bir_lowering=False)
    oT = nc.dram_tensor("oT", [D, NT], BF16, kind="ExternalInput").ap()
    xT = nc.dram_tensor("xT", [D, NT], F32, kind="ExternalInput").ap()
    W = _mix_weight_aps(nc, layer)
    yT = nc.dram_tensor("yT", [D, NT], F32, kind="ExternalOutput").ap()
    es = ExitStack()
    pr = Prog(nc, es, SEM_NAMES)
    with es:
        emit_mix(nc, pr, layer, "m_",
                 lambda dc, tc: oT[dc * P:(dc + 1) * P, tc * QT:(tc + 1) * QT],
                 xT, yT, W)
        pr.finalize()
    return nc


def build_fused():
    nc = bass.Bass("TRN2", target_bir_lowering=False)
    GROUPS = [[0, 1, 2, 3, 4, 5, 6, 7]]
    xTs = nc.dram_tensor("xTs", [D, NT], F32, kind="ExternalInput").ap()
    a_in = {}
    for L in (0, 1):
        p = "A%d_" % L
        a_in[L] = {k: nc.dram_tensor(p + k, [D, 128], F32, kind="ExternalInput").ap()
                   for k in ("wq", "wk", "wv")}
    th = nc.dram_tensor("L0_theta", [8], F32, kind="ExternalInput")
    gam = nc.dram_tensor("L0_gam", [128], F32, kind="ExternalInput")
    bet = nc.dram_tensor("L0_bet", [128], F32, kind="ExternalInput")
    m_in = {}
    for L in (0, 1):
        E = 6 if L == 0 else 8
        NE = 7 if L == 0 else 8
        p = "M%d_" % L
        W = {"wo": nc.dram_tensor(p + "wo", [D, D], F32, kind="ExternalInput").ap(),
             "wob": nc.dram_tensor(p + "wob", [D], F32, kind="ExternalInput"),
             "gw": nc.dram_tensor(p + "gw", [D, E], F32, kind="ExternalInput").ap(),
             "gb": nc.dram_tensor(p + "gb", [E], F32, kind="ExternalInput")}
        if L == 0:
            for k in ("w1t", "w3t", "w2t"):
                W[k] = nc.dram_tensor(p + k, [NE, D, D], F32, kind="ExternalInput").ap()
        else:
            W["gupt"] = nc.dram_tensor(p + "gupt", [E, D, 4096], F32, kind="ExternalInput").ap()
            W["gupb"] = nc.dram_tensor(p + "gupb", [E, 4096], F32, kind="ExternalInput").ap()
            W["dnt"] = nc.dram_tensor(p + "dnt", [E, 2048, D], F32, kind="ExternalInput").ap()
            W["dnb"] = nc.dram_tensor(p + "dnb", [E, D], F32, kind="ExternalInput").ap()
        m_in[L] = W
    yout = nc.dram_tensor("yout", [D, NT], F32, kind="ExternalOutput").ap()
    agx_in = [nc.dram_tensor("agx_in%d" % L, [D, NT], BF16) for L in (0, 1)]
    agx1_in = [nc.dram_tensor("agx1_in%d" % i, [D, NT // 2], BF16) for i in range(2)]
    agx = [nc.dram_tensor("agx%d" % L, [8 * D, NT], BF16, addr_space="Shared")
           for L in (0, 1)]
    agx1h = [nc.dram_tensor("agx1h%d" % i, [8 * D, NT // 2], BF16, addr_space="Shared")
             for i in range(2)]
    ago_in = [nc.dram_tensor("ago_in%d" % L, [D, NT], BF16) for L in (0, 1)]
    ago = [nc.dram_tensor("ago%d" % L, [D, NT], BF16) for L in (0, 1)]
    x2s = nc.dram_tensor("x2s", [D, NT], F32)

    es = ExitStack()
    pr = Prog(nc, es, SEM_NAMES)

    def agx_chunk(L, b):
        if L == 1:
            def f1(dc, tc):
                r = b * 4 + tc // 4
                cc = tc % 4
                hh = agx1h[0] if cc < 2 else agx1h[1]
                return bass.AP(hh, (r * D + dc * P) * (NT // 2) + (cc % 2) * QT,
                               [[NT // 2, P], [1, QT]])
            return f1
        h = agx[L]

        def f(dc, tc):
            r = b * 4 + tc // 4
            return bass.AP(h, (r * D + dc * P) * NT + (tc % 4) * QT,
                           [[NT, P], [1, QT]])
        return f

    def a2a_odst(L, b, HD):
        h = ago_in[L]

        def f(hh, qt):
            row = (b * 4 + qt // 4) * 128 + hh * HD
            return bass.AP(h, row * NT + (qt % 4) * QT, [[NT, HD], [1, QT]])
        return f

    def ago_chunk(L):
        h = ago[L]

        def f(dc, tc):
            return bass.AP(h, dc * P * NT + tc * QT, [[NT, P], [1, QT]])
        return f

    def cast_and_ag(L, src_f32):
        if src_f32 is not None:
            pr.op("gpsimd", lambda g: g.dma_start(
                out=agx_in[L].ap(), in_=src_f32), "lw", 16)
            pr.w("gpsimd", "lw")
        else:
            pr.w("gpsimd", "st")
        pr.op("gpsimd", lambda g: g.collective_compute(
            "AllGather", mybir.AluOpType.bypass, replica_groups=GROUPS,
            ins=[agx_in[L].ap()], outs=[agx[L].ap()]), "cc")
        return ("cc", pr.cnt["cc"])

    def a2a_o(L):
        pr.w("gpsimd", "so0")
        pr.w("gpsimd", "so1")
        with allow_ici_all_to_all():
            pr.op("gpsimd", lambda g: g.collective_compute(
                "AllToAll", mybir.AluOpType.bypass, replica_groups=GROUPS,
                ins=[ago_in[L].ap()], outs=[ago[L].ap()]), "cc")
        pr.w("sync", "cc")
        pr.w("sync", "st")

    with es:
        g0 = cast_and_ag(0, xTs)
        sh0 = {}
        for b in range(2):
            emit_attn(nc, pr, 0, "a0%d_" % b, agx_chunk(0, b), a2a_odst(0, b, 128),
                      a_in[0]["wq"], a_in[0]["wk"], a_in[0]["wv"],
                      theta=th, gam=gam, bet=bet, shared=sh0,
                      xgate=(g0 if b == 0 else None))
        a2a_o(0)

        def ag1_hook():
            pr.w("gpsimd", "st")
            pr.op("gpsimd", lambda g: g.collective_compute(
                "AllGather", mybir.AluOpType.bypass, replica_groups=GROUPS,
                ins=[agx1_in[0].ap()], outs=[agx1h[0].ap()]), "cc")
        emit_mix(nc, pr, 0, "m0_", ago_chunk(0), xTs, x2s.ap(), m_in[0],
                 ybf=[agx1_in[0].ap(), agx1_in[1].ap()], ag_hook=ag1_hook)
        g1 = ("cc", pr.cnt["cc"])
        pr.w("gpsimd", "st")
        pr.op("gpsimd", lambda g: g.collective_compute(
            "AllGather", mybir.AluOpType.bypass, replica_groups=GROUPS,
            ins=[agx1_in[1].ap()], outs=[agx1h[1].ap()]), "cc")
        g1b = ("cc", pr.cnt["cc"])
        sh_es = ExitStack()
        sh1 = {"cs": sh_es.enter_context(nc.sbuf_tensor("sh_cs", [P, 2048], F32)),
               "sn": sh_es.enter_context(nc.sbuf_tensor("sh_sn", [P, 2048], F32))}
        with sh_es:
            for b in range(2):
                emit_attn(nc, pr, 1, "a1%d_" % b, agx_chunk(1, b), a2a_odst(1, b, 64),
                          a_in[1]["wq"], a_in[1]["wk"], a_in[1]["wv"], shared=sh1,
                          xgate=(g1 if b == 0 else None),
                          xgate2=(g1b if b == 0 else None))
        a2a_o(1)
        emit_mix(nc, pr, 1, "m1_", ago_chunk(1), x2s.ap(), yout, m_in[1])
        pr.finalize()
    return nc



_BF = ml_dtypes.bfloat16


def _cont(a):
    return np.ascontiguousarray(a)


def _mix_weights(z, layer):
    if layer == 0:
        w1 = [z["L0_ffn_ew1"][e].T for e in range(6)] + [z["L0_ffn_fc1"].T]
        w3 = [z["L0_ffn_ew3"][e].T for e in range(6)] + [z["L0_ffn_fc2"].T]
        w2 = [z["L0_ffn_ew2"][e].T for e in range(6)] + [z["L0_ffn_fc3"].T]
        return {"wo": _cont(z["L0_attn_Wo"]), "wob": _cont(z["L0_attn_bo"]),
                "gw": _cont(z["L0_ffn_gate_w"].T), "gb": _cont(z["L0_ffn_gate_b"]),
                "w1t": _cont(np.stack(w1)), "w3t": _cont(np.stack(w3)),
                "w2t": _cont(np.stack(w2))}
    return {"wo": _cont(z["L1_attn_wo_w"]), "wob": _cont(z["L1_attn_wo_b"]),
            "gw": _cont(z["L1_ffn_router_weight"].T), "gb": _cont(z["L1_ffn_router_bias"]),
            "gupt": _cont(np.transpose(z["L1_ffn_gate_up_proj"], (0, 2, 1))),
            "gupb": _cont(z["L1_ffn_gate_up_proj_bias"]),
            "dnt": _cont(np.transpose(z["L1_ffn_down_proj"], (0, 2, 1))),
            "dnb": _cont(z["L1_ffn_down_proj_bias"])}


_PROGS = {}


def _prog(kind):
    if kind not in _PROGS:
        _PROGS[kind] = build_fused()
    return _PROGS[kind]


def kernel(**inputs):
    z = {k: np.asarray(v) for k, v in inputs.items()}
    x = z["L0_attn_x"]
    xT = [_cont(x[b].T) for b in range(2)]
    mw = {L: _mix_weights(z, L) for L in (0, 1)}
    maps = []
    for c in range(8):
        b, ts = c // 4, c % 4
        cs = slice(c * 128, (c + 1) * 128)
        m = {"xTs": _cont(xT[b][:, ts * NT:(ts + 1) * NT]),
             "A0_wq": _cont(z["L0_attn_Wq"][:, cs]),
             "A0_wk": _cont(z["L0_attn_Wk"][:, cs]),
             "A0_wv": _cont(z["L0_attn_Wv"][:, cs]),
             "A1_wq": _cont(z["L1_attn_wq"][:, cs]),
             "A1_wk": _cont(z["L1_attn_wk"][:, cs]),
             "A1_wv": _cont(z["L1_attn_wv"][:, cs]),
             "L0_theta": _cont(z["L0_attn_theta"]),
             "L0_gam": _cont(z["L0_attn_gamma"][cs]),
             "L0_bet": _cont(z["L0_attn_beta"][cs])}
        for L in (0, 1):
            for k, v in mw[L].items():
                m["M%d_%s" % (L, k)] = v
        maps.append(m)
    res = run_bass_kernel_spmd(_prog("fused"), maps, core_ids=list(range(8)))
    out = np.empty((2, T, D), np.float32)
    for c in range(8):
        b, ts = c // 4, c % 4
        out[b, ts * NT:(ts + 1) * NT, :] = np.asarray(res.results[c]["yout"]).T
    return out


# version 25
# speedup vs baseline: 1.1165x; 1.1165x over previous
import math
from contextlib import ExitStack

import numpy as np
import ml_dtypes
import os
os.environ.setdefault("TRN_ALLOW_ICI_ALL_TO_ALL", "1")

import concourse.bass as bass
import concourse.mybir as mybir
from concourse.bass_utils import run_bass_kernel_spmd
from concourse.aot_env import allow_ici_all_to_all

F32 = mybir.dt.float32
BF16 = mybir.dt.bfloat16
AF = mybir.ActivationFunctionType
ALU = mybir.AluOpType
AX = mybir.AxisListType

P = 128
D = 1024
T = 8192
NT = 2048
QT = 512
NQT = T // QT

ENGS = ("sync", "gpsimd", "tensor", "vector", "scalar")


class Prog:

    def __init__(self, nc, es, sem_names):
        self.nc = nc
        self.ops = {e: [] for e in ENGS}
        self.sems = {n: es.enter_context(nc.semaphore(n)) for n in sem_names}
        self.cnt = {n: 0 for n in sem_names}

    def w(self, eng, name, val=None):
        v = self.cnt[name] if val is None else val
        if v <= 0:
            return
        s = self.sems[name]
        self.ops[eng].append(lambda e: e.wait_ge(s, v))

    def op(self, eng, fn, inc=None, k=1):
        if inc is None:
            self.ops[eng].append(fn)
        else:
            s = self.sems[inc]
            self.ops[eng].append(lambda e: fn(e).then_inc(s, k))
            self.cnt[inc] += k

    def block(self):
        cur = dict(self.cnt)
        for e in ENGS:
            for s, v in cur.items():
                if v:
                    self.w(e, s, v)

    def finalize(self):
        live = [e for e in ENGS if self.ops[e]]
        with self.nc.Block() as blk:
            for e in live:
                lst = self.ops[e]
                self.ops[e] = []

                def body(eng, _lst=lst):
                    for f in _lst:
                        f(eng)
                getattr(blk, e)(body)



def emit_attn(nc, pr, layer, pfx, xchunk, odst, wq, wk, wv,
              theta=None, gam=None, bet=None, flash_qts=None, shared=None,
              xgate=None, xgate2=None):
    HD = 128 if layer == 0 else 64
    H = 128 // HD
    TOTC = 128
    OFF = QT + 127
    GW = T + OFF + 1
    if layer == 0:
        GW2 = GW + 128
        g2 = (shared["g2"] if shared and "g2" in shared
              else nc.dram_tensor(pfx + "g2", [P, GW2], BF16))

    es = ExitStack()

    def sbuf(name, shape, dt=F32):
        return es.enter_context(nc.sbuf_tensor(pfx + name, shape, dt))

    ident = sbuf("ident", [P, P])
    identb = sbuf("identb", [P, P], BF16)
    ones_r = sbuf("ones_r", [1, P])
    ones_cb = sbuf("ones_cb", [P, 1], BF16)
    wq_sb = sbuf("wq_sb", [P, 8 * TOTC], BF16)
    wk_sb = sbuf("wk_sb", [P, 8 * TOTC], BF16)
    wv_sb = sbuf("wv_sb", [P, 8 * TOTC], BF16)
    qT_sb = sbuf("qT_sb", [P, T], BF16)
    kT_sb = sbuf("kT_sb", [P, T], BF16)
    v1 = sbuf("v1", [P, H * 64 * 65], BF16)
    v2 = sbuf("v2", [P, H * 64 * 64], BF16) if layer == 0 else None
    xbuf = [sbuf("xb%d" % i, [P, 8 * QT], BF16) for i in range(2)]
    vtmp = [sbuf("vtmp%d" % i, [P, QT], BF16) for i in range(2)]
    bias_sb = sbuf("bias_sb", [P, H])
    binm = sbuf("binm", [P, 4 * QT], BF16)
    stat = sbuf("stat", [1, 64])
    sq_scr = sbuf("sq_scr", [P, QT], BF16)
    pt_sb = [sbuf("pt%d" % i, [P, QT], BF16) for i in range(4)]
    oa_sbs = [sbuf("oa_sb%d" % i, [65, QT]) for i in range(H)]
    ob_sbs = [sbuf("ob_sb%d" % i, [64, QT]) for i in range(H)] if layer == 0 else None
    otile = [sbuf("otile%d" % i, [P, HD], BF16) for i in range(2)]
    ot_f = sbuf("ot_f", [P, 128])
    gn_scr = sbuf("gn_scr", [P, 128], BF16)
    linv = sbuf("linv", [P, 1])
    if layer == 0:
        dt_sb = [sbuf("dt%d" % i, [P, QT], BF16) for i in range(4)]
        sp_sb = [sbuf("sp%d" % i, [P, QT]) for i in range(4)]
        gB = sbuf("gB", [P, TOTC])
        bB = sbuf("bB", [P, TOTC])
        grow = sbuf("grow", [1, TOTC])
        brow = sbuf("brow", [1, TOTC])
        gnmu = sbuf("gnmu", [P, 1])
        gnsq = sbuf("gnsq", [P, 1])
        gnms = sbuf("gnms", [P, 1])
        gnvar = sbuf("gnvar", [P, 1])
        gnrs = sbuf("gnrs", [P, 1])
        rg_sb = sbuf("rg_sb", [P, 128])
        gn_f2 = sbuf("gn_f2", [P, 128])
        ones_c8 = sbuf("ones_c8", [8, 1])
    else:
        if shared is not None:
            cs_sb = shared["cs"]
            sn_sb = shared["sn"]
        else:
            cs_sb = sbuf("cs_sb", [P, 2048])
            sn_sb = sbuf("sn_sb", [P, 2048])
        rtm = sbuf("rtm", [P, 512])
        rtm2 = sbuf("rtm2", [P, 512])
        rsc = sbuf("rsc", [P, 256])
        rsc2 = sbuf("rsc2", [P, 256])
        halfpi = sbuf("halfpi", [P, 1])
        ev_sb = [sbuf("ev_sb%d" % i, [64, QT]) for i in range(4)]

    C = pr.cnt
    B0 = dict(C)

    with es:
        ph_es = ExitStack()
        ps_pr = ph_es.enter_context(nc.psum_tensor(pfx + "ps_pr", [P, QT], F32))
        ps_pr2 = ph_es.enter_context(nc.psum_tensor(pfx + "ps_pr2", [P, QT], F32))
        ps_pr3 = ph_es.enter_context(nc.psum_tensor(pfx + "ps_pr3", [P, QT], F32))
        ps_pr4 = ph_es.enter_context(nc.psum_tensor(pfx + "ps_pr4", [P, QT], F32))
        ps_pr5 = ph_es.enter_context(nc.psum_tensor(pfx + "ps_pr5", [P, QT], F32))
        ps_pr6 = ph_es.enter_context(nc.psum_tensor(pfx + "ps_pr6", [P, QT], F32))
        ps_vtb = ph_es.enter_context(nc.psum_tensor(pfx + "ps_vtb", [P, QT], BF16))

        def g_memsets(g):
            g.memset(ident[:, :], 0.0)
            g.memset(identb[:, :], 0.0)
            g.memset(ones_r[:, :], 1.0)
            g.memset(ones_cb[:, :], 1.0)
            if layer == 0:
                g.memset(ones_c8[:, :], 1.0)
            for h in range(H):
                g.memset(v1[:, h * 4160 + 64:(h + 1) * 4160:65], 1.0)
            for i in range(3):
                g.memset(binm[:, i * QT:(i + 1) * QT], 1.0)
            return g.memset(binm[:, 3 * QT:4 * QT], 1.0)
        pr.op("gpsimd", g_memsets, "gx")
        pr.w("gpsimd", "gx")

        def g_selects(g):
            g.affine_select(out=ident[:, :], in_=ident[:, :], compare_op=ALU.not_equal,
                            fill=1.0, base=0, pattern=[[-1, P]], channel_multiplier=1)
            g.affine_select(out=identb[:, :], in_=identb[:, :], compare_op=ALU.not_equal,
                            fill=1.0, base=0, pattern=[[-1, P]], channel_multiplier=1)
            r = None
            for i in range(4):
                m = binm[:, i * QT:(i + 1) * QT]
                r = g.affine_select(out=m, in_=m, compare_op=ALU.is_ge, fill=0.0,
                                    base=-128 * i, pattern=[[1, QT]], channel_multiplier=-1)
            return r
        pr.op("gpsimd", g_selects, "gx")
        pr.w("gpsimd", "gx")
        for dc in range(8):
            for w_ap, w_sb in ((wq, wq_sb), (wk, wk_sb), (wv, wv_sb)):
                pr.op("gpsimd", lambda g, a=w_ap, s=w_sb, d=dc: g.dma_start(
                    out=s[:, d * TOTC:(d + 1) * TOTC], in_=a[d * P:(d + 1) * P, :]), "lw", 16)
        pr.block()

        skip_tables = bool(shared) and shared.get("built", False)
        if layer == 0 and skip_tables:
            pr.op("gpsimd", lambda g: g.dma_start(
                out=grow[:, :], in_=bass.AP(gam, 0, [[1, 1], [1, TOTC]])), "ld", 16)
            pr.op("gpsimd", lambda g: g.dma_start(
                out=brow[:, :], in_=bass.AP(bet, 0, [[1, 1], [1, TOTC]])), "ld", 16)
            pr.w("tensor", "ld")
            pr.w("tensor", "ac")
            pr.op("tensor", lambda t: t.matmul(ps_pr[:, 0:TOTC], ones_r[:, :], grow[:, :],
                                               start=True, stop=True), "mm")
            pr.op("tensor", lambda t: t.matmul(ps_pr2[:, 0:TOTC], ones_r[:, :], brow[:, :],
                                               start=True, stop=True), "mm")
            pr.w("vector", "mm")
            pr.op("vector", lambda v: v.tensor_copy(gB[:, :], ps_pr[:, 0:TOTC]))
            pr.op("vector", lambda v: v.tensor_copy(bB[:, :], ps_pr2[:, 0:TOTC]), "ev")
            pr.block()
        elif layer == 1 and skip_tables:
            pass
        elif layer == 0:
            t0 = ExitStack()
            th_sb = t0.enter_context(nc.sbuf_tensor(pfx + "th_sb", [8, 1], F32))
            nv8 = t0.enter_context(nc.sbuf_tensor(pfx + "nv8", [8, QT], F32))
            nv8b = t0.enter_context(nc.sbuf_tensor(pfx + "nv8b", [8, QT], F32))
            ind8 = t0.enter_context(nc.sbuf_tensor(pfx + "ind8", [8, QT], F32))
            glc = [t0.enter_context(nc.sbuf_tensor(pfx + "glc%d" % i, [1, QT], BF16)) for i in range(2)]
            gl_lin = nc.dram_tensor(pfx + "gl_lin", [GW], BF16)
            with t0:
                pr.op("gpsimd", lambda g: g.dma_start(
                    out=th_sb[:, :], in_=bass.AP(theta, 0, [[1, 8], [1, 1]])), "ld", 16)
                pr.op("gpsimd", lambda g: g.dma_start(
                    out=grow[:, :], in_=bass.AP(gam, 0, [[1, 1], [1, TOTC]])), "ld", 16)
                pr.op("gpsimd", lambda g: g.dma_start(
                    out=brow[:, :], in_=bass.AP(bet, 0, [[1, 1], [1, TOTC]])), "ld", 16)
                pr.block()
                pr.w("vector", "ld")
                nch = (GW + QT - 1) // QT
                for c in range(nch):
                    w = min(QT, GW - c * QT)
                    pr.w("gpsimd", "mm")
                    pr.op("gpsimd", lambda g, cc=c, ww=w: g.iota(
                        nv8[:, 0:ww], [[1, ww]], base=-OFF + cc * QT, channel_multiplier=0,
                        allow_small_or_imprecise_dtypes=True), "gx")
                    pr.w("vector", "gx")
                    pr.op("vector", lambda v, ww=w: v.tensor_scalar(
                        ind8[:, 0:ww], nv8[:, 0:ww], 0.0, None, op0=ALU.is_ge))
                    pr.op("vector", lambda v, ww=w: v.tensor_scalar(
                        nv8b[:, 0:ww], nv8[:, 0:ww], 0.0, th_sb[:, :],
                        op0=ALU.max, op1=ALU.mult), "ev")
                    pr.w("scalar", "ev")
                    pr.op("scalar", lambda s, ww=w: s.activation(
                        nv8b[:, 0:ww], nv8b[:, 0:ww], AF.Exp, scale=-1.0), "ac")
                    pr.w("vector", "ac")
                    pr.op("vector", lambda v, ww=w: v.tensor_tensor(
                        nv8b[:, 0:ww], nv8b[:, 0:ww], ind8[:, 0:ww], op=ALU.mult), "ev")
                    pr.w("tensor", "ev")
                    pr.w("tensor", "ac")
                    pr.op("tensor", lambda t, ww=w: t.matmul(
                        ps_pr[0:1, 0:ww], ones_c8[:, :], nv8b[:, 0:ww],
                        start=True, stop=True), "mm")
                    pr.w("scalar", "mm")
                    pr.w("scalar", "glw", B0["glw"] + c * 16)
                    pr.op("scalar", lambda s, cc=c, ww=w: s.activation(
                        glc[cc % 2][:, 0:ww], ps_pr[0:1, 0:ww], AF.Copy,
                        scale=1.0 / (8.0 * math.sqrt(HD))), "ac")
                    pr.w("sync", "ac")
                    pr.op("sync", lambda sy, cc=c, ww=w: sy.dma_start(
                        out=bass.AP(gl_lin, cc * QT, [[1, 1], [1, ww]]),
                        in_=glc[cc % 2][:, 0:ww]), "glw", 16)
                pr.w("gpsimd", "glw")
                for p in range(P):
                    pr.op("gpsimd", lambda g, pp=p: g.dma_start(
                        out=bass.AP(g2, pp * GW2 + pp, [[1, 1], [1, GW]]),
                        in_=bass.AP(gl_lin, 0, [[1, 1], [1, GW]])), "g2s", 16)
                pr.w("gpsimd", "g2s")
                pr.op("gpsimd", lambda g: g.nop())
                if shared is not None:
                    shared["g2"] = g2
                    shared["built"] = True
                pr.w("tensor", "ld")
                pr.w("tensor", "ac")
                pr.op("tensor", lambda t: t.matmul(ps_pr[:, 0:TOTC], ones_r[:, :], grow[:, :],
                                                   start=True, stop=True), "mm")
                pr.op("tensor", lambda t: t.matmul(ps_pr2[:, 0:TOTC], ones_r[:, :], brow[:, :],
                                                   start=True, stop=True), "mm")
                pr.w("vector", "mm")
                pr.op("vector", lambda v: v.tensor_copy(gB[:, :], ps_pr[:, 0:TOTC]))
                pr.op("vector", lambda v: v.tensor_copy(bB[:, :], ps_pr2[:, 0:TOTC]), "ev")
                pr.block()
        else:
            t0 = ExitStack()
            inv_sb = t0.enter_context(nc.sbuf_tensor(pfx + "inv_sb", [1, 32], F32))
            ang = t0.enter_context(nc.sbuf_tensor(pfx + "ang", [P, 2048], F32))
            angk = t0.enter_context(nc.sbuf_tensor(pfx + "angk", [P, 2048], F32))
            with t0:
                pr.op("gpsimd", lambda g: g.memset(halfpi[:, :], math.pi / 2))
                pr.op("gpsimd", lambda g: g.iota(inv_sb[:, :], [[1, 32]], base=0,
                                                 channel_multiplier=0,
                                                 allow_small_or_imprecise_dtypes=True))
                pr.op("gpsimd", lambda g: g.iota(ang[:, :], [[128, 64], [0, 32]], base=0,
                                                 channel_multiplier=1,
                                                 allow_small_or_imprecise_dtypes=True))
                pr.block()
                pr.op("scalar", lambda s: s.activation(inv_sb[:, :], inv_sb[:, :], AF.Exp,
                                                       scale=-math.log(10000.0) / 32.0))
                pr.block()
                pr.op("tensor", lambda t: t.matmul(ps_pr[:, 0:32], ones_r[:, :], inv_sb[:, :],
                                                   start=True, stop=True), "mm")
                pr.w("vector", "mm")
                for kc in range(64):
                    pr.op("vector", lambda v, k=kc: v.tensor_copy(
                        angk[:, k * 32:(k + 1) * 32], ps_pr[:, 0:32]))
                pr.op("vector", lambda v: v.nop(), "ev")
                pr.block()
                CBIG = 12582912.0
                pr.op("vector", lambda v: v.tensor_tensor(ang[:, :], ang[:, :], angk[:, :],
                                                          op=ALU.mult), "ev")
                pr.w("vector", "ev")
                pr.op("vector", lambda v: v.tensor_scalar(angk[:, :], ang[:, :],
                                                          1.0 / (2 * math.pi), CBIG,
                                                          op0=ALU.mult, op1=ALU.add), "ev")
                pr.w("vector", "ev")
                pr.op("vector", lambda v: v.tensor_scalar(angk[:, :], angk[:, :], CBIG, None,
                                                          op0=ALU.subtract), "ev")
                pr.w("vector", "ev")
                pr.op("vector", lambda v: v.tensor_scalar(angk[:, :], angk[:, :],
                                                          -2 * math.pi, None, op0=ALU.mult), "ev")
                pr.w("vector", "ev")
                pr.op("vector", lambda v: v.tensor_tensor(ang[:, :], ang[:, :], angk[:, :],
                                                          op=ALU.add))
                pr.block()
                pr.op("scalar", lambda s: s.activation(sn_sb[:, :], ang[:, :], AF.Sin))
                pr.op("scalar", lambda s: s.activation(angk[:, :], ang[:, :], AF.Abs))
                pr.op("scalar", lambda s: s.activation(cs_sb[:, :], angk[:, :], AF.Sin,
                                                       scale=-1.0, bias=halfpi[:, :]))
                pr.block()
                if shared is not None:
                    shared["built"] = True

        mm_b = C["mm"]
        ev_done = {}
        evac1_of = {}
        mm_done = {}
        pr.w("gpsimd", "lw")
        if xgate is not None:
            pr.w("gpsimd", xgate[0], xgate[1])
        part2 = []
        rci = [0]
        cp_of = {}
        rv_of = {}
        bw_of = {}
        st_of = {}
        vev_of = {}
        vci = [0]
        fw_of = {}

        def flush_part2(n):
            while len(part2) > n:
                part2.pop(0)()
        order = list(range(NQT))
        if xgate2 is not None:
            order = ([t for t in range(NQT) if t % 4 < 2]
                     + [t for t in range(NQT) if t % 4 >= 2])
        for idx, tc in enumerate(order):
            if xgate2 is not None and idx == NQT // 2:
                pr.w("gpsimd", xgate2[0], xgate2[1])
            if idx >= 2:
                pr.w("gpsimd", "mm", mm_done[(idx - 2, "xfree")])
            lsem = "ld%d" % (idx % 2)
            for dc in range(8):
                pr.op("gpsimd", lambda g, t=tc, d=dc, ix=idx: g.dma_start(
                    out=xbuf[ix % 2][:, d * QT:(d + 1) * QT], in_=xchunk(d, t)), lsem, 16)
            for h in range(H):
                for mi in range(3):
                    grp = idx * H * 3 + h * 3 + mi
                    bank = (ps_pr, ps_pr4)[grp % 2]
                    pr.w("tensor", "ld%d" % (idx % 2),
                         B0["ld%d" % (idx % 2)] + (idx // 2 + 1) * 128)
                    if grp >= 2:
                        pr.w("tensor", "ev", evac1_of.get(grp - 2, 0))
                    w_sb = (wq_sb, wk_sb, wv_sb)[mi]

                    def projmm(t, h=h, mi=mi, ix=idx, w_sb=w_sb, bank=bank):
                        r = None
                        for dc in range(8):
                            r = t.matmul(bank[0:HD, :],
                                         w_sb[:, dc * TOTC + h * HD: dc * TOTC + (h + 1) * HD],
                                         xbuf[ix % 2][:, dc * QT:(dc + 1) * QT],
                                         start=(dc == 0), stop=(dc == 7))
                        return r
                    pr.op("tensor", projmm, "mm")
                    mmv = C["mm"]
                    col0 = (h * T if layer == 0 else (h // 2) * T) + tc * QT
                    pb = 0 if layer == 0 else (h % 2) * 64
                    if mi < 2 and layer == 0:
                        dst = (qT_sb, kT_sb)[mi]
                        pr.w("vector", "mm", mmv)
                        pr.op("vector", lambda v, d=dst, c=col0, bank=bank: v.tensor_copy(
                            d[0:HD, c:c + QT], bank[0:HD, :]), "ev")
                        ev_done[grp] = C["ev"]
                        evac1_of[grp] = C["ev"]
                    elif mi == 2:
                        pr.w("vector", "mm", mmv)
                        pr.op("vector", lambda v, g2_=grp, bank=bank: v.tensor_copy(
                            vtmp[g2_ % 2][0:HD, :], bank[0:HD, :]), "ev")
                        evv = C["ev"]
                        evac1_of[grp] = evv

                        vc = vci[0]
                        vci[0] += 1

                        def d_vchain(grp=grp, tc=tc, h=h, evv=evv, vc=vc):
                            pr.w("tensor", "ev", evv)
                            if vc >= 1:
                                pr.w("tensor", "ev", vev_of.get(vc - 1, 0))

                            def vtr(t, grp=grp):
                                r = None
                                for stt in range(4):
                                    if layer == 0:
                                        r = t.transpose(ps_vtb[:, stt * 128:(stt + 1) * 128],
                                                        vtmp[grp % 2][:, stt * 128:(stt + 1) * 128],
                                                        identb[:, :])
                                    else:
                                        r = t.transpose(ps_vtb[:, stt * 64:(stt + 1) * 64],
                                                        vtmp[grp % 2][0:64, stt * 128:(stt + 1) * 128],
                                                        identb[0:64, 0:64])
                                return r
                            pr.op("tensor", vtr, "mm")
                            pr.w("vector", "mm", C["mm"])

                            def vev(v, tc=tc, h=h):
                                r = None
                                for stt in range(4):
                                    kc = tc * 4 + stt
                                    if layer == 0:
                                        v.tensor_copy(v1[:, h * 4160 + kc * 65: h * 4160 + kc * 65 + 64],
                                                      ps_vtb[:, stt * 128:stt * 128 + 64])
                                        r = v.tensor_copy(v2[:, h * 4096 + kc * 64: h * 4096 + (kc + 1) * 64],
                                                          ps_vtb[:, stt * 128 + 64:(stt + 1) * 128])
                                    else:
                                        r = v.tensor_copy(v1[:, h * 4160 + kc * 65: h * 4160 + kc * 65 + 64],
                                                          ps_vtb[:, stt * 64:(stt + 1) * 64])
                                return r
                            pr.op("vector", vev, "ev")
                            ev_done[grp] = C["ev"]
                            vev_of[vc] = C["ev"]
                        part2.append(d_vchain)
                        flush_part2(2)
                    else:
                        pr.w("vector", "mm", mmv)
                        rc = rci[0]
                        rci[0] += 1
                        if rc >= 4:
                            pr.w("vector", "mm", fw_of.get(rc - 4, 0))
                        pr.op("vector", lambda v, rc=rc, bank=bank: v.tensor_copy(
                            ev_sb[rc % 4][:, :], bank[0:64, :]), "ev")
                        evac1_of[grp] = C["ev"]
                        evv = C["ev"]

                        def d_rchain(grp=grp, tc=tc, h=h, mi=mi, pb=pb, col0=col0, evv=evv,
                                     rc=rc):
                            rp = rc % 2
                            pfw = (ps_pr2, ps_pr5)[rp]
                            pbw = (ps_pr3, ps_pr6)[rp]
                            ro = rp * 256
                            rh = rp * 128
                            pr.w("tensor", "ev", evv)

                            if rc >= 2:
                                pr.w("tensor", "ac", cp_of.get(rc - 2, 0))

                            def ropefwd(t, rc=rc, pfw=pfw):
                                r = None
                                for stt in range(4):
                                    r = t.transpose(pfw[:, stt * 64:(stt + 1) * 64],
                                                    ev_sb[rc % 4][:, stt * 128:(stt + 1) * 128],
                                                    ident[0:64, 0:64])
                                return r
                            pr.op("tensor", ropefwd, "mm")
                            fw_of[rc] = C["mm"]
                            pr.w("scalar", "mm", C["mm"])
                            if rc >= 2:
                                pr.w("scalar", "ev", rv_of.get(rc - 2, 0))
                            pr.op("scalar", lambda s, pfw=pfw, ro=ro: s.copy(
                                rtm[:, ro:ro + 256], pfw[:, 0:256]), "ac")
                            cp_of[rc] = C["ac"]
                            pr.w("vector", "ac")
                            if rc >= 2:
                                pr.w("vector", "mm", bw_of.get(rc - 2, 0))

                            def ropev(v, tc=tc, ro=ro, rh=rh):
                                ecs = cs_sb[:, tc * 128:(tc + 1) * 128]
                                esn = sn_sb[:, tc * 128:(tc + 1) * 128]
                                ev_ = rtm[:, ro:ro + 256:2]
                                od_ = rtm[:, ro + 1:ro + 256:2]
                                v.tensor_tensor(rtm2[:, ro:ro + 256:2], ev_, ecs, op=ALU.mult)
                                v.tensor_tensor(rsc[:, rh:rh + 128], od_, esn, op=ALU.mult)
                                v.tensor_tensor(rtm2[:, ro + 1:ro + 256:2], ev_, esn, op=ALU.mult)
                                v.tensor_tensor(rsc2[:, rh:rh + 128], od_, ecs, op=ALU.mult)
                                v.tensor_tensor(rtm2[:, ro:ro + 256:2], rtm2[:, ro:ro + 256:2],
                                                rsc[:, rh:rh + 128], op=ALU.subtract)
                                return v.tensor_tensor(rtm2[:, ro + 1:ro + 256:2],
                                                       rtm2[:, ro + 1:ro + 256:2],
                                                       rsc2[:, rh:rh + 128], op=ALU.add)
                            pr.op("vector", ropev, "ev")
                            rv_of[rc] = C["ev"]
                            rv = C["ev"]

                            def d_rtail(grp=grp, mi=mi, pb=pb, col0=col0, pbw=pbw, ro=ro,
                                        rv=rv, rc=rc):
                                pr.w("tensor", "ev", rv)
                                if rc >= 2:
                                    pr.w("tensor", "ev", st_of.get(rc - 2, 0))

                                def ropebwd(t, pbw=pbw, ro=ro):
                                    r = None
                                    for stt in range(4):
                                        r = t.transpose(pbw[0:64, stt * 128:(stt + 1) * 128],
                                                        rtm2[:, ro + stt * 64:ro + (stt + 1) * 64],
                                                        ident[:, :])
                                    return r
                                pr.op("tensor", ropebwd, "mm")
                                bw_of[rc] = C["mm"]
                                pr.w("vector", "mm", C["mm"])
                                dst = (qT_sb, kT_sb)[mi]
                                pr.op("vector", lambda v, dst=dst, pb=pb, col0=col0, pbw=pbw:
                                      v.tensor_copy(dst[pb:pb + 64, col0:col0 + QT],
                                                    pbw[0:64, 0:QT]), "ev")
                                ev_done[grp] = C["ev"]
                                st_of[rc] = C["ev"]
                            part2.append(d_rtail)
                        part2.append(d_rchain)
                        flush_part2(2)
            mm_done[(idx, "xfree")] = C["mm"]
        flush_part2(0)
        pr.block()

        for h in range(H):
            for qk in range(2):
                src = (qT_sb, kT_sb)[qk]
                col0 = h * T if layer == 0 else (h // 2) * T
                pb = 0 if layer == 0 else (h % 2) * 64
                for c in range(NQT):
                    pr.w("scalar", "mm")
                    pr.op("scalar", lambda s, src=src, pb=pb, col0=col0, c=c: s.activation(
                        sq_scr[0:HD, :], src[pb:pb + HD, col0 + c * QT: col0 + (c + 1) * QT],
                        AF.Square), "ac")
                    pr.w("tensor", "ac")
                    pr.w("tensor", "ev")
                    pr.op("tensor", lambda t: t.matmul(ps_pr[0:1, :], ones_cb[0:HD, :],
                                                       sq_scr[0:HD, :], start=True, stop=True),
                          "mm")
                    pr.w("vector", "mm")
                    pr.op("vector", lambda v, c=c: v.tensor_reduce(
                        stat[:, c:c + 1], ps_pr[0:1, :], axis=AX.X, op=ALU.max), "ev")
                pr.w("vector", "ev")
                pr.op("vector", lambda v, h=h, qk=qk: v.tensor_reduce(
                    stat[:, 32 + h * 2 + qk: 33 + h * 2 + qk], stat[:, 0:NQT],
                    axis=AX.X, op=ALU.max), "ev")
        pr.block()
        sc = 1.0 / math.sqrt(HD) if layer == 0 else 1.0 / 8.0
        pr.w("vector", "ev")
        for h in range(H):
            pr.op("vector", lambda v, h=h: v.tensor_tensor(
                stat[:, 40 + h:41 + h], stat[:, 32 + 2 * h:33 + 2 * h],
                stat[:, 33 + 2 * h:34 + 2 * h], op=ALU.mult), "ev")
        pr.w("scalar", "ev")
        pr.op("scalar", lambda s: s.activation(stat[:, 48:48 + H], stat[:, 40:40 + H],
                                               AF.Ln), "ac")
        pr.w("scalar", "ac")
        pr.op("scalar", lambda s: s.activation(stat[:, 48:48 + H], stat[:, 48:48 + H],
                                               AF.Exp, scale=0.5), "ac")
        pr.w("vector", "ac")
        pr.op("vector", lambda v: v.tensor_scalar(stat[:, 48:48 + H], stat[:, 48:48 + H],
                                                  -sc, None, op0=ALU.mult), "ev")
        pr.w("tensor", "ev")
        for h in range(H):
            pr.op("tensor", lambda t, h=h: t.matmul(ps_pr[:, h:h + 1], ones_r[:, :],
                                                    stat[:, 48 + h:49 + h],
                                                    start=True, stop=True), "mm")
        pr.w("vector", "mm")
        pr.op("vector", lambda v: v.tensor_copy(bias_sb[:, :], ps_pr[:, 0:H]), "ev")
        pr.block()
        ph_es.close()

        fl = ExitStack()
        ps_s = [fl.enter_context(nc.psum_tensor(pfx + "ps_s%d" % i, [P, QT], F32)) for i in range(4)]
        ps_oa = [fl.enter_context(nc.psum_tensor(pfx + "ps_oa%d" % i, [65, QT], F32))
                 for i in range(H)]
        ps_obm = (fl.enter_context(nc.psum_tensor(pfx + "ps_obm", [P, QT], F32))
                  if layer == 0 else None)
        ps_ot = fl.enter_context(nc.psum_tensor(pfx + "ps_ot", [P, QT], BF16))
        ps_ep = fl.enter_context(nc.psum_tensor(pfx + "ps_ep", [P, QT], F32))
        stg = [es.enter_context(nc.sbuf_tensor(pfx + "stg%d" % i, [P, QT], BF16))
               for i in range(2)]
        with fl:
            si = 0
            cci = 0
            mkc = 0
            dv_of = {}
            px_of = {}
            ot_store = {0: C["mm"], 1: C["mm"]}
            stg_store = {0: C["so0"], 1: C["so1"]}
            stg_evl = [0]
            scnt = [0]
            oaev = {}
            obac = {}
            tr1_of = {}
            epf_free = [0]
            prev_unit = []
            ep_units = []
            pend = []

            def emit_tr2_prev(keep=1):
                while len(prev_unit) > keep:
                    emit_tr2_one()

            def emit_tr2_one():
                uqt, h, stt, fin = prev_unit.pop(0)
                pr.w("tensor", "ev", fin)
                if stt == 0:
                    pr.w("tensor", "ev", stg_evl[0])
                pr.op("tensor", lambda t, stt=stt: t.transpose(
                    ps_ot[0:HD, stt * 128:(stt + 1) * 128],
                    otile[stt % 2][:, :], identb[:, :]), "mm")
                ot_store[stt % 2] = C["mm"]
                if stt == 3:
                    scnt[0] += 1
                    par = scnt[0] % 2
                    ssem = "so%d" % par
                    pr.w("vector", "mm", C["mm"])
                    pr.w("vector", ssem, stg_store[par])
                    pr.op("vector", lambda v, par=par: v.tensor_copy(
                        stg[par][0:HD, :], ps_ot[0:HD, :]), "ev")
                    stg_evl[0] = C["ev"]
                    pr.w("sync", "ev", stg_evl[0])
                    pr.op("sync", lambda sy, uqt=uqt, h=h, par=par: sy.dma_start(
                        out=odst(h, uqt), in_=stg[par][0:HD, :]), ssem, 16)
                    stg_store[par] = C[ssem]

            def emit_unit(uqt, h, stt):
                emit_tr2_prev()
                pr.w("tensor", "ev", oaev[h])
                if layer == 0:
                    pr.w("tensor", "ac", obac[h])
                pr.w("tensor", "ev", epf_free[0])

                def tr1(t, h=h, stt=stt):
                    r = t.transpose(ps_ep[:, 0:65],
                                    oa_sbs[h][:, stt * 128:(stt + 1) * 128],
                                    ident[0:65, 0:65])
                    if layer == 0:
                        r = t.transpose(ps_ep[:, 65:129],
                                        ob_sbs[h][:, stt * 128:(stt + 1) * 128],
                                        ident[0:64, 0:64])
                    return r
                pr.op("tensor", tr1, "mm")
                tr1_of[h] = C["mm"]
                pr.w("vector", "mm", C["mm"])
                pr.op("vector", lambda v: v.reciprocal(linv[:, :], ps_ep[:, 64:65]), "ev")
                pr.w("vector", "ev")
                pr.w("vector", "mm", ot_store[stt % 2])
                if layer == 0:
                    pr.op("vector", lambda v: v.tensor_scalar(
                        ot_f[:, 0:64], ps_ep[:, 0:64],
                        linv[:, :], None, op0=ALU.mult))
                    pr.op("vector", lambda v: v.tensor_scalar(
                        ot_f[:, 64:128], ps_ep[:, 65:129],
                        linv[:, :], None, op0=ALU.mult), "ev")
                    epf_free[0] = C["ev"]
                    pr.w("vector", "ev")
                    pr.op("vector", lambda v: v.tensor_reduce(
                        gnmu[:, :], ot_f[:, :], axis=AX.X, op=ALU.add), "ev")
                    pr.w("scalar", "ev")
                    pr.op("scalar", lambda s: s.activation(
                        gn_scr[:, :], ot_f[:, :], AF.Square, accum_out=gnsq[:, :]), "ac")
                    pr.w("vector", "ac")
                    pr.op("vector", lambda v: v.tensor_scalar(
                        gnms[:, :], gnmu[:, :], 1.0 / 128.0, None, op0=ALU.mult))
                    pr.op("vector", lambda v: v.tensor_scalar(
                        gnsq[:, :], gnsq[:, :], 1.0 / 128.0, 1e-5,
                        op0=ALU.mult, op1=ALU.add), "ev")
                    pr.w("vector", "ev")
                    pr.op("vector", lambda v: v.tensor_tensor(
                        gnvar[:, :], gnms[:, :], gnms[:, :], op=ALU.mult), "ev")
                    pr.w("vector", "ev")
                    pr.op("vector", lambda v: v.tensor_tensor(
                        gnvar[:, :], gnsq[:, :], gnvar[:, :], op=ALU.subtract), "ev")
                    pr.w("scalar", "ev")
                    pr.op("scalar", lambda s: s.activation(gnrs[:, :], gnvar[:, :],
                                                           AF.Ln), "ac")
                    pr.w("scalar", "ac")
                    pr.op("scalar", lambda s: s.activation(gnrs[:, :], gnrs[:, :],
                                                           AF.Exp, scale=-0.5), "ac")
                    pr.w("vector", "ac")
                    pr.op("vector", lambda v, h=h: v.tensor_scalar(
                        rg_sb[:, :], gB[:, h * 128:(h + 1) * 128], gnrs[:, :],
                        None, op0=ALU.mult))
                    pr.op("vector", lambda v: v.tensor_scalar(
                        gn_f2[:, :], ot_f[:, :], gnms[:, :], None,
                        op0=ALU.subtract), "ev")
                    pr.w("vector", "ev")
                    pr.op("vector", lambda v: v.tensor_tensor(
                        ot_f[:, :], gn_f2[:, :], rg_sb[:, :], op=ALU.mult), "ev")
                    pr.w("vector", "ev")
                    pr.op("vector", lambda v, h=h, stt=stt: v.tensor_tensor(
                        otile[stt % 2][:, :], ot_f[:, :], bB[:, h * 128:(h + 1) * 128],
                        op=ALU.add), "ev")
                else:
                    pr.op("vector", lambda v, stt=stt: v.tensor_scalar(
                        otile[stt % 2][:, :], ps_ep[:, 0:64],
                        linv[:, :], None, op0=ALU.mult), "ev")
                    epf_free[0] = C["ev"]
                prev_unit.append((uqt, h, stt, C["ev"]))

            def emit_av(args):
                qt, kc, h, s_i, dmask, first, last = args
                pr.w("tensor", "px", B0["px"] + s_i + 1)
                if dmask is not None:
                    pr.w("tensor", "mk", dmask)
                if first:
                    pr.w("tensor", "ev", oaev.get(h, 0))
                    if layer == 0:
                        pr.w("tensor", "ac", obac.get(H - 1, 0))

                def av(t, qt=qt, kc=kc, h=h, s_i=s_i, first=first, last=last):
                    r = t.matmul(ps_oa[h][:, :], v1[:, h * 4160 + kc * 65: h * 4160 + (kc + 1) * 65],
                                 pt_sb[s_i % 4][:, :], start=first, stop=last)
                    if layer == 0:
                        r = t.matmul(ps_obm[h * 64:(h + 1) * 64, :],
                                     v2[:, h * 4096 + kc * 64: h * 4096 + (kc + 1) * 64],
                                     pt_sb[s_i % 4][:, :], start=first, stop=last)
                    return r
                pr.op("tensor", av, "av")

            qt_iter = list(range(NQT)) if flash_qts is None else list(flash_qts)
            for qt in qt_iter:
                KC = 4 * qt + 4
                for kc in range(KC):
                    if layer == 0:
                        if cci >= 4:
                            pr.w("sync", "dv", B0["dv"] + (cci - 3) * H)
                        delta = qt * QT - kc * 128
                        pr.op("sync", lambda sy, cci=cci, delta=delta: sy.dma_start(
                            out=dt_sb[cci % 4][:, :],
                            in_=bass.AP(g2, OFF + delta, [[GW2, P], [1, QT]])),
                            "dts%d" % (cci % 4), 16)
                    for h in range(H):
                        first = (kc == 0)
                        last = (kc == KC - 1)
                        diag = (kc * 128 + 127 > qt * QT)
                        if si >= 4:
                            if layer == 0:
                                pr.w("tensor", "dv", dv_of[si - 4])
                            else:
                                pr.w("tensor", "px", px_of[si - 4])

                        def smm(t, qt=qt, kc=kc, h=h, s_i=si):
                            if layer == 0:
                                return t.matmul(ps_s[s_i % 4][:, :],
                                                kT_sb[:, h * T + kc * 128: h * T + (kc + 1) * 128],
                                                qT_sb[:, h * T + qt * QT: h * T + (qt + 1) * QT],
                                                start=True, stop=True)
                            pb = (h % 2) * 64
                            cb = (h // 2) * T
                            return t.matmul(ps_s[s_i % 4][:, :],
                                            kT_sb[pb:pb + 64, cb + kc * 128: cb + (kc + 1) * 128],
                                            qT_sb[pb:pb + 64, cb + qt * QT: cb + (qt + 1) * QT],
                                            start=True, stop=True)
                        pr.op("tensor", smm, "sm")
                        if layer == 0:
                            pr.w("vector", "sm")
                            pr.w("vector", "dts%d" % (cci % 4),
                                 B0["dts%d" % (cci % 4)] + 16 * (cci // 4 + 1))
                            if si >= 4:
                                pr.w("vector", "px", px_of[si - 4])

                            def dmul(v, s_i=si, cci=cci):
                                return v.tensor_tensor(sp_sb[s_i % 4][:, :],
                                                       ps_s[s_i % 4][:, :],
                                                       dt_sb[cci % 4][:, :], op=ALU.mult)
                            pr.op("vector", dmul, "dv")
                            dv_of[si] = C["dv"]
                        if layer == 0:
                            pr.w("scalar", "dv", dv_of[si])
                        else:
                            pr.w("scalar", "sm")
                        if si >= 4:
                            pr.w("scalar", "av", B0["av"] + si - 3)

                        def expo(s, s_i=si, h=h):
                            src = sp_sb[s_i % 4][:, :] if layer == 0 else ps_s[s_i % 4][:, :]
                            scl = 1.0 if layer == 0 else 0.125
                            return s.activation(pt_sb[s_i % 4][:, :], src, AF.Exp,
                                                bias=bias_sb[:, h:h + 1], scale=scl)
                        pr.op("scalar", expo, "px")
                        px_of[si] = C["px"]
                        dmask = None
                        if diag:
                            pr.w("vector", "px", px_of[si])
                            bi = (kc * 128 - qt * QT) // 128
                            pr.op("vector", lambda v, s_i=si, bi=bi: v.tensor_tensor(
                                pt_sb[s_i % 4][:, :], pt_sb[s_i % 4][:, :],
                                binm[:, bi * QT:(bi + 1) * QT], op=ALU.mult), "mk")
                            mkc += 1
                            dmask = C["mk"]
                        pend.append((qt, kc, h, si, dmask, first, last))
                        si += 1
                    while len(pend) > 3:
                        emit_av(pend.pop(0))
                    cci += 1
                    if ep_units and (layer == 1 or kc % 2 == 1
                                     or len(ep_units) >= KC - 1 - kc):
                        emit_unit(*ep_units.pop(0))
                        if ep_units and len(ep_units) >= KC - 1 - kc:
                            emit_unit(*ep_units.pop(0))
                while pend:
                    emit_av(pend.pop(0))
                while ep_units:
                    emit_unit(*ep_units.pop(0))
                avf = C["av"]
                for h in range(H):
                    pr.w("vector", "av", avf)
                    pr.w("vector", "mm", tr1_of.get(h, 0))
                    pr.op("vector", lambda v, h=h: v.tensor_copy(
                        oa_sbs[h][:, :], ps_oa[h][:, :]), "ev")
                    oaev[h] = C["ev"]
                    if layer == 0:
                        pr.w("scalar", "av", avf)
                        pr.w("scalar", "mm", tr1_of.get(h, 0))
                        pr.op("scalar", lambda s, h=h: s.copy(
                            ob_sbs[h][:, :], ps_obm[h * 64:(h + 1) * 64, :]), "ac")
                        obac[h] = C["ac"]
                ep_units = [(qt, h, stt) for h in range(H) for stt in range(4)]
            while ep_units:
                emit_unit(*ep_units.pop(0))
            emit_tr2_prev(keep=0)
            pr.w("gpsimd", "so0")
            pr.w("gpsimd", "so1")
            pr.op("gpsimd", lambda g: g.nop())
            pr.block()


def emit_mix(nc, pr, layer, pfx, ochunk, xT, yT, W, ybf=None, ag_hook=None):
    E = 6 if layer == 0 else 8
    NE = 7 if layer == 0 else 8
    TOPK = 2 if layer == 0 else 4
    IC_UP = 8 if layer == 0 else 16
    NTH = 1 if layer == 0 else 2
    IC_DN_ = 8 if layer == 0 else 16
    THT = NT // NTH
    NTC = THT // QT
    wo, wob, gw, gb = W["wo"], W["wob"], W["gw"], W["gb"]
    if layer == 0:
        w1t, w3t, w2t = W["w1t"], W["w3t"], W["w2t"]
    else:
        gupt, gupb, dnt, dnb = W["gupt"], W["gupb"], W["dnt"], W["dnb"]

    es = ExitStack()

    def sbuf(name, shape, dt=F32):
        return es.enter_context(nc.sbuf_tensor(pfx + name, shape, dt))

    ident = sbuf("ident", [P, P])
    ones_r = sbuf("ones_r", [1, P])
    ones_rq = sbuf("ones_rq", [1, QT])
    ones_cb = sbuf("ones_cb", [P, 1], BF16)
    x_sb = sbuf("x_sb", [P, 8 * NT])
    xb_sb = sbuf("xb_sb", [P, 8 * NT], BF16)
    gw_sb = sbuf("gw_sb", [P, 8 * E], BF16)
    gb_sb = sbuf("gb_sb", [E, 1])
    ge_tm = sbuf("ge_tm", [P, 16 * E])
    lg_sb = sbuf("lg_sb", [E, NT])
    lt_sb = sbuf("lt_sb", [P, E])
    l2_sb = [sbuf("l2_sb%d" % i, [P, E]) for i in range(2)]
    sel_sb = sbuf("sel_sb", [P, E])
    mrow = [sbuf("mrow%d" % i, [P, 1]) for i in range(6)]
    sg_scr = sbuf("sg_scr", [P, 2 * QT], BF16)
    t1_scr = sbuf("t1_scr", [P, 2 * QT])
    t2_scr = sbuf("t2_scr", [P, 2 * QT])
    rb_scr = sbuf("rb_scr", [P, QT], BF16)
    sq_scr = sbuf("sq_scr", [P, QT], BF16)
    rowa = sbuf("rowa", [1, QT])
    rowb = sbuf("rowb", [1, QT])
    rowc = sbuf("rowc", [1, QT])

    C = pr.cnt
    B0 = dict(C)

    with es:
        ps_a = es.enter_context(nc.psum_tensor(pfx + "ps_a", [P, QT], F32))
        ps_b2 = es.enter_context(nc.psum_tensor(pfx + "ps_b2", [P, QT], F32))
        ps_r1 = es.enter_context(nc.psum_tensor(pfx + "ps_r1", [P, QT], F32))
        ps_r2 = es.enter_context(nc.psum_tensor(pfx + "ps_r2", [P, QT], F32))
        ps_c1 = es.enter_context(nc.psum_tensor(pfx + "ps_c1", [P, QT], F32))
        ps_c2 = es.enter_context(nc.psum_tensor(pfx + "ps_c2", [P, QT], F32))
        ps_t = es.enter_context(nc.psum_tensor(pfx + "ps_t", [P, QT], F32))
        ps_g = es.enter_context(nc.psum_tensor(pfx + "ps_g", [P, QT], F32))

        def g_memsets(g):
            g.memset(ident[:, :], 0.0)
            g.memset(ones_r[:, :], 1.0)
            g.memset(ones_rq[:, :], 1.0)
            return g.memset(ones_cb[:, :], 1.0)
        pr.op("gpsimd", g_memsets, "gx")
        pr.w("gpsimd", "gx")
        pr.op("gpsimd", lambda g: g.affine_select(
            out=ident[:, :], in_=ident[:, :], compare_op=ALU.not_equal,
            fill=1.0, base=0, pattern=[[-1, P]], channel_multiplier=1), "gx")
        o_es = ExitStack()
        wo_sb = o_es.enter_context(nc.sbuf_tensor(pfx + "wo_sb", [P, 8 * D], BF16))
        wob_sb = o_es.enter_context(nc.sbuf_tensor(pfx + "wob_sb", [1, D], F32))
        o_tc = [o_es.enter_context(nc.sbuf_tensor(pfx + "o_tc%d" % i, [P, 8 * QT], BF16))
                for i in range(2)]
        for dc in range(8):
            pr.op("gpsimd", lambda g, d=dc: g.dma_start(
                out=wo_sb[:, d * D:(d + 1) * D], in_=wo[d * P:(d + 1) * P, :]), "lw", 16)
            pr.op("gpsimd", lambda g, d=dc: g.dma_start(
                out=gw_sb[:, d * E:(d + 1) * E], in_=gw[d * P:(d + 1) * P, :]), "lw", 16)
        pr.op("sync", lambda sy: sy.dma_start(
            out=wob_sb[:, :], in_=bass.AP(wob, 0, [[1, 1], [1, D]])), "lx", 16)
        pr.op("sync", lambda sy: sy.dma_start(
            out=gb_sb[:, :], in_=bass.AP(gb, 0, [[1, E], [1, 1]])), "lx", 16)
        for dc in range(8):
            pr.op("sync", lambda sy, d=dc: sy.dma_start(
                out=x_sb[:, d * NT:(d + 1) * NT], in_=xT[d * P:(d + 1) * P, :]), "lx", 16)
        pr.block()

        wo_mmN = {}
        wo_ev = {}
        for tc in range(4):
            losem = "lo%d" % (tc % 2)
            if tc >= 2:
                pr.w("sync", "mm", wo_mmN[tc - 2])
            for dc in range(8):
                pr.op("sync", lambda sy, d=dc, tc=tc: sy.dma_start(
                    out=o_tc[tc % 2][:, d * QT:(d + 1) * QT], in_=ochunk(d, tc)), losem, 16)
            for oc in range(8):
                g_ = tc * 8 + oc
                bank = (ps_a, ps_b2)[g_ % 2]
                pr.w("tensor", "lw")
                pr.w("tensor", "lx")
                pr.w("tensor", losem, B0[losem] + (tc // 2 + 1) * 128)
                if wo_ev.get(g_ - 2, 0):
                    pr.w("tensor", "ev", wo_ev[g_ - 2])

                def womm(t, tc=tc, oc=oc, bank=bank):
                    for dc in range(8):
                        t.matmul(bank[:, :], wo_sb[:, dc * D + oc * P: dc * D + (oc + 1) * P],
                                 o_tc[tc % 2][:, dc * QT:(dc + 1) * QT],
                                 start=(dc == 0), stop=False)
                    return t.matmul(bank[:, :], wob_sb[:, oc * P:(oc + 1) * P],
                                    ones_rq[:, :], start=False, stop=True)
                pr.op("tensor", womm, "mm")
                pr.w("vector", "mm")
                pr.op("vector", lambda v, tc=tc, oc=oc, bank=bank: v.tensor_tensor(
                    x_sb[:, oc * NT + tc * QT: oc * NT + (tc + 1) * QT],
                    x_sb[:, oc * NT + tc * QT: oc * NT + (tc + 1) * QT],
                    bank[:, :], op=ALU.add), "ev")
                wo_ev[g_] = C["ev"]
            wo_mmN[tc] = C["mm"]
        pr.block()
        o_es.close()
        ge_row = sbuf("ge_row", [1, NT])
        geB = sbuf("geB", [P, NT])
        h_sb = sbuf("h_sb", [P, IC_UP * THT], BF16)
        wu1 = [sbuf("wu1_%d" % i, [P, 8 * P], BF16) for i in range(2)]
        wu3 = [sbuf("wu3_%d" % i, [P, 8 * P], BF16) for i in range(2)]
        wd2 = [sbuf("wd2_%d" % i, [P, IC_DN_ * P], BF16) for i in range(2)]
        if layer == 1:
            gub_pm = sbuf("gub_pm", [P, 32])
            gub_tmp = sbuf("gub_tmp", [32, P])
            dnb_sb = sbuf("dnb_sb", [E, D])
            cbT = sbuf("cbT", [8, QT])
            pr.op("sync", lambda sy: sy.dma_start(out=dnb_sb[:, :], in_=dnb[:, :]), "lx", 16)

        def emit_ln(tcs=(0, 1, 2, 3)):
            banks = {0: (ps_r1, ps_r2), 1: (ps_a, ps_b2)}
            rowfree = {0: C["ev"], 1: C["ev"]}
            rowsmm = {}
            bc_mm = [0]
            xrdy = C["ev"]

            def stats(tc):
                b1, b2 = banks[tc % 2]
                for dc in range(8):
                    pr.w("scalar", "ev", xrdy)
                    pr.w("scalar", "ac")
                    pr.w("scalar", "mm")
                    pr.op("scalar", lambda s, tc=tc, dc=dc: s.copy(
                        rb_scr[:, :],
                        x_sb[:, dc * NT + tc * QT: dc * NT + (tc + 1) * QT]), "ac")
                    pr.w("scalar", "ac")
                    pr.op("scalar", lambda s: s.activation(
                        sq_scr[:, :], rb_scr[:, :], AF.Square), "ac")
                    pr.w("tensor", "ac")
                    if dc == 0:
                        pr.w("tensor", "ev", rowfree[tc % 2])

                    def lnmm(t, dc=dc, b1=b1, b2=b2):
                        t.matmul(b1[0:1, :], ones_cb[:, :], rb_scr[:, :],
                                 start=(dc == 0), stop=(dc == 7))
                        return t.matmul(b2[0:1, :], ones_cb[:, :], sq_scr[:, :],
                                        start=(dc == 0), stop=(dc == 7))
                    pr.op("tensor", lnmm, "mm")
                rowsmm[tc] = C["mm"]

            def rows_and_norm(tc):
                b1, b2 = banks[tc % 2]
                pr.w("vector", "mm", rowsmm[tc])
                if bc_mm[0]:
                    pr.w("vector", "mm", bc_mm[0])
                pr.op("vector", lambda v, b1=b1: v.tensor_scalar(
                    rowa[:, :], b1[0:1, :], 1.0 / D, None, op0=ALU.mult), "ev")
                pr.w("vector", "ev")
                pr.op("vector", lambda v, b2=b2: v.tensor_scalar(
                    rowb[:, :], b2[0:1, :], 1.0 / D, 1e-5, op0=ALU.mult, op1=ALU.add),
                    "ev")
                rowfree[tc % 2] = C["ev"]
                pr.w("vector", "ev")
                pr.op("vector", lambda v: v.tensor_tensor(
                    rowc[:, :], rowa[:, :], rowa[:, :], op=ALU.mult), "ev")
                pr.w("vector", "ev")
                pr.op("vector", lambda v: v.tensor_tensor(
                    rowb[:, :], rowb[:, :], rowc[:, :], op=ALU.subtract), "ev")
                pr.w("scalar", "ev")
                pr.op("scalar", lambda s: s.activation(rowb[:, :], rowb[:, :], AF.Ln),
                      "ac")
                pr.w("scalar", "ac")
                pr.op("scalar", lambda s: s.activation(rowb[:, :], rowb[:, :], AF.Exp,
                                                       scale=-0.5), "ac")
                pr.w("vector", "ac")
                pr.op("vector", lambda v: v.tensor_tensor(
                    rowa[:, :], rowa[:, :], rowb[:, :], op=ALU.mult), "ev")
                pr.w("tensor", "ev")

                def bcmm(t):
                    t.matmul(ps_c1[:, :], ones_r[:, :], rowb[:, :], start=True, stop=True)
                    return t.matmul(ps_c2[:, :], ones_r[:, :], rowa[:, :],
                                    start=True, stop=True)
                pr.op("tensor", bcmm, "mm")
                bc_mm[0] = C["mm"]
                pr.w("vector", "mm")
                for dc in range(8):
                    pr.op("vector", lambda v, dc=dc, tc=tc: v.tensor_tensor(
                        t1_scr[:, 0:QT], x_sb[:, dc * NT + tc * QT: dc * NT + (tc + 1) * QT],
                        ps_c1[:, :], op=ALU.mult), "ev")
                    pr.w("vector", "ev")
                    pr.op("vector", lambda v, dc=dc, tc=tc: v.tensor_tensor(
                        x_sb[:, dc * NT + tc * QT: dc * NT + (tc + 1) * QT],
                        t1_scr[:, 0:QT], ps_c2[:, :], op=ALU.subtract), "ev")
                    pr.w("vector", "ev")
                    pr.op("vector", lambda v, dc=dc, tc=tc: v.tensor_copy(
                        xb_sb[:, dc * NT + tc * QT: dc * NT + (tc + 1) * QT],
                        x_sb[:, dc * NT + tc * QT: dc * NT + (tc + 1) * QT]), "ev")
                    pr.w("vector", "ev")

            prev = None
            for tc in tcs:
                stats(tc)
                if prev is not None:
                    rows_and_norm(prev)
                prev = tc
            rows_and_norm(prev)

        emit_ln()
        pr.block()

        ge_cols = {}
        for tc in range(4):
            pr.w("tensor", "ev")

            def gmm(t, tc=tc):
                r = None
                for dc in range(8):
                    r = t.matmul(ps_a[0:E, :], gw_sb[:, dc * E:(dc + 1) * E],
                                 xb_sb[:, dc * NT + tc * QT: dc * NT + (tc + 1) * QT],
                                 start=(dc == 0), stop=(dc == 7))
                return r
            pr.op("tensor", gmm, "mm")
            if layer == 0:
                pr.w("scalar", "mm")
                pr.w("scalar", "lx")
                pr.op("scalar", lambda s, tc=tc: s.activation(
                    lg_sb[:, tc * QT:(tc + 1) * QT], ps_a[0:E, :], AF.Sigmoid,
                    bias=gb_sb[:, :]), "ac")
                pr.w("tensor", "ac")
            else:
                pr.w("vector", "mm")
                pr.w("vector", "lx")
                pr.op("vector", lambda v, tc=tc: v.tensor_scalar(
                    lg_sb[:, tc * QT:(tc + 1) * QT], ps_a[0:E, :], gb_sb[:, :], None,
                    op0=ALU.add), "ev")
                pr.w("tensor", "ev")
        for tt in range(16):
            pr.w("tensor", "ev")

            def ltr(t, tt=tt):
                return t.transpose(ps_t[:, 0:E], lg_sb[:, tt * P:(tt + 1) * P],
                                   ident[0:E, 0:E])
            pr.op("tensor", ltr, "mm")
            pr.w("vector", "mm")
            pr.op("vector", lambda v: v.tensor_copy(lt_sb[:, :], ps_t[:, 0:E]), "ev")
            pr.w("vector", "ev")
            pr.op("vector", lambda v: v.tensor_reduce(
                mrow[0][:, :], lt_sb[:, :], axis=AX.X, op=ALU.max), "ev")
            pr.w("vector", "ev")
            nk = TOPK
            cur = lt_sb
            for j in range(nk - 1):
                nxt = l2_sb[j % 2]
                pr.op("vector", lambda v, cur=cur, j=j: v.tensor_scalar(
                    sel_sb[:, :], cur[:, :], mrow[j][:, :], -1.0e30,
                    op0=ALU.is_ge, op1=ALU.mult), "ev")
                pr.w("vector", "ev")
                pr.op("vector", lambda v, cur=cur, nxt=nxt: v.tensor_tensor(
                    nxt[:, :], cur[:, :], sel_sb[:, :], op=ALU.add), "ev")
                pr.w("vector", "ev")
                pr.op("vector", lambda v, nxt=nxt, j=j: v.tensor_reduce(
                    mrow[j + 1][:, :], nxt[:, :], axis=AX.X, op=ALU.max), "ev")
                pr.w("vector", "ev")
                cur = nxt
            pr.op("vector", lambda v, nk=nk: v.tensor_scalar(
                sel_sb[:, :], lt_sb[:, :], mrow[nk - 1][:, :], None, op0=ALU.is_ge), "ev")
            pr.w("vector", "ev")
            if layer == 0:
                pr.op("vector", lambda v: v.tensor_scalar(
                    mrow[4][:, :], mrow[0][:, :], mrow[1][:, :], 1e-8,
                    op0=ALU.add, op1=ALU.add), "ev")
                pr.w("vector", "ev")
                pr.op("vector", lambda v: v.reciprocal(mrow[5][:, :], mrow[4][:, :]), "ev")
                pr.w("vector", "ev")
                pr.op("vector", lambda v: v.tensor_tensor(
                    l2_sb[0][:, :], sel_sb[:, :], lt_sb[:, :], op=ALU.mult), "ev")
                pr.w("vector", "ev")
                pr.op("vector", lambda v, tt=tt: v.tensor_scalar(
                    ge_tm[:, tt * E:(tt + 1) * E], l2_sb[0][:, :], mrow[5][:, :], None,
                    op0=ALU.mult), "ev")
                pr.w("vector", "ev")
            else:
                pr.op("vector", lambda v: v.tensor_scalar(
                    l2_sb[0][:, :], lt_sb[:, :], mrow[0][:, :], None, op0=ALU.subtract),
                    "ev")
                pr.w("scalar", "ev")
                pr.op("scalar", lambda s: s.activation(
                    sg_scr[0:P, 0:E], l2_sb[0][:, :], AF.Exp), "ac")
                pr.w("vector", "ac")
                pr.op("vector", lambda v: v.tensor_tensor(
                    l2_sb[1][:, :], sel_sb[:, :], sg_scr[0:P, 0:E], op=ALU.mult), "ev")
                pr.w("vector", "ev")
                pr.op("vector", lambda v: v.tensor_reduce(
                    mrow[4][:, :], l2_sb[1][:, :], axis=AX.X, op=ALU.add), "ev")
                pr.w("vector", "ev")
                pr.op("vector", lambda v: v.reciprocal(mrow[5][:, :], mrow[4][:, :]), "ev")
                pr.w("vector", "ev")
                pr.op("vector", lambda v, tt=tt: v.tensor_scalar(
                    ge_tm[:, tt * E:(tt + 1) * E], l2_sb[1][:, :], mrow[5][:, :], None,
                    op0=ALU.mult), "ev")
                pr.w("vector", "ev")
        pr.block()


        IC_DN = 8 if layer == 0 else 16
        NDP = 2 if layer == 0 else 1
        h_w1 = w1t.tensor if layer == 0 else gupt.tensor
        h_w3 = w3t.tensor if layer == 0 else None
        h_w2 = w2t.tensor if layer == 0 else dnt.tensor

        upf = {}
        dnf = {}
        upc = [0]
        dnc = [0]
        ucnt = [0]
        gub_rdy = [(0, 0)]
        act_done = {}
        sg_of = {}
        stt_of = {}
        h_of = {}
        vcnt = [0]
        dn_ev = {}
        ub_guard = [0]

        def up_load(e, ic, th):
            par = upc[0] % 2
            pr.w("gpsimd", "mm", upf.get(upc[0] - 2, 0))
            if layer == 0:
                pr.op("gpsimd", lambda g, e=e, ic=ic, par=par: g.dma_start(
                    out=wu1[par][:, :],
                    in_=bass.AP(h_w1, e * D * D + ic * 128,
                                [[D, P], [P * D, 8], [1, P]])), "lw0" if par == 0 else "lw1", 16)
                pr.op("gpsimd", lambda g, e=e, ic=ic, par=par: g.dma_start(
                    out=wu3[par][:, :],
                    in_=bass.AP(h_w3, e * D * D + ic * 128,
                                [[D, P], [P * D, 8], [1, P]])), "lw0" if par == 0 else "lw1", 16)
            else:
                pr.op("gpsimd", lambda g, e=e, ic=ic, par=par: g.dma_start(
                    out=wu1[par][:, :],
                    in_=bass.AP(h_w1, e * D * 4096 + ic * 128,
                                [[4096, P], [P * 4096, 8], [1, P]])), "lw0" if par == 0 else "lw1", 16)
                pr.op("gpsimd", lambda g, e=e, ic=ic, par=par: g.dma_start(
                    out=wu3[par][:, :],
                    in_=bass.AP(h_w1, e * D * 4096 + (16 + ic) * 128,
                                [[4096, P], [P * 4096, 8], [1, P]])), "lw0" if par == 0 else "lw1", 16)
            upc[0] += 1
            return par, C["lw%d" % par]

        for th in range(NTH):
            tok0 = th * THT
            for e in range(NE):
                shared = (layer == 0 and e == 6)
                if layer == 1:
                    pr.w("sync", "ev")
                    pr.op("sync", lambda sy, e=e: sy.dma_start(
                        out=gub_tmp[:, :],
                        in_=bass.AP(gupb.tensor, e * 4096, [[P, 32], [1, P]])), "lx", 16)
                    pr.w("tensor", "lx")
                    pr.w("tensor", "ev")
                    pr.op("tensor", lambda t: t.transpose(ps_t[:, 0:32], gub_tmp[:, :],
                                                          ident[0:32, 0:32]), "mm")
                    pr.w("scalar", "mm")
                    pr.op("scalar", lambda s: s.activation(
                        gub_pm[:, :], ps_t[:, 0:32], AF.Copy, scale=1.0), "ac")
                    pr.w("vector", "ac")
                    pr.op("vector", lambda v: v.tensor_scalar(
                        gub_pm[:, 16:32], gub_pm[:, 16:32], 1.0, None, op0=ALU.add), "ev")
                    pr.w("vector", "ev")
                    gub_rdy[0] = (C["ac"], C["ev"])
                ubanks = [(ps_a, ps_b2), (ps_c1, ps_c2)]
                for ic in range(IC_UP):
                    par, lwn = up_load(e, ic, th)
                    sem = "lw%d" % par
                    for tcc in range(NTC):
                        u = ucnt[0]
                        ucnt[0] += 1
                        pA, pB = ubanks[u % 2]
                        uq = slice((u % 2) * QT, (u % 2) * QT + QT)
                        pr.w("tensor", sem, lwn)
                        uw = max(act_done.get(u - 2, 0), ub_guard[0])
                        if uw:
                            pr.w("tensor", "ev", uw)

                        def upmm(t, par=par, tcc=tcc, tok0=tok0, pA=pA, pB=pB):
                            r = None
                            for dc in range(8):
                                t.matmul(pA[:, :], wu1[par][:, dc * P:(dc + 1) * P],
                                         xb_sb[:, dc * NT + tok0 + tcc * QT:
                                               dc * NT + tok0 + (tcc + 1) * QT],
                                         start=(dc == 0), stop=(dc == 7))
                            for dc in range(8):
                                r = t.matmul(pB[:, :], wu3[par][:, dc * P:(dc + 1) * P],
                                             xb_sb[:, dc * NT + tok0 + tcc * QT:
                                                   dc * NT + tok0 + (tcc + 1) * QT],
                                             start=(dc == 0), stop=(dc == 7))
                            return r
                        pr.op("tensor", upmm, "mm")
                        upf[upc[0] - 1] = C["mm"]
                        mmv = C["mm"]
                        hcol = ic * THT + tcc * QT
                        if layer == 0:
                            pr.w("scalar", "mm", mmv)
                            if act_done.get(u - 2, 0):
                                pr.w("scalar", "ev", act_done[u - 2])
                            pr.op("scalar", lambda s, pA=pA, uq=uq: s.activation(
                                sg_scr[:, uq], pA[:, :], AF.Sigmoid), "ac")
                            pr.w("vector", "ac")
                            pr.op("vector", lambda v, pA=pA, uq=uq: v.tensor_tensor(
                                t1_scr[:, uq], pA[:, :], sg_scr[:, uq], op=ALU.mult), "ev")
                            pr.w("vector", "ev")
                            pr.op("vector", lambda v, hcol=hcol, pB=pB, uq=uq: v.tensor_tensor(
                                h_sb[:, hcol:hcol + QT], t1_scr[:, uq], pB[:, :],
                                op=ALU.mult), "ev")
                        else:
                            pr.w("vector", "mm", mmv)
                            pr.w("vector", "ac", gub_rdy[0][0])
                            pr.w("vector", "ev", gub_rdy[0][1])
                            if u >= 2:
                                pr.w("vector", "ev", h_of.get(u - 2, 0))
                            pr.op("vector", lambda v, ic=ic, pA=pA, uq=uq: v.tensor_scalar(
                                t1_scr[:, uq], pA[:, :], gub_pm[:, ic:ic + 1], 7.0,
                                op0=ALU.add, op1=ALU.min), "ev")
                            pr.op("vector", lambda v, ic=ic, pB=pB, uq=uq: v.tensor_scalar(
                                t2_scr[:, uq], pB[:, :], gub_pm[:, 16 + ic:17 + ic], -6.0,
                                op0=ALU.add, op1=ALU.max), "ev")
                            act_done[u] = C["ev"]
                            pr.w("scalar", "ev")
                            if u >= 2:
                                pr.w("scalar", "ev", h_of.get(u - 2, 0))
                            pr.op("scalar", lambda s, uq=uq: s.activation(
                                sg_scr[:, uq], t1_scr[:, uq], AF.Sigmoid, scale=1.702), "ac")
                            sg_of[u] = C["ac"]
                            pr.w("vector", "ev")
                            pr.op("vector", lambda v, uq=uq: v.scalar_tensor_tensor(
                                t2_scr[:, uq], t2_scr[:, uq], 8.0, t1_scr[:, uq],
                                op0=ALU.min, op1=ALU.mult), "ev")
                            stt_of[u] = C["ev"]
                            pr.w("vector", "ac", sg_of[u])
                            pr.op("vector", lambda v, hcol=hcol, uq=uq: v.tensor_tensor(
                                h_sb[:, hcol:hcol + QT], t2_scr[:, uq], sg_scr[:, uq],
                                op=ALU.mult), "ev")
                            h_of[u] = C["ev"]
                        if layer == 0:
                            act_done[u] = C["ev"]
                if not shared:
                    for tc4 in range(4):
                        pr.w("tensor", "ev")

                        def getr(t, e=e, tc4=tc4):
                            r = None
                            for tt4 in range(4):
                                tt = tc4 * 4 + tt4
                                r = t.transpose(ps_t[0:1, tt4 * P:(tt4 + 1) * P],
                                                ge_tm[:, tt * E + e: tt * E + e + 1],
                                                ident[:, :])
                            return r
                        pr.op("tensor", getr, "mm")
                        pr.w("vector", "mm")
                        pr.op("vector", lambda v, tc4=tc4: v.tensor_copy(
                            ge_row[:, tc4 * QT:(tc4 + 1) * QT], ps_t[0:1, 0:QT]), "ev")
                        pr.w("tensor", "ev")
                        pr.op("tensor", lambda t, tc4=tc4: t.matmul(
                            ps_g[:, :], ones_r[:, :], ge_row[:, tc4 * QT:(tc4 + 1) * QT],
                            start=True, stop=True), "mm")
                        pr.w("vector", "mm")
                        pr.op("vector", lambda v, tc4=tc4: v.tensor_copy(
                            geB[:, tc4 * QT:(tc4 + 1) * QT], ps_g[:, :]), "ev")
                dn_guard = C["ev"]
                dbanks = [ps_a, ps_b2, ps_c1, ps_c2]
                for do in range(8):
                    par = dnc[0] % 2
                    sem = "lw%d" % (2 + par)
                    pr.w("gpsimd", "mm", dnf.get(dnc[0] - 2, 0))
                    rows = D if layer == 0 else 2048
                    pr.op("gpsimd", lambda g, e=e, do=do, rows=rows, par=par: g.dma_start(
                        out=wd2[par][:, :],
                        in_=bass.AP(h_w2, e * rows * D + do * P,
                                    [[D, P], [P * D, IC_DN], [1, P]])), sem, 16)
                    dnc[0] += 1
                    lwn = C[sem]
                    for tcc in range(NTC):
                        dv_ = vcnt[0]
                        vcnt[0] += 1
                        bank = dbanks[dv_ % 4]
                        tq = slice((dv_ % 2) * QT, (dv_ % 2) * QT + QT)
                        pr.w("tensor", sem, lwn)
                        pr.w("tensor", "ev", max(dn_ev.get(dv_ - 4, 0), dn_guard))

                        def dnmm(t, par=par, tcc=tcc, bank=bank):
                            r = None
                            for ic2 in range(IC_DN):
                                r = t.matmul(bank[:, :], wd2[par][:, ic2 * P:(ic2 + 1) * P],
                                             h_sb[:, ic2 * THT + tcc * QT:
                                                  ic2 * THT + (tcc + 1) * QT],
                                             start=(ic2 == 0), stop=(ic2 == IC_DN - 1))
                            return r
                        pr.op("tensor", dnmm, "mm")
                        dnf[dnc[0] - 1] = C["mm"]
                        mmv = C["mm"]
                        xcol = do * NT + tok0 + tcc * QT
                        pr.w("vector", "mm", mmv)
                        if shared:
                            pr.op("vector", lambda v, xcol=xcol, bank=bank: v.tensor_tensor(
                                x_sb[:, xcol:xcol + QT], x_sb[:, xcol:xcol + QT],
                                bank[:, :], op=ALU.add), "ev")
                        else:
                            pr.op("vector", lambda v, bank=bank, tcc=tcc, tok0=tok0, tq=tq:
                                  v.tensor_tensor(t2_scr[:, tq], bank[:, :],
                                                  geB[:, tok0 + tcc * QT:
                                                      tok0 + (tcc + 1) * QT],
                                                  op=ALU.mult), "ev")
                            pr.w("vector", "ev")
                            pr.op("vector", lambda v, xcol=xcol, tq=tq: v.tensor_tensor(
                                x_sb[:, xcol:xcol + QT], x_sb[:, xcol:xcol + QT],
                                t2_scr[:, tq], op=ALU.add), "ev")
                        dn_ev[dv_] = C["ev"]
                ub_guard[0] = C["ev"]

        if layer == 1:
            for tc in range(4):
                for tt4 in range(4):
                    tt = tc * 4 + tt4
                    pr.w("tensor", "ev")
                    pr.op("tensor", lambda t, tt=tt, tt4=tt4: t.transpose(
                        ps_t[0:E, tt4 * P:(tt4 + 1) * P], ge_tm[:, tt * E:(tt + 1) * E],
                        ident[:, :]), "mm")
                pr.w("vector", "mm")
                pr.op("vector", lambda v: v.tensor_copy(
                    cbT[:, :], ps_t[0:E, 0:QT]), "ev")
                for do in range(8):
                    pr.w("tensor", "ev")
                    pr.op("tensor", lambda t, tc=tc, do=do: t.matmul(
                        ps_a[:, :], dnb_sb[:, do * P:(do + 1) * P],
                        cbT[:, :], start=True, stop=True), "mm")
                    pr.w("vector", "mm")
                    xcol = do * NT + tc * QT
                    pr.op("vector", lambda v, xcol=xcol: v.tensor_tensor(
                        x_sb[:, xcol:xcol + QT], x_sb[:, xcol:xcol + QT],
                        ps_a[:, :], op=ALU.add), "ev")
        pr.block()


        st_half = 0
        if ybf is not None:
            emit_ln(tcs=(0, 1))
            pr.w("sync", "ev")
            for dc in range(8):
                pr.op("sync", lambda sy, d=dc: sy.dma_start(
                    out=ybf[0][d * P:(d + 1) * P, :],
                    in_=xb_sb[:, d * NT: d * NT + NT // 2]), "st", 16)
            st_half = C["st"]
            if ag_hook is not None:
                ag_hook()
            emit_ln(tcs=(2, 3))
            pr.block()
            for dc in range(8):
                pr.op("sync", lambda sy, d=dc: sy.dma_start(
                    out=yT[d * P:(d + 1) * P, :], in_=x_sb[:, d * NT:(d + 1) * NT]),
                    "st", 16)
                pr.op("sync", lambda sy, d=dc: sy.dma_start(
                    out=ybf[1][d * P:(d + 1) * P, :],
                    in_=xb_sb[:, d * NT + NT // 2: (d + 1) * NT]), "st", 16)
        else:
            emit_ln()
            pr.block()
            for dc in range(8):
                pr.op("sync", lambda sy, d=dc: sy.dma_start(
                    out=yT[d * P:(d + 1) * P, :], in_=x_sb[:, d * NT:(d + 1) * NT]),
                    "st", 16)
        pr.w("gpsimd", "st")
        pr.op("gpsimd", lambda g: g.nop())
        pr.block()
        return {"st_half": st_half}




SEM_NAMES = ["ld", "ld0", "ld1", "lw", "lw0", "lw1", "lw2", "lw3", "gx",
             "mm", "ev", "ac", "sm", "dv", "px", "mk", "av", "dts0", "dts1",
             "dts2", "dts3", "g2s", "glw", "so0", "so1", "lo0", "lo1", "lx",
             "st", "cc"]


def build_attn(layer, flash_qts=None):
    nc = bass.Bass("TRN2", target_bir_lowering=False)
    xT = [nc.dram_tensor("xT%d" % b, [D, T], F32, kind="ExternalInput").ap()
          for b in range(2)]
    wq = nc.dram_tensor("wq", [D, 128], F32, kind="ExternalInput").ap()
    wk = nc.dram_tensor("wk", [D, 128], F32, kind="ExternalInput").ap()
    wv = nc.dram_tensor("wv", [D, 128], F32, kind="ExternalInput").ap()
    theta = gam = bet = None
    if layer == 0:
        theta = nc.dram_tensor("theta", [8], F32, kind="ExternalInput")
        gam = nc.dram_tensor("gam", [128], F32, kind="ExternalInput")
        bet = nc.dram_tensor("bet", [128], F32, kind="ExternalInput")
    otm = [nc.dram_tensor("otm%d" % b, [128, T], BF16, kind="ExternalOutput").ap()
           for b in range(2)]
    es = ExitStack()
    pr = Prog(nc, es, SEM_NAMES)
    HD = 128 if layer == 0 else 64
    with es:
        for b in range(2):
            emit_attn(nc, pr, layer, "a%d_" % b,
                      lambda dc, tc, b=b: xT[b][dc * P:(dc + 1) * P, tc * QT:(tc + 1) * QT],
                      lambda h, qt, b=b: otm[b][h * HD:(h + 1) * HD, qt * QT:(qt + 1) * QT],
                      wq, wk, wv, theta=theta, gam=gam, bet=bet,
                      flash_qts=flash_qts)
        pr.finalize()
    return nc


def _mix_weight_aps(nc, layer):
    E = 6 if layer == 0 else 8
    NE = 7 if layer == 0 else 8
    W = {"wo": nc.dram_tensor("wo", [D, D], F32, kind="ExternalInput").ap(),
         "wob": nc.dram_tensor("wob", [D], F32, kind="ExternalInput"),
         "gw": nc.dram_tensor("gw", [D, E], F32, kind="ExternalInput").ap(),
         "gb": nc.dram_tensor("gb", [E], F32, kind="ExternalInput")}
    if layer == 0:
        W["w1t"] = nc.dram_tensor("w1t", [NE, D, D], F32, kind="ExternalInput").ap()
        W["w3t"] = nc.dram_tensor("w3t", [NE, D, D], F32, kind="ExternalInput").ap()
        W["w2t"] = nc.dram_tensor("w2t", [NE, D, D], F32, kind="ExternalInput").ap()
    else:
        W["gupt"] = nc.dram_tensor("gupt", [E, D, 4096], F32, kind="ExternalInput").ap()
        W["gupb"] = nc.dram_tensor("gupb", [E, 4096], F32, kind="ExternalInput").ap()
        W["dnt"] = nc.dram_tensor("dnt", [E, 2048, D], F32, kind="ExternalInput").ap()
        W["dnb"] = nc.dram_tensor("dnb", [E, D], F32, kind="ExternalInput").ap()
    return W


def build_mix(layer):
    nc = bass.Bass("TRN2", target_bir_lowering=False)
    oT = nc.dram_tensor("oT", [D, NT], BF16, kind="ExternalInput").ap()
    xT = nc.dram_tensor("xT", [D, NT], F32, kind="ExternalInput").ap()
    W = _mix_weight_aps(nc, layer)
    yT = nc.dram_tensor("yT", [D, NT], F32, kind="ExternalOutput").ap()
    es = ExitStack()
    pr = Prog(nc, es, SEM_NAMES)
    with es:
        emit_mix(nc, pr, layer, "m_",
                 lambda dc, tc: oT[dc * P:(dc + 1) * P, tc * QT:(tc + 1) * QT],
                 xT, yT, W)
        pr.finalize()
    return nc


def build_fused():
    nc = bass.Bass("TRN2", target_bir_lowering=False)
    GROUPS = [[0, 1, 2, 3, 4, 5, 6, 7]]
    xTs = nc.dram_tensor("xTs", [D, NT], F32, kind="ExternalInput").ap()
    xTf = [nc.dram_tensor("xTf%d" % b, [D, T], F32, kind="ExternalInput").ap()
           for b in range(2)]
    a_in = {}
    for L in (0, 1):
        p = "A%d_" % L
        a_in[L] = {k: nc.dram_tensor(p + k, [D, 128], F32, kind="ExternalInput").ap()
                   for k in ("wq", "wk", "wv")}
    th = nc.dram_tensor("L0_theta", [8], F32, kind="ExternalInput")
    gam = nc.dram_tensor("L0_gam", [128], F32, kind="ExternalInput")
    bet = nc.dram_tensor("L0_bet", [128], F32, kind="ExternalInput")
    m_in = {}
    for L in (0, 1):
        E = 6 if L == 0 else 8
        NE = 7 if L == 0 else 8
        p = "M%d_" % L
        W = {"wo": nc.dram_tensor(p + "wo", [D, D], F32, kind="ExternalInput").ap(),
             "wob": nc.dram_tensor(p + "wob", [D], F32, kind="ExternalInput"),
             "gw": nc.dram_tensor(p + "gw", [D, E], F32, kind="ExternalInput").ap(),
             "gb": nc.dram_tensor(p + "gb", [E], F32, kind="ExternalInput")}
        if L == 0:
            for k in ("w1t", "w3t", "w2t"):
                W[k] = nc.dram_tensor(p + k, [NE, D, D], F32, kind="ExternalInput").ap()
        else:
            W["gupt"] = nc.dram_tensor(p + "gupt", [E, D, 4096], F32, kind="ExternalInput").ap()
            W["gupb"] = nc.dram_tensor(p + "gupb", [E, 4096], F32, kind="ExternalInput").ap()
            W["dnt"] = nc.dram_tensor(p + "dnt", [E, 2048, D], F32, kind="ExternalInput").ap()
            W["dnb"] = nc.dram_tensor(p + "dnb", [E, D], F32, kind="ExternalInput").ap()
        m_in[L] = W
    yout = nc.dram_tensor("yout", [D, NT], F32, kind="ExternalOutput").ap()
    agx_in = [nc.dram_tensor("agx_in%d" % L, [D, NT], BF16) for L in (0, 1)]
    agx1_in = [nc.dram_tensor("agx1_in%d" % i, [D, NT // 2], BF16) for i in range(2)]
    agx = [nc.dram_tensor("agx%d" % L, [8 * D, NT], BF16, addr_space="Shared")
           for L in (0, 1)]
    agx1h = [nc.dram_tensor("agx1h%d" % i, [8 * D, NT // 2], BF16, addr_space="Shared")
             for i in range(2)]
    ago_in = [nc.dram_tensor("ago_in%d" % L, [D, NT], BF16) for L in (0, 1)]
    ago = [nc.dram_tensor("ago%d" % L, [D, NT], BF16) for L in (0, 1)]
    x2s = nc.dram_tensor("x2s", [D, NT], F32)

    es = ExitStack()
    pr = Prog(nc, es, SEM_NAMES)

    def agx_chunk(L, b):
        if L == 1:
            def f1(dc, tc):
                r = b * 4 + tc // 4
                cc = tc % 4
                hh = agx1h[0] if cc < 2 else agx1h[1]
                return bass.AP(hh, (r * D + dc * P) * (NT // 2) + (cc % 2) * QT,
                               [[NT // 2, P], [1, QT]])
            return f1
        h = agx[L]

        def f(dc, tc):
            r = b * 4 + tc // 4
            return bass.AP(h, (r * D + dc * P) * NT + (tc % 4) * QT,
                           [[NT, P], [1, QT]])
        return f

    def a2a_odst(L, b, HD):
        h = ago_in[L]

        def f(hh, qt):
            row = (b * 4 + qt // 4) * 128 + hh * HD
            return bass.AP(h, row * NT + (qt % 4) * QT, [[NT, HD], [1, QT]])
        return f

    def ago_chunk(L):
        h = ago[L]

        def f(dc, tc):
            return bass.AP(h, dc * P * NT + tc * QT, [[NT, P], [1, QT]])
        return f

    def cast_and_ag(L, src_f32):
        if src_f32 is not None:
            pr.op("gpsimd", lambda g: g.dma_start(
                out=agx_in[L].ap(), in_=src_f32), "lw", 16)
            pr.w("gpsimd", "lw")
        else:
            pr.w("gpsimd", "st")
        pr.op("gpsimd", lambda g: g.collective_compute(
            "AllGather", mybir.AluOpType.bypass, replica_groups=GROUPS,
            ins=[agx_in[L].ap()], outs=[agx[L].ap()]), "cc")
        return ("cc", pr.cnt["cc"])

    def a2a_o(L):
        pr.w("gpsimd", "so0")
        pr.w("gpsimd", "so1")
        with allow_ici_all_to_all():
            pr.op("gpsimd", lambda g: g.collective_compute(
                "AllToAll", mybir.AluOpType.bypass, replica_groups=GROUPS,
                ins=[ago_in[L].ap()], outs=[ago[L].ap()]), "cc")
        pr.w("sync", "cc")
        pr.w("sync", "st")

    with es:
        sh0 = {}
        for b in range(2):
            emit_attn(nc, pr, 0, "a0%d_" % b,
                      lambda dc, tc, b=b: xTf[b][dc * P:(dc + 1) * P,
                                                 tc * QT:(tc + 1) * QT],
                      a2a_odst(0, b, 128),
                      a_in[0]["wq"], a_in[0]["wk"], a_in[0]["wv"],
                      theta=th, gam=gam, bet=bet, shared=sh0)
        a2a_o(0)

        def ag1_hook():
            pr.w("gpsimd", "st")
            pr.op("gpsimd", lambda g: g.collective_compute(
                "AllGather", mybir.AluOpType.bypass, replica_groups=GROUPS,
                ins=[agx1_in[0].ap()], outs=[agx1h[0].ap()]), "cc")
        emit_mix(nc, pr, 0, "m0_", ago_chunk(0), xTs, x2s.ap(), m_in[0],
                 ybf=[agx1_in[0].ap(), agx1_in[1].ap()], ag_hook=ag1_hook)
        g1 = ("cc", pr.cnt["cc"])
        pr.w("gpsimd", "st")
        pr.op("gpsimd", lambda g: g.collective_compute(
            "AllGather", mybir.AluOpType.bypass, replica_groups=GROUPS,
            ins=[agx1_in[1].ap()], outs=[agx1h[1].ap()]), "cc")
        g1b = ("cc", pr.cnt["cc"])
        sh_es = ExitStack()
        sh1 = {"cs": sh_es.enter_context(nc.sbuf_tensor("sh_cs", [P, 2048], F32)),
               "sn": sh_es.enter_context(nc.sbuf_tensor("sh_sn", [P, 2048], F32))}
        with sh_es:
            for b in range(2):
                emit_attn(nc, pr, 1, "a1%d_" % b, agx_chunk(1, b), a2a_odst(1, b, 64),
                          a_in[1]["wq"], a_in[1]["wk"], a_in[1]["wv"], shared=sh1,
                          xgate=(g1 if b == 0 else None),
                          xgate2=(g1b if b == 0 else None))
        a2a_o(1)
        emit_mix(nc, pr, 1, "m1_", ago_chunk(1), x2s.ap(), yout, m_in[1])
        pr.finalize()
    return nc



_BF = ml_dtypes.bfloat16


def _cont(a):
    return np.ascontiguousarray(a)


def _mix_weights(z, layer):
    if layer == 0:
        w1 = [z["L0_ffn_ew1"][e].T for e in range(6)] + [z["L0_ffn_fc1"].T]
        w3 = [z["L0_ffn_ew3"][e].T for e in range(6)] + [z["L0_ffn_fc2"].T]
        w2 = [z["L0_ffn_ew2"][e].T for e in range(6)] + [z["L0_ffn_fc3"].T]
        return {"wo": _cont(z["L0_attn_Wo"]), "wob": _cont(z["L0_attn_bo"]),
                "gw": _cont(z["L0_ffn_gate_w"].T), "gb": _cont(z["L0_ffn_gate_b"]),
                "w1t": _cont(np.stack(w1)), "w3t": _cont(np.stack(w3)),
                "w2t": _cont(np.stack(w2))}
    return {"wo": _cont(z["L1_attn_wo_w"]), "wob": _cont(z["L1_attn_wo_b"]),
            "gw": _cont(z["L1_ffn_router_weight"].T), "gb": _cont(z["L1_ffn_router_bias"]),
            "gupt": _cont(np.transpose(z["L1_ffn_gate_up_proj"], (0, 2, 1))),
            "gupb": _cont(z["L1_ffn_gate_up_proj_bias"]),
            "dnt": _cont(np.transpose(z["L1_ffn_down_proj"], (0, 2, 1))),
            "dnb": _cont(z["L1_ffn_down_proj_bias"])}


_PROGS = {}


def _prog(kind):
    if kind not in _PROGS:
        _PROGS[kind] = build_fused()
    return _PROGS[kind]


def kernel(**inputs):
    z = {k: np.asarray(v) for k, v in inputs.items()}
    x = z["L0_attn_x"]
    xT = [_cont(x[b].T) for b in range(2)]
    mw = {L: _mix_weights(z, L) for L in (0, 1)}
    maps = []
    for c in range(8):
        b, ts = c // 4, c % 4
        cs = slice(c * 128, (c + 1) * 128)
        m = {"xTs": _cont(xT[b][:, ts * NT:(ts + 1) * NT]),
             "xTf0": xT[0], "xTf1": xT[1],
             "A0_wq": _cont(z["L0_attn_Wq"][:, cs]),
             "A0_wk": _cont(z["L0_attn_Wk"][:, cs]),
             "A0_wv": _cont(z["L0_attn_Wv"][:, cs]),
             "A1_wq": _cont(z["L1_attn_wq"][:, cs]),
             "A1_wk": _cont(z["L1_attn_wk"][:, cs]),
             "A1_wv": _cont(z["L1_attn_wv"][:, cs]),
             "L0_theta": _cont(z["L0_attn_theta"]),
             "L0_gam": _cont(z["L0_attn_gamma"][cs]),
             "L0_bet": _cont(z["L0_attn_beta"][cs])}
        for L in (0, 1):
            for k, v in mw[L].items():
                m["M%d_%s" % (L, k)] = v
        maps.append(m)
    res = run_bass_kernel_spmd(_prog("fused"), maps, core_ids=list(range(8)))
    out = np.empty((2, T, D), np.float32)
    for c in range(8):
        b, ts = c // 4, c % 4
        out[b, ts * NT:(ts + 1) * NT, :] = np.asarray(res.results[c]["yout"]).T
    return out


# version 26
# speedup vs baseline: 1.1242x; 1.0069x over previous
import math
from contextlib import ExitStack

import numpy as np
import ml_dtypes
import os
os.environ.setdefault("TRN_ALLOW_ICI_ALL_TO_ALL", "1")

import concourse.bass as bass
import concourse.mybir as mybir
from concourse.bass_utils import run_bass_kernel_spmd
from concourse.aot_env import allow_ici_all_to_all

F32 = mybir.dt.float32
BF16 = mybir.dt.bfloat16
AF = mybir.ActivationFunctionType
ALU = mybir.AluOpType
AX = mybir.AxisListType

P = 128
D = 1024
T = 8192
NT = 2048
QT = 512
NQT = T // QT

ENGS = ("sync", "gpsimd", "tensor", "vector", "scalar")


class Prog:

    def __init__(self, nc, es, sem_names):
        self.nc = nc
        self.ops = {e: [] for e in ENGS}
        self.sems = {n: es.enter_context(nc.semaphore(n)) for n in sem_names}
        self.cnt = {n: 0 for n in sem_names}

    def w(self, eng, name, val=None):
        v = self.cnt[name] if val is None else val
        if v <= 0:
            return
        s = self.sems[name]
        self.ops[eng].append(lambda e: e.wait_ge(s, v))

    def op(self, eng, fn, inc=None, k=1):
        if inc is None:
            self.ops[eng].append(fn)
        else:
            s = self.sems[inc]
            self.ops[eng].append(lambda e: fn(e).then_inc(s, k))
            self.cnt[inc] += k

    def block(self):
        cur = dict(self.cnt)
        for e in ENGS:
            for s, v in cur.items():
                if v:
                    self.w(e, s, v)

    def finalize(self):
        live = [e for e in ENGS if self.ops[e]]
        with self.nc.Block() as blk:
            for e in live:
                lst = self.ops[e]
                self.ops[e] = []

                def body(eng, _lst=lst):
                    for f in _lst:
                        f(eng)
                getattr(blk, e)(body)



def emit_attn(nc, pr, layer, pfx, xchunk, odst, wq, wk, wv,
              theta=None, gam=None, bet=None, flash_qts=None, shared=None,
              xgate=None, xgate2=None):
    HD = 128 if layer == 0 else 64
    H = 128 // HD
    TOTC = 128
    OFF = QT + 127
    GW = T + OFF + 1
    if layer == 0:
        GW2 = GW + 128
        g2 = (shared["g2"] if shared and "g2" in shared
              else nc.dram_tensor(pfx + "g2", [P, GW2], BF16))

    es = ExitStack()

    def sbuf(name, shape, dt=F32):
        return es.enter_context(nc.sbuf_tensor(pfx + name, shape, dt))

    ident = sbuf("ident", [P, P])
    identb = sbuf("identb", [P, P], BF16)
    ones_r = sbuf("ones_r", [1, P])
    ones_cb = sbuf("ones_cb", [P, 1], BF16)
    wq_sb = sbuf("wq_sb", [P, 8 * TOTC], BF16)
    wk_sb = sbuf("wk_sb", [P, 8 * TOTC], BF16)
    wv_sb = sbuf("wv_sb", [P, 8 * TOTC], BF16)
    qT_sb = sbuf("qT_sb", [P, T], BF16)
    kT_sb = sbuf("kT_sb", [P, T], BF16)
    v1 = sbuf("v1", [P, H * 64 * 65], BF16)
    v2 = sbuf("v2", [P, H * 64 * 64], BF16) if layer == 0 else None
    xbuf = [sbuf("xb%d" % i, [P, 8 * QT], BF16) for i in range(2)]
    vtmp = [sbuf("vtmp%d" % i, [P, QT], BF16) for i in range(2)]
    bias_sb = sbuf("bias_sb", [P, H])
    binm = sbuf("binm", [P, 4 * QT], BF16)
    stat = sbuf("stat", [1, 64])
    sq_scr = sbuf("sq_scr", [P, QT], BF16)
    pt_sb = [sbuf("pt%d" % i, [P, QT], BF16) for i in range(4)]
    oa_sbs = [sbuf("oa_sb%d" % i, [65, QT]) for i in range(H)]
    ob_sbs = [sbuf("ob_sb%d" % i, [64, QT]) for i in range(H)] if layer == 0 else None
    otile = [sbuf("otile%d" % i, [P, HD], BF16) for i in range(2)]
    ot_f = sbuf("ot_f", [P, 128])
    gn_scr = sbuf("gn_scr", [P, 128], BF16)
    linv = sbuf("linv", [P, 1])
    if layer == 0:
        dt_sb = [sbuf("dt%d" % i, [P, QT], BF16) for i in range(4)]
        sp_sb = [sbuf("sp%d" % i, [P, QT]) for i in range(4)]
        gB = sbuf("gB", [P, TOTC])
        bB = sbuf("bB", [P, TOTC])
        grow = sbuf("grow", [1, TOTC])
        brow = sbuf("brow", [1, TOTC])
        gnmu = sbuf("gnmu", [P, 1])
        gnsq = sbuf("gnsq", [P, 1])
        gnms = sbuf("gnms", [P, 1])
        gnvar = sbuf("gnvar", [P, 1])
        gnrs = sbuf("gnrs", [P, 1])
        rg_sb = sbuf("rg_sb", [P, 128])
        gn_f2 = sbuf("gn_f2", [P, 128])
        ones_c8 = sbuf("ones_c8", [8, 1])
    else:
        if shared is not None:
            cs_sb = shared["cs"]
            sn_sb = shared["sn"]
        else:
            cs_sb = sbuf("cs_sb", [P, 2048])
            sn_sb = sbuf("sn_sb", [P, 2048])
        rtm = sbuf("rtm", [P, 512])
        rtm2 = sbuf("rtm2", [P, 512])
        rsc = sbuf("rsc", [P, 256])
        rsc2 = sbuf("rsc2", [P, 256])
        halfpi = sbuf("halfpi", [P, 1])
        ev_sb = [sbuf("ev_sb%d" % i, [64, QT]) for i in range(4)]

    C = pr.cnt
    B0 = dict(C)

    with es:
        ph_es = ExitStack()
        ps_pr = ph_es.enter_context(nc.psum_tensor(pfx + "ps_pr", [P, QT], F32))
        ps_pr2 = ph_es.enter_context(nc.psum_tensor(pfx + "ps_pr2", [P, QT], F32))
        ps_pr3 = ph_es.enter_context(nc.psum_tensor(pfx + "ps_pr3", [P, QT], F32))
        ps_pr4 = ph_es.enter_context(nc.psum_tensor(pfx + "ps_pr4", [P, QT], F32))
        ps_pr5 = ph_es.enter_context(nc.psum_tensor(pfx + "ps_pr5", [P, QT], F32))
        ps_pr6 = ph_es.enter_context(nc.psum_tensor(pfx + "ps_pr6", [P, QT], F32))
        ps_vtb = ph_es.enter_context(nc.psum_tensor(pfx + "ps_vtb", [P, QT], BF16))

        def g_memsets(g):
            g.memset(ident[:, :], 0.0)
            g.memset(identb[:, :], 0.0)
            g.memset(ones_r[:, :], 1.0)
            g.memset(ones_cb[:, :], 1.0)
            if layer == 0:
                g.memset(ones_c8[:, :], 1.0)
            for h in range(H):
                g.memset(v1[:, h * 4160 + 64:(h + 1) * 4160:65], 1.0)
            for i in range(3):
                g.memset(binm[:, i * QT:(i + 1) * QT], 1.0)
            return g.memset(binm[:, 3 * QT:4 * QT], 1.0)
        pr.op("gpsimd", g_memsets, "gx")
        pr.w("gpsimd", "gx")

        def g_selects(g):
            g.affine_select(out=ident[:, :], in_=ident[:, :], compare_op=ALU.not_equal,
                            fill=1.0, base=0, pattern=[[-1, P]], channel_multiplier=1)
            g.affine_select(out=identb[:, :], in_=identb[:, :], compare_op=ALU.not_equal,
                            fill=1.0, base=0, pattern=[[-1, P]], channel_multiplier=1)
            r = None
            for i in range(4):
                m = binm[:, i * QT:(i + 1) * QT]
                r = g.affine_select(out=m, in_=m, compare_op=ALU.is_ge, fill=0.0,
                                    base=-128 * i, pattern=[[1, QT]], channel_multiplier=-1)
            return r
        pr.op("gpsimd", g_selects, "gx")
        pr.w("gpsimd", "gx")
        for dc in range(8):
            for w_ap, w_sb in ((wq, wq_sb), (wk, wk_sb), (wv, wv_sb)):
                pr.op("gpsimd", lambda g, a=w_ap, s=w_sb, d=dc: g.dma_start(
                    out=s[:, d * TOTC:(d + 1) * TOTC], in_=a[d * P:(d + 1) * P, :]), "lw", 16)
        pr.block()

        skip_tables = bool(shared) and shared.get("built", False)
        table_steps = []
        if layer == 0 and skip_tables:
            pr.op("gpsimd", lambda g: g.dma_start(
                out=grow[:, :], in_=bass.AP(gam, 0, [[1, 1], [1, TOTC]])), "ld", 16)
            pr.op("gpsimd", lambda g: g.dma_start(
                out=brow[:, :], in_=bass.AP(bet, 0, [[1, 1], [1, TOTC]])), "ld", 16)
            pr.w("tensor", "ld")
            pr.w("tensor", "ac")
            pr.op("tensor", lambda t: t.matmul(ps_pr[:, 0:TOTC], ones_r[:, :], grow[:, :],
                                               start=True, stop=True), "mm")
            pr.op("tensor", lambda t: t.matmul(ps_pr2[:, 0:TOTC], ones_r[:, :], brow[:, :],
                                               start=True, stop=True), "mm")
            pr.w("vector", "mm")
            pr.op("vector", lambda v: v.tensor_copy(gB[:, :], ps_pr[:, 0:TOTC]))
            pr.op("vector", lambda v: v.tensor_copy(bB[:, :], ps_pr2[:, 0:TOTC]), "ev")
            pr.block()
        elif layer == 1 and skip_tables:
            pass
        elif layer == 0:
            th_sb = sbuf("th_sb", [8, 1])
            nv8 = sbuf("nv8", [8, QT])
            nv8b = sbuf("nv8b", [8, QT])
            ind8 = sbuf("ind8", [8, QT])
            glc = [sbuf("glc%d" % i, [1, QT], BF16) for i in range(2)]
            gl_lin = nc.dram_tensor(pfx + "gl_lin", [GW], BF16)
            pr.op("gpsimd", lambda g: g.dma_start(
                out=th_sb[:, :], in_=bass.AP(theta, 0, [[1, 8], [1, 1]])), "ld", 16)
            pr.op("gpsimd", lambda g: g.dma_start(
                out=grow[:, :], in_=bass.AP(gam, 0, [[1, 1], [1, TOTC]])), "ld", 16)
            pr.op("gpsimd", lambda g: g.dma_start(
                out=brow[:, :], in_=bass.AP(bet, 0, [[1, 1], [1, TOTC]])), "ld", 16)
            nch = (GW + QT - 1) // QT
            tbl_mm = [0]

            def mk_chunk(c):
                def go():
                    w = min(QT, GW - c * QT)
                    pr.w("gpsimd", "mm", tbl_mm[0])
                    pr.op("gpsimd", lambda g, cc=c, ww=w: g.iota(
                        nv8[:, 0:ww], [[1, ww]], base=-OFF + cc * QT, channel_multiplier=0,
                        allow_small_or_imprecise_dtypes=True), "gx")
                    pr.w("vector", "ld")
                    pr.w("vector", "gx")
                    pr.op("vector", lambda v, ww=w: v.tensor_scalar(
                        ind8[:, 0:ww], nv8[:, 0:ww], 0.0, None, op0=ALU.is_ge))
                    pr.op("vector", lambda v, ww=w: v.tensor_scalar(
                        nv8b[:, 0:ww], nv8[:, 0:ww], 0.0, th_sb[:, :],
                        op0=ALU.max, op1=ALU.mult), "ev")
                    pr.w("scalar", "ev")
                    pr.op("scalar", lambda s, ww=w: s.activation(
                        nv8b[:, 0:ww], nv8b[:, 0:ww], AF.Exp, scale=-1.0), "ac")
                    pr.w("vector", "ac")
                    pr.op("vector", lambda v, ww=w: v.tensor_tensor(
                        nv8b[:, 0:ww], nv8b[:, 0:ww], ind8[:, 0:ww], op=ALU.mult), "ev")
                    pr.w("tensor", "ev")
                    pr.w("tensor", "ac")
                    pr.op("tensor", lambda t, ww=w: t.matmul(
                        ps_pr5[0:1, 0:ww], ones_c8[:, :], nv8b[:, 0:ww],
                        start=True, stop=True), "mm")
                    tbl_mm[0] = C["mm"]
                    pr.w("scalar", "mm")
                    pr.w("scalar", "glw", B0["glw"] + c * 16)
                    pr.op("scalar", lambda s, cc=c, ww=w: s.activation(
                        glc[cc % 2][:, 0:ww], ps_pr5[0:1, 0:ww], AF.Copy,
                        scale=1.0 / (8.0 * math.sqrt(HD))), "ac")
                    pr.w("sync", "ac")
                    pr.op("sync", lambda sy, cc=c, ww=w: sy.dma_start(
                        out=bass.AP(gl_lin, cc * QT, [[1, 1], [1, ww]]),
                        in_=glc[cc % 2][:, 0:ww]), "glw", 16)
                return go

            def tbl_finish():
                pr.w("gpsimd", "glw")
                pr.w("sync", "glw")
                for p in range(P):
                    pr.op("sync" if p % 2 else "gpsimd",
                          lambda e, pp=p: e.dma_start(
                              out=bass.AP(g2, pp * GW2 + pp, [[1, 1], [1, GW]]),
                              in_=bass.AP(gl_lin, 0, [[1, 1], [1, GW]])), "g2s", 16)
                pr.w("gpsimd", "g2s")
                pr.op("gpsimd", lambda g: g.nop())
                if shared is not None:
                    shared["g2"] = g2
                    shared["built"] = True
                pr.w("tensor", "ld")
                pr.w("tensor", "ac")
                pr.op("tensor", lambda t: t.matmul(ps_pr5[:, 0:TOTC], ones_r[:, :],
                                                   grow[:, :], start=True, stop=True), "mm")
                pr.op("tensor", lambda t: t.matmul(ps_pr6[:, 0:TOTC], ones_r[:, :],
                                                   brow[:, :], start=True, stop=True), "mm")
                pr.w("vector", "mm")
                pr.op("vector", lambda v: v.tensor_copy(gB[:, :], ps_pr5[:, 0:TOTC]))
                pr.op("vector", lambda v: v.tensor_copy(bB[:, :], ps_pr6[:, 0:TOTC]), "ev")

            table_steps = [mk_chunk(c) for c in range(nch)] + [tbl_finish]
        else:
            t0 = ExitStack()
            inv_sb = t0.enter_context(nc.sbuf_tensor(pfx + "inv_sb", [1, 32], F32))
            ang = t0.enter_context(nc.sbuf_tensor(pfx + "ang", [P, 2048], F32))
            angk = t0.enter_context(nc.sbuf_tensor(pfx + "angk", [P, 2048], F32))
            with t0:
                pr.op("gpsimd", lambda g: g.memset(halfpi[:, :], math.pi / 2))
                pr.op("gpsimd", lambda g: g.iota(inv_sb[:, :], [[1, 32]], base=0,
                                                 channel_multiplier=0,
                                                 allow_small_or_imprecise_dtypes=True))
                pr.op("gpsimd", lambda g: g.iota(ang[:, :], [[128, 64], [0, 32]], base=0,
                                                 channel_multiplier=1,
                                                 allow_small_or_imprecise_dtypes=True))
                pr.block()
                pr.op("scalar", lambda s: s.activation(inv_sb[:, :], inv_sb[:, :], AF.Exp,
                                                       scale=-math.log(10000.0) / 32.0))
                pr.block()
                pr.op("tensor", lambda t: t.matmul(ps_pr[:, 0:32], ones_r[:, :], inv_sb[:, :],
                                                   start=True, stop=True), "mm")
                pr.w("vector", "mm")
                for kc in range(64):
                    pr.op("vector", lambda v, k=kc: v.tensor_copy(
                        angk[:, k * 32:(k + 1) * 32], ps_pr[:, 0:32]))
                pr.op("vector", lambda v: v.nop(), "ev")
                pr.block()
                CBIG = 12582912.0
                pr.op("vector", lambda v: v.tensor_tensor(ang[:, :], ang[:, :], angk[:, :],
                                                          op=ALU.mult), "ev")
                pr.w("vector", "ev")
                pr.op("vector", lambda v: v.tensor_scalar(angk[:, :], ang[:, :],
                                                          1.0 / (2 * math.pi), CBIG,
                                                          op0=ALU.mult, op1=ALU.add), "ev")
                pr.w("vector", "ev")
                pr.op("vector", lambda v: v.tensor_scalar(angk[:, :], angk[:, :], CBIG, None,
                                                          op0=ALU.subtract), "ev")
                pr.w("vector", "ev")
                pr.op("vector", lambda v: v.tensor_scalar(angk[:, :], angk[:, :],
                                                          -2 * math.pi, None, op0=ALU.mult), "ev")
                pr.w("vector", "ev")
                pr.op("vector", lambda v: v.tensor_tensor(ang[:, :], ang[:, :], angk[:, :],
                                                          op=ALU.add))
                pr.block()
                pr.op("scalar", lambda s: s.activation(sn_sb[:, :], ang[:, :], AF.Sin))
                pr.op("scalar", lambda s: s.activation(angk[:, :], ang[:, :], AF.Abs))
                pr.op("scalar", lambda s: s.activation(cs_sb[:, :], angk[:, :], AF.Sin,
                                                       scale=-1.0, bias=halfpi[:, :]))
                pr.block()
                if shared is not None:
                    shared["built"] = True

        mm_b = C["mm"]
        ev_done = {}
        evac1_of = {}
        mm_done = {}
        pr.w("gpsimd", "lw")
        if xgate is not None:
            pr.w("gpsimd", xgate[0], xgate[1])
        part2 = []
        rci = [0]
        cp_of = {}
        rv_of = {}
        bw_of = {}
        st_of = {}
        vev_of = {}
        vci = [0]
        fw_of = {}

        def flush_part2(n):
            while len(part2) > n:
                part2.pop(0)()
        order = list(range(NQT))
        if xgate2 is not None:
            order = ([t for t in range(NQT) if t % 4 < 2]
                     + [t for t in range(NQT) if t % 4 >= 2])
        for idx, tc in enumerate(order):
            if xgate2 is not None and idx == NQT // 2:
                pr.w("gpsimd", xgate2[0], xgate2[1])
            if idx >= 2:
                pr.w("gpsimd", "mm", mm_done[(idx - 2, "xfree")])
            lsem = "ld%d" % (idx % 2)
            for dc in range(8):
                pr.op("gpsimd", lambda g, t=tc, d=dc, ix=idx: g.dma_start(
                    out=xbuf[ix % 2][:, d * QT:(d + 1) * QT], in_=xchunk(d, t)), lsem, 16)
            for h in range(H):
                for mi in range(3):
                    grp = idx * H * 3 + h * 3 + mi
                    bank = (ps_pr, ps_pr4)[grp % 2]
                    pr.w("tensor", "ld%d" % (idx % 2),
                         B0["ld%d" % (idx % 2)] + (idx // 2 + 1) * 128)
                    if grp >= 2:
                        pr.w("tensor", "ev", evac1_of.get(grp - 2, 0))
                    w_sb = (wq_sb, wk_sb, wv_sb)[mi]

                    def projmm(t, h=h, mi=mi, ix=idx, w_sb=w_sb, bank=bank):
                        r = None
                        for dc in range(8):
                            r = t.matmul(bank[0:HD, :],
                                         w_sb[:, dc * TOTC + h * HD: dc * TOTC + (h + 1) * HD],
                                         xbuf[ix % 2][:, dc * QT:(dc + 1) * QT],
                                         start=(dc == 0), stop=(dc == 7))
                        return r
                    pr.op("tensor", projmm, "mm")
                    mmv = C["mm"]
                    col0 = (h * T if layer == 0 else (h // 2) * T) + tc * QT
                    pb = 0 if layer == 0 else (h % 2) * 64
                    if mi < 2 and layer == 0:
                        dst = (qT_sb, kT_sb)[mi]
                        pr.w("vector", "mm", mmv)
                        pr.op("vector", lambda v, d=dst, c=col0, bank=bank: v.tensor_copy(
                            d[0:HD, c:c + QT], bank[0:HD, :]), "ev")
                        ev_done[grp] = C["ev"]
                        evac1_of[grp] = C["ev"]
                    elif mi == 2:
                        pr.w("vector", "mm", mmv)
                        pr.op("vector", lambda v, g2_=grp, bank=bank: v.tensor_copy(
                            vtmp[g2_ % 2][0:HD, :], bank[0:HD, :]), "ev")
                        evv = C["ev"]
                        evac1_of[grp] = evv

                        vc = vci[0]
                        vci[0] += 1

                        def d_vchain(grp=grp, tc=tc, h=h, evv=evv, vc=vc):
                            pr.w("tensor", "ev", evv)
                            if vc >= 1:
                                pr.w("tensor", "ev", vev_of.get(vc - 1, 0))

                            def vtr(t, grp=grp):
                                r = None
                                for stt in range(4):
                                    if layer == 0:
                                        r = t.transpose(ps_vtb[:, stt * 128:(stt + 1) * 128],
                                                        vtmp[grp % 2][:, stt * 128:(stt + 1) * 128],
                                                        identb[:, :])
                                    else:
                                        r = t.transpose(ps_vtb[:, stt * 64:(stt + 1) * 64],
                                                        vtmp[grp % 2][0:64, stt * 128:(stt + 1) * 128],
                                                        identb[0:64, 0:64])
                                return r
                            pr.op("tensor", vtr, "mm")
                            pr.w("vector", "mm", C["mm"])

                            def vev(v, tc=tc, h=h):
                                r = None
                                for stt in range(4):
                                    kc = tc * 4 + stt
                                    if layer == 0:
                                        v.tensor_copy(v1[:, h * 4160 + kc * 65: h * 4160 + kc * 65 + 64],
                                                      ps_vtb[:, stt * 128:stt * 128 + 64])
                                        r = v.tensor_copy(v2[:, h * 4096 + kc * 64: h * 4096 + (kc + 1) * 64],
                                                          ps_vtb[:, stt * 128 + 64:(stt + 1) * 128])
                                    else:
                                        r = v.tensor_copy(v1[:, h * 4160 + kc * 65: h * 4160 + kc * 65 + 64],
                                                          ps_vtb[:, stt * 64:(stt + 1) * 64])
                                return r
                            pr.op("vector", vev, "ev")
                            ev_done[grp] = C["ev"]
                            vev_of[vc] = C["ev"]
                        part2.append(d_vchain)
                        flush_part2(2)
                    else:
                        pr.w("vector", "mm", mmv)
                        rc = rci[0]
                        rci[0] += 1
                        if rc >= 4:
                            pr.w("vector", "mm", fw_of.get(rc - 4, 0))
                        pr.op("vector", lambda v, rc=rc, bank=bank: v.tensor_copy(
                            ev_sb[rc % 4][:, :], bank[0:64, :]), "ev")
                        evac1_of[grp] = C["ev"]
                        evv = C["ev"]

                        def d_rchain(grp=grp, tc=tc, h=h, mi=mi, pb=pb, col0=col0, evv=evv,
                                     rc=rc):
                            rp = rc % 2
                            pfw = (ps_pr2, ps_pr5)[rp]
                            pbw = (ps_pr3, ps_pr6)[rp]
                            ro = rp * 256
                            rh = rp * 128
                            pr.w("tensor", "ev", evv)

                            if rc >= 2:
                                pr.w("tensor", "ac", cp_of.get(rc - 2, 0))

                            def ropefwd(t, rc=rc, pfw=pfw):
                                r = None
                                for stt in range(4):
                                    r = t.transpose(pfw[:, stt * 64:(stt + 1) * 64],
                                                    ev_sb[rc % 4][:, stt * 128:(stt + 1) * 128],
                                                    ident[0:64, 0:64])
                                return r
                            pr.op("tensor", ropefwd, "mm")
                            fw_of[rc] = C["mm"]
                            pr.w("scalar", "mm", C["mm"])
                            if rc >= 2:
                                pr.w("scalar", "ev", rv_of.get(rc - 2, 0))
                            pr.op("scalar", lambda s, pfw=pfw, ro=ro: s.copy(
                                rtm[:, ro:ro + 256], pfw[:, 0:256]), "ac")
                            cp_of[rc] = C["ac"]
                            pr.w("vector", "ac")
                            if rc >= 2:
                                pr.w("vector", "mm", bw_of.get(rc - 2, 0))

                            def ropev(v, tc=tc, ro=ro, rh=rh):
                                ecs = cs_sb[:, tc * 128:(tc + 1) * 128]
                                esn = sn_sb[:, tc * 128:(tc + 1) * 128]
                                ev_ = rtm[:, ro:ro + 256:2]
                                od_ = rtm[:, ro + 1:ro + 256:2]
                                v.tensor_tensor(rtm2[:, ro:ro + 256:2], ev_, ecs, op=ALU.mult)
                                v.tensor_tensor(rsc[:, rh:rh + 128], od_, esn, op=ALU.mult)
                                v.tensor_tensor(rtm2[:, ro + 1:ro + 256:2], ev_, esn, op=ALU.mult)
                                v.tensor_tensor(rsc2[:, rh:rh + 128], od_, ecs, op=ALU.mult)
                                v.tensor_tensor(rtm2[:, ro:ro + 256:2], rtm2[:, ro:ro + 256:2],
                                                rsc[:, rh:rh + 128], op=ALU.subtract)
                                return v.tensor_tensor(rtm2[:, ro + 1:ro + 256:2],
                                                       rtm2[:, ro + 1:ro + 256:2],
                                                       rsc2[:, rh:rh + 128], op=ALU.add)
                            pr.op("vector", ropev, "ev")
                            rv_of[rc] = C["ev"]
                            rv = C["ev"]

                            def d_rtail(grp=grp, mi=mi, pb=pb, col0=col0, pbw=pbw, ro=ro,
                                        rv=rv, rc=rc):
                                pr.w("tensor", "ev", rv)
                                if rc >= 2:
                                    pr.w("tensor", "ev", st_of.get(rc - 2, 0))

                                def ropebwd(t, pbw=pbw, ro=ro):
                                    r = None
                                    for stt in range(4):
                                        r = t.transpose(pbw[0:64, stt * 128:(stt + 1) * 128],
                                                        rtm2[:, ro + stt * 64:ro + (stt + 1) * 64],
                                                        ident[:, :])
                                    return r
                                pr.op("tensor", ropebwd, "mm")
                                bw_of[rc] = C["mm"]
                                pr.w("vector", "mm", C["mm"])
                                dst = (qT_sb, kT_sb)[mi]
                                pr.op("vector", lambda v, dst=dst, pb=pb, col0=col0, pbw=pbw:
                                      v.tensor_copy(dst[pb:pb + 64, col0:col0 + QT],
                                                    pbw[0:64, 0:QT]), "ev")
                                ev_done[grp] = C["ev"]
                                st_of[rc] = C["ev"]
                            part2.append(d_rtail)
                        part2.append(d_rchain)
                        flush_part2(2)
            mm_done[(idx, "xfree")] = C["mm"]
            if table_steps:
                table_steps.pop(0)()
        flush_part2(0)
        while table_steps:
            table_steps.pop(0)()
        pr.block()

        for h in range(H):
            for qk in range(2):
                src = (qT_sb, kT_sb)[qk]
                col0 = h * T if layer == 0 else (h // 2) * T
                pb = 0 if layer == 0 else (h % 2) * 64
                for c in range(NQT):
                    pr.w("scalar", "mm")
                    pr.op("scalar", lambda s, src=src, pb=pb, col0=col0, c=c: s.activation(
                        sq_scr[0:HD, :], src[pb:pb + HD, col0 + c * QT: col0 + (c + 1) * QT],
                        AF.Square), "ac")
                    pr.w("tensor", "ac")
                    pr.w("tensor", "ev")
                    pr.op("tensor", lambda t: t.matmul(ps_pr[0:1, :], ones_cb[0:HD, :],
                                                       sq_scr[0:HD, :], start=True, stop=True),
                          "mm")
                    pr.w("vector", "mm")
                    pr.op("vector", lambda v, c=c: v.tensor_reduce(
                        stat[:, c:c + 1], ps_pr[0:1, :], axis=AX.X, op=ALU.max), "ev")
                pr.w("vector", "ev")
                pr.op("vector", lambda v, h=h, qk=qk: v.tensor_reduce(
                    stat[:, 32 + h * 2 + qk: 33 + h * 2 + qk], stat[:, 0:NQT],
                    axis=AX.X, op=ALU.max), "ev")
        pr.block()
        sc = 1.0 / math.sqrt(HD) if layer == 0 else 1.0 / 8.0
        pr.w("vector", "ev")
        for h in range(H):
            pr.op("vector", lambda v, h=h: v.tensor_tensor(
                stat[:, 40 + h:41 + h], stat[:, 32 + 2 * h:33 + 2 * h],
                stat[:, 33 + 2 * h:34 + 2 * h], op=ALU.mult), "ev")
        pr.w("scalar", "ev")
        pr.op("scalar", lambda s: s.activation(stat[:, 48:48 + H], stat[:, 40:40 + H],
                                               AF.Ln), "ac")
        pr.w("scalar", "ac")
        pr.op("scalar", lambda s: s.activation(stat[:, 48:48 + H], stat[:, 48:48 + H],
                                               AF.Exp, scale=0.5), "ac")
        pr.w("vector", "ac")
        pr.op("vector", lambda v: v.tensor_scalar(stat[:, 48:48 + H], stat[:, 48:48 + H],
                                                  -sc, None, op0=ALU.mult), "ev")
        pr.w("tensor", "ev")
        for h in range(H):
            pr.op("tensor", lambda t, h=h: t.matmul(ps_pr[:, h:h + 1], ones_r[:, :],
                                                    stat[:, 48 + h:49 + h],
                                                    start=True, stop=True), "mm")
        pr.w("vector", "mm")
        pr.op("vector", lambda v: v.tensor_copy(bias_sb[:, :], ps_pr[:, 0:H]), "ev")
        pr.block()
        ph_es.close()

        fl = ExitStack()
        ps_s = [fl.enter_context(nc.psum_tensor(pfx + "ps_s%d" % i, [P, QT], F32)) for i in range(4)]
        ps_oa = [fl.enter_context(nc.psum_tensor(pfx + "ps_oa%d" % i, [65, QT], F32))
                 for i in range(H)]
        ps_obm = (fl.enter_context(nc.psum_tensor(pfx + "ps_obm", [P, QT], F32))
                  if layer == 0 else None)
        ps_ot = fl.enter_context(nc.psum_tensor(pfx + "ps_ot", [P, QT], BF16))
        ps_ep = fl.enter_context(nc.psum_tensor(pfx + "ps_ep", [P, QT], F32))
        stg = [es.enter_context(nc.sbuf_tensor(pfx + "stg%d" % i, [P, QT], BF16))
               for i in range(2)]
        with fl:
            si = 0
            cci = 0
            mkc = 0
            dv_of = {}
            px_of = {}
            ot_store = {0: C["mm"], 1: C["mm"]}
            stg_store = {0: C["so0"], 1: C["so1"]}
            stg_evl = [0]
            scnt = [0]
            oaev = {}
            obac = {}
            tr1_of = {}
            epf_free = [0]
            prev_unit = []
            ep_units = []
            pend = []

            def emit_tr2_prev(keep=1):
                while len(prev_unit) > keep:
                    emit_tr2_one()

            def emit_tr2_one():
                uqt, h, stt, fin = prev_unit.pop(0)
                pr.w("tensor", "ev", fin)
                if stt == 0:
                    pr.w("tensor", "ev", stg_evl[0])
                pr.op("tensor", lambda t, stt=stt: t.transpose(
                    ps_ot[0:HD, stt * 128:(stt + 1) * 128],
                    otile[stt % 2][:, :], identb[:, :]), "mm")
                ot_store[stt % 2] = C["mm"]
                if stt == 3:
                    scnt[0] += 1
                    par = scnt[0] % 2
                    ssem = "so%d" % par
                    pr.w("vector", "mm", C["mm"])
                    pr.w("vector", ssem, stg_store[par])
                    pr.op("vector", lambda v, par=par: v.tensor_copy(
                        stg[par][0:HD, :], ps_ot[0:HD, :]), "ev")
                    stg_evl[0] = C["ev"]
                    pr.w("sync", "ev", stg_evl[0])
                    pr.op("sync", lambda sy, uqt=uqt, h=h, par=par: sy.dma_start(
                        out=odst(h, uqt), in_=stg[par][0:HD, :]), ssem, 16)
                    stg_store[par] = C[ssem]

            def emit_unit(uqt, h, stt):
                emit_tr2_prev()
                pr.w("tensor", "ev", oaev[h])
                if layer == 0:
                    pr.w("tensor", "ac", obac[h])
                pr.w("tensor", "ev", epf_free[0])

                def tr1(t, h=h, stt=stt):
                    r = t.transpose(ps_ep[:, 0:65],
                                    oa_sbs[h][:, stt * 128:(stt + 1) * 128],
                                    ident[0:65, 0:65])
                    if layer == 0:
                        r = t.transpose(ps_ep[:, 65:129],
                                        ob_sbs[h][:, stt * 128:(stt + 1) * 128],
                                        ident[0:64, 0:64])
                    return r
                pr.op("tensor", tr1, "mm")
                tr1_of[h] = C["mm"]
                pr.w("vector", "mm", C["mm"])
                pr.op("vector", lambda v: v.reciprocal(linv[:, :], ps_ep[:, 64:65]), "ev")
                pr.w("vector", "ev")
                pr.w("vector", "mm", ot_store[stt % 2])
                if layer == 0:
                    pr.op("vector", lambda v: v.tensor_scalar(
                        ot_f[:, 0:64], ps_ep[:, 0:64],
                        linv[:, :], None, op0=ALU.mult))
                    pr.op("vector", lambda v: v.tensor_scalar(
                        ot_f[:, 64:128], ps_ep[:, 65:129],
                        linv[:, :], None, op0=ALU.mult), "ev")
                    epf_free[0] = C["ev"]
                    pr.w("vector", "ev")
                    pr.op("vector", lambda v: v.tensor_reduce(
                        gnmu[:, :], ot_f[:, :], axis=AX.X, op=ALU.add), "ev")
                    pr.w("scalar", "ev")
                    pr.op("scalar", lambda s: s.activation(
                        gn_scr[:, :], ot_f[:, :], AF.Square, accum_out=gnsq[:, :]), "ac")
                    pr.w("vector", "ac")
                    pr.op("vector", lambda v: v.tensor_scalar(
                        gnms[:, :], gnmu[:, :], 1.0 / 128.0, None, op0=ALU.mult))
                    pr.op("vector", lambda v: v.tensor_scalar(
                        gnsq[:, :], gnsq[:, :], 1.0 / 128.0, 1e-5,
                        op0=ALU.mult, op1=ALU.add), "ev")
                    pr.w("vector", "ev")
                    pr.op("vector", lambda v: v.tensor_tensor(
                        gnvar[:, :], gnms[:, :], gnms[:, :], op=ALU.mult), "ev")
                    pr.w("vector", "ev")
                    pr.op("vector", lambda v: v.tensor_tensor(
                        gnvar[:, :], gnsq[:, :], gnvar[:, :], op=ALU.subtract), "ev")
                    pr.w("scalar", "ev")
                    pr.op("scalar", lambda s: s.activation(gnrs[:, :], gnvar[:, :],
                                                           AF.Ln), "ac")
                    pr.w("scalar", "ac")
                    pr.op("scalar", lambda s: s.activation(gnrs[:, :], gnrs[:, :],
                                                           AF.Exp, scale=-0.5), "ac")
                    pr.w("vector", "ac")
                    pr.op("vector", lambda v, h=h: v.tensor_scalar(
                        rg_sb[:, :], gB[:, h * 128:(h + 1) * 128], gnrs[:, :],
                        None, op0=ALU.mult))
                    pr.op("vector", lambda v: v.tensor_scalar(
                        gn_f2[:, :], ot_f[:, :], gnms[:, :], None,
                        op0=ALU.subtract), "ev")
                    pr.w("vector", "ev")
                    pr.op("vector", lambda v: v.tensor_tensor(
                        ot_f[:, :], gn_f2[:, :], rg_sb[:, :], op=ALU.mult), "ev")
                    pr.w("vector", "ev")
                    pr.op("vector", lambda v, h=h, stt=stt: v.tensor_tensor(
                        otile[stt % 2][:, :], ot_f[:, :], bB[:, h * 128:(h + 1) * 128],
                        op=ALU.add), "ev")
                else:
                    pr.op("vector", lambda v, stt=stt: v.tensor_scalar(
                        otile[stt % 2][:, :], ps_ep[:, 0:64],
                        linv[:, :], None, op0=ALU.mult), "ev")
                    epf_free[0] = C["ev"]
                prev_unit.append((uqt, h, stt, C["ev"]))

            def emit_av(args):
                qt, kc, h, s_i, dmask, first, last = args
                pr.w("tensor", "px", B0["px"] + s_i + 1)
                if dmask is not None:
                    pr.w("tensor", "mk", dmask)
                if first:
                    pr.w("tensor", "ev", oaev.get(h, 0))
                    if layer == 0:
                        pr.w("tensor", "ac", obac.get(H - 1, 0))

                def av(t, qt=qt, kc=kc, h=h, s_i=s_i, first=first, last=last):
                    r = t.matmul(ps_oa[h][:, :], v1[:, h * 4160 + kc * 65: h * 4160 + (kc + 1) * 65],
                                 pt_sb[s_i % 4][:, :], start=first, stop=last)
                    if layer == 0:
                        r = t.matmul(ps_obm[h * 64:(h + 1) * 64, :],
                                     v2[:, h * 4096 + kc * 64: h * 4096 + (kc + 1) * 64],
                                     pt_sb[s_i % 4][:, :], start=first, stop=last)
                    return r
                pr.op("tensor", av, "av")

            qt_iter = list(range(NQT)) if flash_qts is None else list(flash_qts)
            for qt in qt_iter:
                KC = 4 * qt + 4
                for kc in range(KC):
                    if layer == 0:
                        if cci >= 4:
                            pr.w("sync", "dv", B0["dv"] + (cci - 3) * H)
                        delta = qt * QT - kc * 128
                        pr.op("sync", lambda sy, cci=cci, delta=delta: sy.dma_start(
                            out=dt_sb[cci % 4][:, :],
                            in_=bass.AP(g2, OFF + delta, [[GW2, P], [1, QT]])),
                            "dts%d" % (cci % 4), 16)
                    for h in range(H):
                        first = (kc == 0)
                        last = (kc == KC - 1)
                        diag = (kc * 128 + 127 > qt * QT)
                        if si >= 4:
                            if layer == 0:
                                pr.w("tensor", "dv", dv_of[si - 4])
                            else:
                                pr.w("tensor", "px", px_of[si - 4])

                        def smm(t, qt=qt, kc=kc, h=h, s_i=si):
                            if layer == 0:
                                return t.matmul(ps_s[s_i % 4][:, :],
                                                kT_sb[:, h * T + kc * 128: h * T + (kc + 1) * 128],
                                                qT_sb[:, h * T + qt * QT: h * T + (qt + 1) * QT],
                                                start=True, stop=True)
                            pb = (h % 2) * 64
                            cb = (h // 2) * T
                            return t.matmul(ps_s[s_i % 4][:, :],
                                            kT_sb[pb:pb + 64, cb + kc * 128: cb + (kc + 1) * 128],
                                            qT_sb[pb:pb + 64, cb + qt * QT: cb + (qt + 1) * QT],
                                            start=True, stop=True)
                        pr.op("tensor", smm, "sm")
                        if layer == 0:
                            pr.w("vector", "sm")
                            pr.w("vector", "dts%d" % (cci % 4),
                                 B0["dts%d" % (cci % 4)] + 16 * (cci // 4 + 1))
                            if si >= 4:
                                pr.w("vector", "px", px_of[si - 4])

                            def dmul(v, s_i=si, cci=cci):
                                return v.tensor_tensor(sp_sb[s_i % 4][:, :],
                                                       ps_s[s_i % 4][:, :],
                                                       dt_sb[cci % 4][:, :], op=ALU.mult)
                            pr.op("vector", dmul, "dv")
                            dv_of[si] = C["dv"]
                        if layer == 0:
                            pr.w("scalar", "dv", dv_of[si])
                        else:
                            pr.w("scalar", "sm")
                        if si >= 4:
                            pr.w("scalar", "av", B0["av"] + si - 3)

                        def expo(s, s_i=si, h=h):
                            src = sp_sb[s_i % 4][:, :] if layer == 0 else ps_s[s_i % 4][:, :]
                            scl = 1.0 if layer == 0 else 0.125
                            return s.activation(pt_sb[s_i % 4][:, :], src, AF.Exp,
                                                bias=bias_sb[:, h:h + 1], scale=scl)
                        pr.op("scalar", expo, "px")
                        px_of[si] = C["px"]
                        dmask = None
                        if diag:
                            pr.w("vector", "px", px_of[si])
                            bi = (kc * 128 - qt * QT) // 128
                            pr.op("vector", lambda v, s_i=si, bi=bi: v.tensor_tensor(
                                pt_sb[s_i % 4][:, :], pt_sb[s_i % 4][:, :],
                                binm[:, bi * QT:(bi + 1) * QT], op=ALU.mult), "mk")
                            mkc += 1
                            dmask = C["mk"]
                        pend.append((qt, kc, h, si, dmask, first, last))
                        si += 1
                    while len(pend) > 3:
                        emit_av(pend.pop(0))
                    cci += 1
                    if ep_units and (layer == 1 or kc % 2 == 1
                                     or len(ep_units) >= KC - 1 - kc):
                        emit_unit(*ep_units.pop(0))
                        if ep_units and len(ep_units) >= KC - 1 - kc:
                            emit_unit(*ep_units.pop(0))
                while pend:
                    emit_av(pend.pop(0))
                while ep_units:
                    emit_unit(*ep_units.pop(0))
                avf = C["av"]
                for h in range(H):
                    pr.w("vector", "av", avf)
                    pr.w("vector", "mm", tr1_of.get(h, 0))
                    pr.op("vector", lambda v, h=h: v.tensor_copy(
                        oa_sbs[h][:, :], ps_oa[h][:, :]), "ev")
                    oaev[h] = C["ev"]
                    if layer == 0:
                        pr.w("scalar", "av", avf)
                        pr.w("scalar", "mm", tr1_of.get(h, 0))
                        pr.op("scalar", lambda s, h=h: s.copy(
                            ob_sbs[h][:, :], ps_obm[h * 64:(h + 1) * 64, :]), "ac")
                        obac[h] = C["ac"]
                ep_units = [(qt, h, stt) for h in range(H) for stt in range(4)]
            while ep_units:
                emit_unit(*ep_units.pop(0))
            emit_tr2_prev(keep=0)
            pr.w("gpsimd", "so0")
            pr.w("gpsimd", "so1")
            pr.op("gpsimd", lambda g: g.nop())
            pr.block()


def emit_mix(nc, pr, layer, pfx, ochunk, xT, yT, W, ybf=None, ag_hook=None):
    E = 6 if layer == 0 else 8
    NE = 7 if layer == 0 else 8
    TOPK = 2 if layer == 0 else 4
    IC_UP = 8 if layer == 0 else 16
    NTH = 1 if layer == 0 else 2
    IC_DN_ = 8 if layer == 0 else 16
    THT = NT // NTH
    NTC = THT // QT
    wo, wob, gw, gb = W["wo"], W["wob"], W["gw"], W["gb"]
    if layer == 0:
        w1t, w3t, w2t = W["w1t"], W["w3t"], W["w2t"]
    else:
        gupt, gupb, dnt, dnb = W["gupt"], W["gupb"], W["dnt"], W["dnb"]

    es = ExitStack()

    def sbuf(name, shape, dt=F32):
        return es.enter_context(nc.sbuf_tensor(pfx + name, shape, dt))

    ident = sbuf("ident", [P, P])
    ones_r = sbuf("ones_r", [1, P])
    ones_rq = sbuf("ones_rq", [1, QT])
    ones_cb = sbuf("ones_cb", [P, 1], BF16)
    x_sb = sbuf("x_sb", [P, 8 * NT])
    xb_sb = sbuf("xb_sb", [P, 8 * NT], BF16)
    gw_sb = sbuf("gw_sb", [P, 8 * E], BF16)
    gb_sb = sbuf("gb_sb", [E, 1])
    ge_tm = sbuf("ge_tm", [P, 16 * E])
    lg_sb = sbuf("lg_sb", [E, NT])
    lt_sb = sbuf("lt_sb", [P, E])
    l2_sb = [sbuf("l2_sb%d" % i, [P, E]) for i in range(2)]
    sel_sb = sbuf("sel_sb", [P, E])
    mrow = [sbuf("mrow%d" % i, [P, 1]) for i in range(6)]
    sg_scr = sbuf("sg_scr", [P, 2 * QT], BF16)
    t1_scr = sbuf("t1_scr", [P, 2 * QT])
    t2_scr = sbuf("t2_scr", [P, 2 * QT])
    rb_scr = sbuf("rb_scr", [P, QT], BF16)
    sq_scr = sbuf("sq_scr", [P, QT], BF16)
    rowa = sbuf("rowa", [1, QT])
    rowb = sbuf("rowb", [1, QT])
    rowc = sbuf("rowc", [1, QT])

    C = pr.cnt
    B0 = dict(C)

    with es:
        ps_a = es.enter_context(nc.psum_tensor(pfx + "ps_a", [P, QT], F32))
        ps_b2 = es.enter_context(nc.psum_tensor(pfx + "ps_b2", [P, QT], F32))
        ps_r1 = es.enter_context(nc.psum_tensor(pfx + "ps_r1", [P, QT], F32))
        ps_r2 = es.enter_context(nc.psum_tensor(pfx + "ps_r2", [P, QT], F32))
        ps_c1 = es.enter_context(nc.psum_tensor(pfx + "ps_c1", [P, QT], F32))
        ps_c2 = es.enter_context(nc.psum_tensor(pfx + "ps_c2", [P, QT], F32))
        ps_t = es.enter_context(nc.psum_tensor(pfx + "ps_t", [P, QT], F32))
        ps_g = es.enter_context(nc.psum_tensor(pfx + "ps_g", [P, QT], F32))

        def g_memsets(g):
            g.memset(ident[:, :], 0.0)
            g.memset(ones_r[:, :], 1.0)
            g.memset(ones_rq[:, :], 1.0)
            return g.memset(ones_cb[:, :], 1.0)
        pr.op("gpsimd", g_memsets, "gx")
        pr.w("gpsimd", "gx")
        pr.op("gpsimd", lambda g: g.affine_select(
            out=ident[:, :], in_=ident[:, :], compare_op=ALU.not_equal,
            fill=1.0, base=0, pattern=[[-1, P]], channel_multiplier=1), "gx")
        o_es = ExitStack()
        wo_sb = o_es.enter_context(nc.sbuf_tensor(pfx + "wo_sb", [P, 8 * D], BF16))
        wob_sb = o_es.enter_context(nc.sbuf_tensor(pfx + "wob_sb", [1, D], F32))
        o_tc = [o_es.enter_context(nc.sbuf_tensor(pfx + "o_tc%d" % i, [P, 8 * QT], BF16))
                for i in range(2)]
        for dc in range(8):
            pr.op("gpsimd", lambda g, d=dc: g.dma_start(
                out=wo_sb[:, d * D:(d + 1) * D], in_=wo[d * P:(d + 1) * P, :]), "lw", 16)
            pr.op("gpsimd", lambda g, d=dc: g.dma_start(
                out=gw_sb[:, d * E:(d + 1) * E], in_=gw[d * P:(d + 1) * P, :]), "lw", 16)
        pr.op("sync", lambda sy: sy.dma_start(
            out=wob_sb[:, :], in_=bass.AP(wob, 0, [[1, 1], [1, D]])), "lx", 16)
        pr.op("sync", lambda sy: sy.dma_start(
            out=gb_sb[:, :], in_=bass.AP(gb, 0, [[1, E], [1, 1]])), "lx", 16)
        for dc in range(8):
            pr.op("sync", lambda sy, d=dc: sy.dma_start(
                out=x_sb[:, d * NT:(d + 1) * NT], in_=xT[d * P:(d + 1) * P, :]), "lx", 16)
        pr.block()

        wo_mmN = {}
        wo_ev = {}
        for tc in range(4):
            losem = "lo%d" % (tc % 2)
            if tc >= 2:
                pr.w("sync", "mm", wo_mmN[tc - 2])
            for dc in range(8):
                pr.op("sync", lambda sy, d=dc, tc=tc: sy.dma_start(
                    out=o_tc[tc % 2][:, d * QT:(d + 1) * QT], in_=ochunk(d, tc)), losem, 16)
            for oc in range(8):
                g_ = tc * 8 + oc
                bank = (ps_a, ps_b2)[g_ % 2]
                pr.w("tensor", "lw")
                pr.w("tensor", "lx")
                pr.w("tensor", losem, B0[losem] + (tc // 2 + 1) * 128)
                if wo_ev.get(g_ - 2, 0):
                    pr.w("tensor", "ev", wo_ev[g_ - 2])

                def womm(t, tc=tc, oc=oc, bank=bank):
                    for dc in range(8):
                        t.matmul(bank[:, :], wo_sb[:, dc * D + oc * P: dc * D + (oc + 1) * P],
                                 o_tc[tc % 2][:, dc * QT:(dc + 1) * QT],
                                 start=(dc == 0), stop=False)
                    return t.matmul(bank[:, :], wob_sb[:, oc * P:(oc + 1) * P],
                                    ones_rq[:, :], start=False, stop=True)
                pr.op("tensor", womm, "mm")
                pr.w("vector", "mm")
                pr.op("vector", lambda v, tc=tc, oc=oc, bank=bank: v.tensor_tensor(
                    x_sb[:, oc * NT + tc * QT: oc * NT + (tc + 1) * QT],
                    x_sb[:, oc * NT + tc * QT: oc * NT + (tc + 1) * QT],
                    bank[:, :], op=ALU.add), "ev")
                wo_ev[g_] = C["ev"]
            wo_mmN[tc] = C["mm"]
        pr.block()
        o_es.close()
        ge_row = sbuf("ge_row", [1, NT])
        geB = sbuf("geB", [P, NT])
        h_sb = sbuf("h_sb", [P, IC_UP * THT], BF16)
        wu1 = [sbuf("wu1_%d" % i, [P, 8 * P], BF16) for i in range(2)]
        wu3 = [sbuf("wu3_%d" % i, [P, 8 * P], BF16) for i in range(2)]
        wd2 = [sbuf("wd2_%d" % i, [P, IC_DN_ * P], BF16) for i in range(2)]
        if layer == 1:
            gub_pm = sbuf("gub_pm", [P, 32])
            gub_tmp = sbuf("gub_tmp", [32, P])
            dnb_sb = sbuf("dnb_sb", [E, D])
            cbT = sbuf("cbT", [8, QT])
            pr.op("sync", lambda sy: sy.dma_start(out=dnb_sb[:, :], in_=dnb[:, :]), "lx", 16)

        def emit_ln(tcs=(0, 1, 2, 3)):
            banks = {0: (ps_r1, ps_r2), 1: (ps_a, ps_b2)}
            rowfree = {0: C["ev"], 1: C["ev"]}
            rowsmm = {}
            bc_mm = [0]
            xrdy = C["ev"]

            def stats(tc):
                b1, b2 = banks[tc % 2]
                for dc in range(8):
                    pr.w("scalar", "ev", xrdy)
                    pr.w("scalar", "ac")
                    pr.w("scalar", "mm")
                    pr.op("scalar", lambda s, tc=tc, dc=dc: s.copy(
                        rb_scr[:, :],
                        x_sb[:, dc * NT + tc * QT: dc * NT + (tc + 1) * QT]), "ac")
                    pr.w("scalar", "ac")
                    pr.op("scalar", lambda s: s.activation(
                        sq_scr[:, :], rb_scr[:, :], AF.Square), "ac")
                    pr.w("tensor", "ac")
                    if dc == 0:
                        pr.w("tensor", "ev", rowfree[tc % 2])

                    def lnmm(t, dc=dc, b1=b1, b2=b2):
                        t.matmul(b1[0:1, :], ones_cb[:, :], rb_scr[:, :],
                                 start=(dc == 0), stop=(dc == 7))
                        return t.matmul(b2[0:1, :], ones_cb[:, :], sq_scr[:, :],
                                        start=(dc == 0), stop=(dc == 7))
                    pr.op("tensor", lnmm, "mm")
                rowsmm[tc] = C["mm"]

            def rows_and_norm(tc):
                b1, b2 = banks[tc % 2]
                pr.w("vector", "mm", rowsmm[tc])
                if bc_mm[0]:
                    pr.w("vector", "mm", bc_mm[0])
                pr.op("vector", lambda v, b1=b1: v.tensor_scalar(
                    rowa[:, :], b1[0:1, :], 1.0 / D, None, op0=ALU.mult), "ev")
                pr.w("vector", "ev")
                pr.op("vector", lambda v, b2=b2: v.tensor_scalar(
                    rowb[:, :], b2[0:1, :], 1.0 / D, 1e-5, op0=ALU.mult, op1=ALU.add),
                    "ev")
                rowfree[tc % 2] = C["ev"]
                pr.w("vector", "ev")
                pr.op("vector", lambda v: v.tensor_tensor(
                    rowc[:, :], rowa[:, :], rowa[:, :], op=ALU.mult), "ev")
                pr.w("vector", "ev")
                pr.op("vector", lambda v: v.tensor_tensor(
                    rowb[:, :], rowb[:, :], rowc[:, :], op=ALU.subtract), "ev")
                pr.w("scalar", "ev")
                pr.op("scalar", lambda s: s.activation(rowb[:, :], rowb[:, :], AF.Ln),
                      "ac")
                pr.w("scalar", "ac")
                pr.op("scalar", lambda s: s.activation(rowb[:, :], rowb[:, :], AF.Exp,
                                                       scale=-0.5), "ac")
                pr.w("vector", "ac")
                pr.op("vector", lambda v: v.tensor_tensor(
                    rowa[:, :], rowa[:, :], rowb[:, :], op=ALU.mult), "ev")
                pr.w("tensor", "ev")

                def bcmm(t):
                    t.matmul(ps_c1[:, :], ones_r[:, :], rowb[:, :], start=True, stop=True)
                    return t.matmul(ps_c2[:, :], ones_r[:, :], rowa[:, :],
                                    start=True, stop=True)
                pr.op("tensor", bcmm, "mm")
                bc_mm[0] = C["mm"]
                pr.w("vector", "mm")
                for dc in range(8):
                    pr.op("vector", lambda v, dc=dc, tc=tc: v.tensor_tensor(
                        t1_scr[:, 0:QT], x_sb[:, dc * NT + tc * QT: dc * NT + (tc + 1) * QT],
                        ps_c1[:, :], op=ALU.mult), "ev")
                    pr.w("vector", "ev")
                    pr.op("vector", lambda v, dc=dc, tc=tc: v.tensor_tensor(
                        x_sb[:, dc * NT + tc * QT: dc * NT + (tc + 1) * QT],
                        t1_scr[:, 0:QT], ps_c2[:, :], op=ALU.subtract), "ev")
                    pr.w("vector", "ev")
                    pr.op("vector", lambda v, dc=dc, tc=tc: v.tensor_copy(
                        xb_sb[:, dc * NT + tc * QT: dc * NT + (tc + 1) * QT],
                        x_sb[:, dc * NT + tc * QT: dc * NT + (tc + 1) * QT]), "ev")
                    pr.w("vector", "ev")

            prev = None
            for tc in tcs:
                stats(tc)
                if prev is not None:
                    rows_and_norm(prev)
                prev = tc
            rows_and_norm(prev)

        emit_ln()
        pr.block()

        ge_cols = {}
        for tc in range(4):
            pr.w("tensor", "ev")

            def gmm(t, tc=tc):
                r = None
                for dc in range(8):
                    r = t.matmul(ps_a[0:E, :], gw_sb[:, dc * E:(dc + 1) * E],
                                 xb_sb[:, dc * NT + tc * QT: dc * NT + (tc + 1) * QT],
                                 start=(dc == 0), stop=(dc == 7))
                return r
            pr.op("tensor", gmm, "mm")
            if layer == 0:
                pr.w("scalar", "mm")
                pr.w("scalar", "lx")
                pr.op("scalar", lambda s, tc=tc: s.activation(
                    lg_sb[:, tc * QT:(tc + 1) * QT], ps_a[0:E, :], AF.Sigmoid,
                    bias=gb_sb[:, :]), "ac")
                pr.w("tensor", "ac")
            else:
                pr.w("vector", "mm")
                pr.w("vector", "lx")
                pr.op("vector", lambda v, tc=tc: v.tensor_scalar(
                    lg_sb[:, tc * QT:(tc + 1) * QT], ps_a[0:E, :], gb_sb[:, :], None,
                    op0=ALU.add), "ev")
                pr.w("tensor", "ev")
        for tt in range(16):
            pr.w("tensor", "ev")

            def ltr(t, tt=tt):
                return t.transpose(ps_t[:, 0:E], lg_sb[:, tt * P:(tt + 1) * P],
                                   ident[0:E, 0:E])
            pr.op("tensor", ltr, "mm")
            pr.w("vector", "mm")
            pr.op("vector", lambda v: v.tensor_copy(lt_sb[:, :], ps_t[:, 0:E]), "ev")
            pr.w("vector", "ev")
            pr.op("vector", lambda v: v.tensor_reduce(
                mrow[0][:, :], lt_sb[:, :], axis=AX.X, op=ALU.max), "ev")
            pr.w("vector", "ev")
            nk = TOPK
            cur = lt_sb
            for j in range(nk - 1):
                nxt = l2_sb[j % 2]
                pr.op("vector", lambda v, cur=cur, j=j: v.tensor_scalar(
                    sel_sb[:, :], cur[:, :], mrow[j][:, :], -1.0e30,
                    op0=ALU.is_ge, op1=ALU.mult), "ev")
                pr.w("vector", "ev")
                pr.op("vector", lambda v, cur=cur, nxt=nxt: v.tensor_tensor(
                    nxt[:, :], cur[:, :], sel_sb[:, :], op=ALU.add), "ev")
                pr.w("vector", "ev")
                pr.op("vector", lambda v, nxt=nxt, j=j: v.tensor_reduce(
                    mrow[j + 1][:, :], nxt[:, :], axis=AX.X, op=ALU.max), "ev")
                pr.w("vector", "ev")
                cur = nxt
            pr.op("vector", lambda v, nk=nk: v.tensor_scalar(
                sel_sb[:, :], lt_sb[:, :], mrow[nk - 1][:, :], None, op0=ALU.is_ge), "ev")
            pr.w("vector", "ev")
            if layer == 0:
                pr.op("vector", lambda v: v.tensor_scalar(
                    mrow[4][:, :], mrow[0][:, :], mrow[1][:, :], 1e-8,
                    op0=ALU.add, op1=ALU.add), "ev")
                pr.w("vector", "ev")
                pr.op("vector", lambda v: v.reciprocal(mrow[5][:, :], mrow[4][:, :]), "ev")
                pr.w("vector", "ev")
                pr.op("vector", lambda v: v.tensor_tensor(
                    l2_sb[0][:, :], sel_sb[:, :], lt_sb[:, :], op=ALU.mult), "ev")
                pr.w("vector", "ev")
                pr.op("vector", lambda v, tt=tt: v.tensor_scalar(
                    ge_tm[:, tt * E:(tt + 1) * E], l2_sb[0][:, :], mrow[5][:, :], None,
                    op0=ALU.mult), "ev")
                pr.w("vector", "ev")
            else:
                pr.op("vector", lambda v: v.tensor_scalar(
                    l2_sb[0][:, :], lt_sb[:, :], mrow[0][:, :], None, op0=ALU.subtract),
                    "ev")
                pr.w("scalar", "ev")
                pr.op("scalar", lambda s: s.activation(
                    sg_scr[0:P, 0:E], l2_sb[0][:, :], AF.Exp), "ac")
                pr.w("vector", "ac")
                pr.op("vector", lambda v: v.tensor_tensor(
                    l2_sb[1][:, :], sel_sb[:, :], sg_scr[0:P, 0:E], op=ALU.mult), "ev")
                pr.w("vector", "ev")
                pr.op("vector", lambda v: v.tensor_reduce(
                    mrow[4][:, :], l2_sb[1][:, :], axis=AX.X, op=ALU.add), "ev")
                pr.w("vector", "ev")
                pr.op("vector", lambda v: v.reciprocal(mrow[5][:, :], mrow[4][:, :]), "ev")
                pr.w("vector", "ev")
                pr.op("vector", lambda v, tt=tt: v.tensor_scalar(
                    ge_tm[:, tt * E:(tt + 1) * E], l2_sb[1][:, :], mrow[5][:, :], None,
                    op0=ALU.mult), "ev")
                pr.w("vector", "ev")
        pr.block()


        IC_DN = 8 if layer == 0 else 16
        NDP = 2 if layer == 0 else 1
        h_w1 = w1t.tensor if layer == 0 else gupt.tensor
        h_w3 = w3t.tensor if layer == 0 else None
        h_w2 = w2t.tensor if layer == 0 else dnt.tensor

        upf = {}
        dnf = {}
        upc = [0]
        dnc = [0]
        ucnt = [0]
        gub_rdy = [(0, 0)]
        act_done = {}
        sg_of = {}
        stt_of = {}
        h_of = {}
        vcnt = [0]
        dn_ev = {}
        ub_guard = [0]

        def up_load(e, ic, th):
            par = upc[0] % 2
            pr.w("gpsimd", "mm", upf.get(upc[0] - 2, 0))
            if layer == 0:
                pr.op("gpsimd", lambda g, e=e, ic=ic, par=par: g.dma_start(
                    out=wu1[par][:, :],
                    in_=bass.AP(h_w1, e * D * D + ic * 128,
                                [[D, P], [P * D, 8], [1, P]])), "lw0" if par == 0 else "lw1", 16)
                pr.op("gpsimd", lambda g, e=e, ic=ic, par=par: g.dma_start(
                    out=wu3[par][:, :],
                    in_=bass.AP(h_w3, e * D * D + ic * 128,
                                [[D, P], [P * D, 8], [1, P]])), "lw0" if par == 0 else "lw1", 16)
            else:
                pr.op("gpsimd", lambda g, e=e, ic=ic, par=par: g.dma_start(
                    out=wu1[par][:, :],
                    in_=bass.AP(h_w1, e * D * 4096 + ic * 128,
                                [[4096, P], [P * 4096, 8], [1, P]])), "lw0" if par == 0 else "lw1", 16)
                pr.op("gpsimd", lambda g, e=e, ic=ic, par=par: g.dma_start(
                    out=wu3[par][:, :],
                    in_=bass.AP(h_w1, e * D * 4096 + (16 + ic) * 128,
                                [[4096, P], [P * 4096, 8], [1, P]])), "lw0" if par == 0 else "lw1", 16)
            upc[0] += 1
            return par, C["lw%d" % par]

        for th in range(NTH):
            tok0 = th * THT
            for e in range(NE):
                shared = (layer == 0 and e == 6)
                if layer == 1:
                    pr.w("sync", "ev")
                    pr.op("sync", lambda sy, e=e: sy.dma_start(
                        out=gub_tmp[:, :],
                        in_=bass.AP(gupb.tensor, e * 4096, [[P, 32], [1, P]])), "lx", 16)
                    pr.w("tensor", "lx")
                    pr.w("tensor", "ev")
                    pr.op("tensor", lambda t: t.transpose(ps_t[:, 0:32], gub_tmp[:, :],
                                                          ident[0:32, 0:32]), "mm")
                    pr.w("scalar", "mm")
                    pr.op("scalar", lambda s: s.activation(
                        gub_pm[:, :], ps_t[:, 0:32], AF.Copy, scale=1.0), "ac")
                    pr.w("vector", "ac")
                    pr.op("vector", lambda v: v.tensor_scalar(
                        gub_pm[:, 16:32], gub_pm[:, 16:32], 1.0, None, op0=ALU.add), "ev")
                    pr.w("vector", "ev")
                    gub_rdy[0] = (C["ac"], C["ev"])
                ubanks = [(ps_a, ps_b2), (ps_c1, ps_c2)]
                for ic in range(IC_UP):
                    par, lwn = up_load(e, ic, th)
                    sem = "lw%d" % par
                    for tcc in range(NTC):
                        u = ucnt[0]
                        ucnt[0] += 1
                        pA, pB = ubanks[u % 2]
                        uq = slice((u % 2) * QT, (u % 2) * QT + QT)
                        pr.w("tensor", sem, lwn)
                        uw = max(act_done.get(u - 2, 0), ub_guard[0])
                        if uw:
                            pr.w("tensor", "ev", uw)

                        def upmm(t, par=par, tcc=tcc, tok0=tok0, pA=pA, pB=pB):
                            r = None
                            for dc in range(8):
                                t.matmul(pA[:, :], wu1[par][:, dc * P:(dc + 1) * P],
                                         xb_sb[:, dc * NT + tok0 + tcc * QT:
                                               dc * NT + tok0 + (tcc + 1) * QT],
                                         start=(dc == 0), stop=(dc == 7))
                            for dc in range(8):
                                r = t.matmul(pB[:, :], wu3[par][:, dc * P:(dc + 1) * P],
                                             xb_sb[:, dc * NT + tok0 + tcc * QT:
                                                   dc * NT + tok0 + (tcc + 1) * QT],
                                             start=(dc == 0), stop=(dc == 7))
                            return r
                        pr.op("tensor", upmm, "mm")
                        upf[upc[0] - 1] = C["mm"]
                        mmv = C["mm"]
                        hcol = ic * THT + tcc * QT
                        if layer == 0:
                            pr.w("scalar", "mm", mmv)
                            if act_done.get(u - 2, 0):
                                pr.w("scalar", "ev", act_done[u - 2])
                            pr.op("scalar", lambda s, pA=pA, uq=uq: s.activation(
                                sg_scr[:, uq], pA[:, :], AF.Sigmoid), "ac")
                            pr.w("vector", "ac")
                            pr.op("vector", lambda v, pA=pA, uq=uq: v.tensor_tensor(
                                t1_scr[:, uq], pA[:, :], sg_scr[:, uq], op=ALU.mult), "ev")
                            pr.w("vector", "ev")
                            pr.op("vector", lambda v, hcol=hcol, pB=pB, uq=uq: v.tensor_tensor(
                                h_sb[:, hcol:hcol + QT], t1_scr[:, uq], pB[:, :],
                                op=ALU.mult), "ev")
                        else:
                            pr.w("vector", "mm", mmv)
                            pr.w("vector", "ac", gub_rdy[0][0])
                            pr.w("vector", "ev", gub_rdy[0][1])
                            if u >= 2:
                                pr.w("vector", "ev", h_of.get(u - 2, 0))
                            pr.op("vector", lambda v, ic=ic, pA=pA, uq=uq: v.tensor_scalar(
                                t1_scr[:, uq], pA[:, :], gub_pm[:, ic:ic + 1], 7.0,
                                op0=ALU.add, op1=ALU.min), "ev")
                            pr.op("vector", lambda v, ic=ic, pB=pB, uq=uq: v.tensor_scalar(
                                t2_scr[:, uq], pB[:, :], gub_pm[:, 16 + ic:17 + ic], -6.0,
                                op0=ALU.add, op1=ALU.max), "ev")
                            act_done[u] = C["ev"]
                            pr.w("scalar", "ev")
                            if u >= 2:
                                pr.w("scalar", "ev", h_of.get(u - 2, 0))
                            pr.op("scalar", lambda s, uq=uq: s.activation(
                                sg_scr[:, uq], t1_scr[:, uq], AF.Sigmoid, scale=1.702), "ac")
                            sg_of[u] = C["ac"]
                            pr.w("vector", "ev")
                            pr.op("vector", lambda v, uq=uq: v.scalar_tensor_tensor(
                                t2_scr[:, uq], t2_scr[:, uq], 8.0, t1_scr[:, uq],
                                op0=ALU.min, op1=ALU.mult), "ev")
                            stt_of[u] = C["ev"]
                            pr.w("vector", "ac", sg_of[u])
                            pr.op("vector", lambda v, hcol=hcol, uq=uq: v.tensor_tensor(
                                h_sb[:, hcol:hcol + QT], t2_scr[:, uq], sg_scr[:, uq],
                                op=ALU.mult), "ev")
                            h_of[u] = C["ev"]
                        if layer == 0:
                            act_done[u] = C["ev"]
                if not shared:
                    for tc4 in range(4):
                        pr.w("tensor", "ev")

                        def getr(t, e=e, tc4=tc4):
                            r = None
                            for tt4 in range(4):
                                tt = tc4 * 4 + tt4
                                r = t.transpose(ps_t[0:1, tt4 * P:(tt4 + 1) * P],
                                                ge_tm[:, tt * E + e: tt * E + e + 1],
                                                ident[:, :])
                            return r
                        pr.op("tensor", getr, "mm")
                        pr.w("vector", "mm")
                        pr.op("vector", lambda v, tc4=tc4: v.tensor_copy(
                            ge_row[:, tc4 * QT:(tc4 + 1) * QT], ps_t[0:1, 0:QT]), "ev")
                        pr.w("tensor", "ev")
                        pr.op("tensor", lambda t, tc4=tc4: t.matmul(
                            ps_g[:, :], ones_r[:, :], ge_row[:, tc4 * QT:(tc4 + 1) * QT],
                            start=True, stop=True), "mm")
                        pr.w("vector", "mm")
                        pr.op("vector", lambda v, tc4=tc4: v.tensor_copy(
                            geB[:, tc4 * QT:(tc4 + 1) * QT], ps_g[:, :]), "ev")
                dn_guard = C["ev"]
                dbanks = [ps_a, ps_b2, ps_c1, ps_c2]
                for do in range(8):
                    par = dnc[0] % 2
                    sem = "lw%d" % (2 + par)
                    pr.w("gpsimd", "mm", dnf.get(dnc[0] - 2, 0))
                    rows = D if layer == 0 else 2048
                    pr.op("gpsimd", lambda g, e=e, do=do, rows=rows, par=par: g.dma_start(
                        out=wd2[par][:, :],
                        in_=bass.AP(h_w2, e * rows * D + do * P,
                                    [[D, P], [P * D, IC_DN], [1, P]])), sem, 16)
                    dnc[0] += 1
                    lwn = C[sem]
                    for tcc in range(NTC):
                        dv_ = vcnt[0]
                        vcnt[0] += 1
                        bank = dbanks[dv_ % 4]
                        tq = slice((dv_ % 2) * QT, (dv_ % 2) * QT + QT)
                        pr.w("tensor", sem, lwn)
                        pr.w("tensor", "ev", max(dn_ev.get(dv_ - 4, 0), dn_guard))

                        def dnmm(t, par=par, tcc=tcc, bank=bank):
                            r = None
                            for ic2 in range(IC_DN):
                                r = t.matmul(bank[:, :], wd2[par][:, ic2 * P:(ic2 + 1) * P],
                                             h_sb[:, ic2 * THT + tcc * QT:
                                                  ic2 * THT + (tcc + 1) * QT],
                                             start=(ic2 == 0), stop=(ic2 == IC_DN - 1))
                            return r
                        pr.op("tensor", dnmm, "mm")
                        dnf[dnc[0] - 1] = C["mm"]
                        mmv = C["mm"]
                        xcol = do * NT + tok0 + tcc * QT
                        pr.w("vector", "mm", mmv)
                        if shared:
                            pr.op("vector", lambda v, xcol=xcol, bank=bank: v.tensor_tensor(
                                x_sb[:, xcol:xcol + QT], x_sb[:, xcol:xcol + QT],
                                bank[:, :], op=ALU.add), "ev")
                        else:
                            pr.op("vector", lambda v, bank=bank, tcc=tcc, tok0=tok0, tq=tq:
                                  v.tensor_tensor(t2_scr[:, tq], bank[:, :],
                                                  geB[:, tok0 + tcc * QT:
                                                      tok0 + (tcc + 1) * QT],
                                                  op=ALU.mult), "ev")
                            pr.w("vector", "ev")
                            pr.op("vector", lambda v, xcol=xcol, tq=tq: v.tensor_tensor(
                                x_sb[:, xcol:xcol + QT], x_sb[:, xcol:xcol + QT],
                                t2_scr[:, tq], op=ALU.add), "ev")
                        dn_ev[dv_] = C["ev"]
                ub_guard[0] = C["ev"]

        if layer == 1:
            for tc in range(4):
                for tt4 in range(4):
                    tt = tc * 4 + tt4
                    pr.w("tensor", "ev")
                    pr.op("tensor", lambda t, tt=tt, tt4=tt4: t.transpose(
                        ps_t[0:E, tt4 * P:(tt4 + 1) * P], ge_tm[:, tt * E:(tt + 1) * E],
                        ident[:, :]), "mm")
                pr.w("vector", "mm")
                pr.op("vector", lambda v: v.tensor_copy(
                    cbT[:, :], ps_t[0:E, 0:QT]), "ev")
                for do in range(8):
                    pr.w("tensor", "ev")
                    pr.op("tensor", lambda t, tc=tc, do=do: t.matmul(
                        ps_a[:, :], dnb_sb[:, do * P:(do + 1) * P],
                        cbT[:, :], start=True, stop=True), "mm")
                    pr.w("vector", "mm")
                    xcol = do * NT + tc * QT
                    pr.op("vector", lambda v, xcol=xcol: v.tensor_tensor(
                        x_sb[:, xcol:xcol + QT], x_sb[:, xcol:xcol + QT],
                        ps_a[:, :], op=ALU.add), "ev")
        pr.block()


        st_half = 0
        if ybf is not None:
            emit_ln(tcs=(0, 1))
            pr.w("sync", "ev")
            for dc in range(8):
                pr.op("sync", lambda sy, d=dc: sy.dma_start(
                    out=ybf[0][d * P:(d + 1) * P, :],
                    in_=xb_sb[:, d * NT: d * NT + NT // 2]), "st", 16)
            st_half = C["st"]
            if ag_hook is not None:
                ag_hook()
            emit_ln(tcs=(2, 3))
            pr.block()
            for dc in range(8):
                pr.op("sync", lambda sy, d=dc: sy.dma_start(
                    out=yT[d * P:(d + 1) * P, :], in_=x_sb[:, d * NT:(d + 1) * NT]),
                    "st", 16)
                pr.op("sync", lambda sy, d=dc: sy.dma_start(
                    out=ybf[1][d * P:(d + 1) * P, :],
                    in_=xb_sb[:, d * NT + NT // 2: (d + 1) * NT]), "st", 16)
        else:
            emit_ln()
            pr.block()
            for dc in range(8):
                pr.op("sync", lambda sy, d=dc: sy.dma_start(
                    out=yT[d * P:(d + 1) * P, :], in_=x_sb[:, d * NT:(d + 1) * NT]),
                    "st", 16)
        pr.w("gpsimd", "st")
        pr.op("gpsimd", lambda g: g.nop())
        pr.block()
        return {"st_half": st_half}




SEM_NAMES = ["ld", "ld0", "ld1", "lw", "lw0", "lw1", "lw2", "lw3", "gx",
             "mm", "ev", "ac", "sm", "dv", "px", "mk", "av", "dts0", "dts1",
             "dts2", "dts3", "g2s", "glw", "so0", "so1", "lo0", "lo1", "lx",
             "st", "cc"]


def build_attn(layer, flash_qts=None):
    nc = bass.Bass("TRN2", target_bir_lowering=False)
    xT = [nc.dram_tensor("xT%d" % b, [D, T], F32, kind="ExternalInput").ap()
          for b in range(2)]
    wq = nc.dram_tensor("wq", [D, 128], F32, kind="ExternalInput").ap()
    wk = nc.dram_tensor("wk", [D, 128], F32, kind="ExternalInput").ap()
    wv = nc.dram_tensor("wv", [D, 128], F32, kind="ExternalInput").ap()
    theta = gam = bet = None
    if layer == 0:
        theta = nc.dram_tensor("theta", [8], F32, kind="ExternalInput")
        gam = nc.dram_tensor("gam", [128], F32, kind="ExternalInput")
        bet = nc.dram_tensor("bet", [128], F32, kind="ExternalInput")
    otm = [nc.dram_tensor("otm%d" % b, [128, T], BF16, kind="ExternalOutput").ap()
           for b in range(2)]
    es = ExitStack()
    pr = Prog(nc, es, SEM_NAMES)
    HD = 128 if layer == 0 else 64
    with es:
        for b in range(2):
            emit_attn(nc, pr, layer, "a%d_" % b,
                      lambda dc, tc, b=b: xT[b][dc * P:(dc + 1) * P, tc * QT:(tc + 1) * QT],
                      lambda h, qt, b=b: otm[b][h * HD:(h + 1) * HD, qt * QT:(qt + 1) * QT],
                      wq, wk, wv, theta=theta, gam=gam, bet=bet,
                      flash_qts=flash_qts)
        pr.finalize()
    return nc


def _mix_weight_aps(nc, layer):
    E = 6 if layer == 0 else 8
    NE = 7 if layer == 0 else 8
    W = {"wo": nc.dram_tensor("wo", [D, D], F32, kind="ExternalInput").ap(),
         "wob": nc.dram_tensor("wob", [D], F32, kind="ExternalInput"),
         "gw": nc.dram_tensor("gw", [D, E], F32, kind="ExternalInput").ap(),
         "gb": nc.dram_tensor("gb", [E], F32, kind="ExternalInput")}
    if layer == 0:
        W["w1t"] = nc.dram_tensor("w1t", [NE, D, D], F32, kind="ExternalInput").ap()
        W["w3t"] = nc.dram_tensor("w3t", [NE, D, D], F32, kind="ExternalInput").ap()
        W["w2t"] = nc.dram_tensor("w2t", [NE, D, D], F32, kind="ExternalInput").ap()
    else:
        W["gupt"] = nc.dram_tensor("gupt", [E, D, 4096], F32, kind="ExternalInput").ap()
        W["gupb"] = nc.dram_tensor("gupb", [E, 4096], F32, kind="ExternalInput").ap()
        W["dnt"] = nc.dram_tensor("dnt", [E, 2048, D], F32, kind="ExternalInput").ap()
        W["dnb"] = nc.dram_tensor("dnb", [E, D], F32, kind="ExternalInput").ap()
    return W


def build_mix(layer):
    nc = bass.Bass("TRN2", target_bir_lowering=False)
    oT = nc.dram_tensor("oT", [D, NT], BF16, kind="ExternalInput").ap()
    xT = nc.dram_tensor("xT", [D, NT], F32, kind="ExternalInput").ap()
    W = _mix_weight_aps(nc, layer)
    yT = nc.dram_tensor("yT", [D, NT], F32, kind="ExternalOutput").ap()
    es = ExitStack()
    pr = Prog(nc, es, SEM_NAMES)
    with es:
        emit_mix(nc, pr, layer, "m_",
                 lambda dc, tc: oT[dc * P:(dc + 1) * P, tc * QT:(tc + 1) * QT],
                 xT, yT, W)
        pr.finalize()
    return nc


def build_fused():
    nc = bass.Bass("TRN2", target_bir_lowering=False)
    GROUPS = [[0, 1, 2, 3, 4, 5, 6, 7]]
    xTs = nc.dram_tensor("xTs", [D, NT], F32, kind="ExternalInput").ap()
    xTf = [nc.dram_tensor("xTf%d" % b, [D, T], F32, kind="ExternalInput").ap()
           for b in range(2)]
    a_in = {}
    for L in (0, 1):
        p = "A%d_" % L
        a_in[L] = {k: nc.dram_tensor(p + k, [D, 128], F32, kind="ExternalInput").ap()
                   for k in ("wq", "wk", "wv")}
    th = nc.dram_tensor("L0_theta", [8], F32, kind="ExternalInput")
    gam = nc.dram_tensor("L0_gam", [128], F32, kind="ExternalInput")
    bet = nc.dram_tensor("L0_bet", [128], F32, kind="ExternalInput")
    m_in = {}
    for L in (0, 1):
        E = 6 if L == 0 else 8
        NE = 7 if L == 0 else 8
        p = "M%d_" % L
        W = {"wo": nc.dram_tensor(p + "wo", [D, D], F32, kind="ExternalInput").ap(),
             "wob": nc.dram_tensor(p + "wob", [D], F32, kind="ExternalInput"),
             "gw": nc.dram_tensor(p + "gw", [D, E], F32, kind="ExternalInput").ap(),
             "gb": nc.dram_tensor(p + "gb", [E], F32, kind="ExternalInput")}
        if L == 0:
            for k in ("w1t", "w3t", "w2t"):
                W[k] = nc.dram_tensor(p + k, [NE, D, D], F32, kind="ExternalInput").ap()
        else:
            W["gupt"] = nc.dram_tensor(p + "gupt", [E, D, 4096], F32, kind="ExternalInput").ap()
            W["gupb"] = nc.dram_tensor(p + "gupb", [E, 4096], F32, kind="ExternalInput").ap()
            W["dnt"] = nc.dram_tensor(p + "dnt", [E, 2048, D], F32, kind="ExternalInput").ap()
            W["dnb"] = nc.dram_tensor(p + "dnb", [E, D], F32, kind="ExternalInput").ap()
        m_in[L] = W
    yout = nc.dram_tensor("yout", [D, NT], F32, kind="ExternalOutput").ap()
    agx_in = [nc.dram_tensor("agx_in%d" % L, [D, NT], BF16) for L in (0, 1)]
    agx1_in = [nc.dram_tensor("agx1_in%d" % i, [D, NT // 2], BF16) for i in range(2)]
    agx = [nc.dram_tensor("agx%d" % L, [8 * D, NT], BF16, addr_space="Shared")
           for L in (0, 1)]
    agx1h = [nc.dram_tensor("agx1h%d" % i, [8 * D, NT // 2], BF16, addr_space="Shared")
             for i in range(2)]
    ago_in = [nc.dram_tensor("ago_in%d" % L, [D, NT], BF16) for L in (0, 1)]
    ago = [nc.dram_tensor("ago%d" % L, [D, NT], BF16) for L in (0, 1)]
    x2s = nc.dram_tensor("x2s", [D, NT], F32)

    es = ExitStack()
    pr = Prog(nc, es, SEM_NAMES)

    def agx_chunk(L, b):
        if L == 1:
            def f1(dc, tc):
                r = b * 4 + tc // 4
                cc = tc % 4
                hh = agx1h[0] if cc < 2 else agx1h[1]
                return bass.AP(hh, (r * D + dc * P) * (NT // 2) + (cc % 2) * QT,
                               [[NT // 2, P], [1, QT]])
            return f1
        h = agx[L]

        def f(dc, tc):
            r = b * 4 + tc // 4
            return bass.AP(h, (r * D + dc * P) * NT + (tc % 4) * QT,
                           [[NT, P], [1, QT]])
        return f

    def a2a_odst(L, b, HD):
        h = ago_in[L]

        def f(hh, qt):
            row = (b * 4 + qt // 4) * 128 + hh * HD
            return bass.AP(h, row * NT + (qt % 4) * QT, [[NT, HD], [1, QT]])
        return f

    def ago_chunk(L):
        h = ago[L]

        def f(dc, tc):
            return bass.AP(h, dc * P * NT + tc * QT, [[NT, P], [1, QT]])
        return f

    def cast_and_ag(L, src_f32):
        if src_f32 is not None:
            pr.op("gpsimd", lambda g: g.dma_start(
                out=agx_in[L].ap(), in_=src_f32), "lw", 16)
            pr.w("gpsimd", "lw")
        else:
            pr.w("gpsimd", "st")
        pr.op("gpsimd", lambda g: g.collective_compute(
            "AllGather", mybir.AluOpType.bypass, replica_groups=GROUPS,
            ins=[agx_in[L].ap()], outs=[agx[L].ap()]), "cc")
        return ("cc", pr.cnt["cc"])

    def a2a_o(L):
        pr.w("gpsimd", "so0")
        pr.w("gpsimd", "so1")
        with allow_ici_all_to_all():
            pr.op("gpsimd", lambda g: g.collective_compute(
                "AllToAll", mybir.AluOpType.bypass, replica_groups=GROUPS,
                ins=[ago_in[L].ap()], outs=[ago[L].ap()]), "cc")
        pr.w("sync", "cc")
        pr.w("sync", "st")

    with es:
        sh0 = {}
        for b in range(2):
            emit_attn(nc, pr, 0, "a0%d_" % b,
                      lambda dc, tc, b=b: xTf[b][dc * P:(dc + 1) * P,
                                                 tc * QT:(tc + 1) * QT],
                      a2a_odst(0, b, 128),
                      a_in[0]["wq"], a_in[0]["wk"], a_in[0]["wv"],
                      theta=th, gam=gam, bet=bet, shared=sh0)
        a2a_o(0)

        def ag1_hook():
            pr.w("gpsimd", "st")
            pr.op("gpsimd", lambda g: g.collective_compute(
                "AllGather", mybir.AluOpType.bypass, replica_groups=GROUPS,
                ins=[agx1_in[0].ap()], outs=[agx1h[0].ap()]), "cc")
        emit_mix(nc, pr, 0, "m0_", ago_chunk(0), xTs, x2s.ap(), m_in[0],
                 ybf=[agx1_in[0].ap(), agx1_in[1].ap()], ag_hook=ag1_hook)
        g1 = ("cc", pr.cnt["cc"])
        pr.w("gpsimd", "st")
        pr.op("gpsimd", lambda g: g.collective_compute(
            "AllGather", mybir.AluOpType.bypass, replica_groups=GROUPS,
            ins=[agx1_in[1].ap()], outs=[agx1h[1].ap()]), "cc")
        g1b = ("cc", pr.cnt["cc"])
        sh_es = ExitStack()
        sh1 = {"cs": sh_es.enter_context(nc.sbuf_tensor("sh_cs", [P, 2048], F32)),
               "sn": sh_es.enter_context(nc.sbuf_tensor("sh_sn", [P, 2048], F32))}
        with sh_es:
            for b in range(2):
                emit_attn(nc, pr, 1, "a1%d_" % b, agx_chunk(1, b), a2a_odst(1, b, 64),
                          a_in[1]["wq"], a_in[1]["wk"], a_in[1]["wv"], shared=sh1,
                          xgate=(g1 if b == 0 else None),
                          xgate2=(g1b if b == 0 else None))
        a2a_o(1)
        emit_mix(nc, pr, 1, "m1_", ago_chunk(1), x2s.ap(), yout, m_in[1])
        pr.finalize()
    return nc



_BF = ml_dtypes.bfloat16


def _cont(a):
    return np.ascontiguousarray(a)


def _mix_weights(z, layer):
    if layer == 0:
        w1 = [z["L0_ffn_ew1"][e].T for e in range(6)] + [z["L0_ffn_fc1"].T]
        w3 = [z["L0_ffn_ew3"][e].T for e in range(6)] + [z["L0_ffn_fc2"].T]
        w2 = [z["L0_ffn_ew2"][e].T for e in range(6)] + [z["L0_ffn_fc3"].T]
        return {"wo": _cont(z["L0_attn_Wo"]), "wob": _cont(z["L0_attn_bo"]),
                "gw": _cont(z["L0_ffn_gate_w"].T), "gb": _cont(z["L0_ffn_gate_b"]),
                "w1t": _cont(np.stack(w1)), "w3t": _cont(np.stack(w3)),
                "w2t": _cont(np.stack(w2))}
    return {"wo": _cont(z["L1_attn_wo_w"]), "wob": _cont(z["L1_attn_wo_b"]),
            "gw": _cont(z["L1_ffn_router_weight"].T), "gb": _cont(z["L1_ffn_router_bias"]),
            "gupt": _cont(np.transpose(z["L1_ffn_gate_up_proj"], (0, 2, 1))),
            "gupb": _cont(z["L1_ffn_gate_up_proj_bias"]),
            "dnt": _cont(np.transpose(z["L1_ffn_down_proj"], (0, 2, 1))),
            "dnb": _cont(z["L1_ffn_down_proj_bias"])}


_PROGS = {}


def _prog(kind):
    if kind not in _PROGS:
        _PROGS[kind] = build_fused()
    return _PROGS[kind]


def kernel(**inputs):
    z = {k: np.asarray(v) for k, v in inputs.items()}
    x = z["L0_attn_x"]
    xT = [_cont(x[b].T) for b in range(2)]
    mw = {L: _mix_weights(z, L) for L in (0, 1)}
    maps = []
    for c in range(8):
        b, ts = c // 4, c % 4
        cs = slice(c * 128, (c + 1) * 128)
        m = {"xTs": _cont(xT[b][:, ts * NT:(ts + 1) * NT]),
             "xTf0": xT[0], "xTf1": xT[1],
             "A0_wq": _cont(z["L0_attn_Wq"][:, cs]),
             "A0_wk": _cont(z["L0_attn_Wk"][:, cs]),
             "A0_wv": _cont(z["L0_attn_Wv"][:, cs]),
             "A1_wq": _cont(z["L1_attn_wq"][:, cs]),
             "A1_wk": _cont(z["L1_attn_wk"][:, cs]),
             "A1_wv": _cont(z["L1_attn_wv"][:, cs]),
             "L0_theta": _cont(z["L0_attn_theta"]),
             "L0_gam": _cont(z["L0_attn_gamma"][cs]),
             "L0_bet": _cont(z["L0_attn_beta"][cs])}
        for L in (0, 1):
            for k, v in mw[L].items():
                m["M%d_%s" % (L, k)] = v
        maps.append(m)
    res = run_bass_kernel_spmd(_prog("fused"), maps, core_ids=list(range(8)))
    out = np.empty((2, T, D), np.float32)
    for c in range(8):
        b, ts = c // 4, c % 4
        out[b, ts * NT:(ts + 1) * NT, :] = np.asarray(res.results[c]["yout"]).T
    return out
